# version 19
# baseline (speedup 1.0000x reference)
import numpy as np
import ml_dtypes
import concourse.bass as bass
import concourse.mybir as mybir
from concourse.bass_utils import run_bass_kernel_spmd
from concourse.ap import AP

F32 = mybir.dt.float32
BF16 = mybir.dt.bfloat16
I32 = mybir.dt.int32
Alu = mybir.AluOpType
Act = mybir.ActivationFunctionType
AX = mybir.AxisListType.X

DEBUG_SPILLS = ()

B, T, C, G, H, GH = 2, 16384, 1024, 8, 4096, 32
NT = 512
NTILES = 10
NI = NTILES - 2
NW = NT * NTILES
NCORE = NT * NI
SC = NT + 1
NQ = max(NT // 128, 1)
QW = min(NT, 128)
EPS = 1e-5
EPS_RMS = 1e-6
EPS_ACT = 1e-5

V_EW, V_EU, V_U, V_L1W, V_L1B, V_L2W, V_L2B, V_FB = range(8)


class Builder:
    def __init__(self):
        nc = bass.Bass("TRN2", target_bir_lowering=False, debug=False,
                       detect_race_conditions=False)
        self.nc = nc

        d = self.d = {}
        for n, shp, dt in [
            ("xT", [C, NW], F32), ("mask2", [128, 2 * NT], BF16),
            ("WkT", [C, C], F32), ("WvT", [C, C], F32), ("WrT", [C, C], F32),
            ("WoT", [C, C], F32), ("WfrT", [C, C], F32),
            ("WfkT", [C, H], F32), ("WfvT", [H, C], F32), ("w8T", [C, C], I32),
            ("ln1w", [128, G], F32), ("ln1b", [128, G], F32),
            ("ln2w", [128, G], F32), ("ln2b", [128, G], F32),
            ("decay", [128, G], F32), ("uu", [128, G], F32),
            ("fbias", [128, G], F32), ("wst", [1, 1], F32),
            ("onesb", [128, 128], BF16), ("identb", [128, 128], BF16),
            ("onesf", [1, 128], F32),
            ("sel4", [2 * NQ, NQ * QW], BF16),
        ]:
            d[n] = nc.dram_tensor(n, shp, dt, kind="ExternalInput").ap()

        d["out"] = nc.dram_tensor("out", [NI, 128, G * NT], F32,
                                  kind="ExternalOutput").ap()
        for n, shp, dt in [
            ("E1sp", [NTILES, 128, G * NT], F32),
            ("gsp", [NTILES, 128, G * NT], F32),
            ("zsp", [NTILES, 128, G * NT], F32),
            ("outfsp", [NTILES, 128, G * NT], F32),
            ("rsp", [NTILES, 128, G * NT], F32),
            ("a1sp", [NTILES, 128, G * NT], F32),
            ("xn2sp", [NTILES, 128, G * NT], BF16),
            ("kksp", [NTILES, 2, 128, 16 * NT], BF16),
            ("dbg", [NTILES, 6, 128, G * NT], F32),
        ]:
            kind = "ExternalOutput" if n in DEBUG_SPILLS else "Internal"
            d[n] = nc.dram_tensor(n, shp, dt, kind=kind).ap()

        s = self.s = {}
        avail = nc.sbuf_bytes_remaining
        arena = nc.alloc_sbuf_tensor("arena", [128, avail - 64],
                                     mybir.dt.uint8)
        base = nc.lookup_mloc(arena).addr
        self._off = base
        A = self._place

        s["ones"] = self.at("ones", [128, 128], BF16, A(256))
        s["identb"] = self.at("identb_s", [128, 128], BF16, A(256))
        s["sel4"] = self.at("sel4_s", [128, NQ * QW], BF16, A(1024))
        s["mask"] = self.at("mask_s", [128, 2 * NT], BF16, A(2 * NT * 2))
        s["c"] = self.at("c_s", [128, G * NTILES], F32, A(G * NTILES * 4))
        s["vec"] = self.at("vec_s", [128, 8 * G], F32, A(8 * G * 4))
        s["wsbc"] = self.at("wsbc_s", [128, 1], F32, A(32))
        s["onec"] = self.at("onec_s", [128, 1], F32, A(32))
        s["onesf"] = self.at("onesf_s", [128, 128], F32, A(512))
        s["wst"] = self.at("wst_s", [128, 1], F32, A(32))
        F4 = NT * 4
        toff = A(5 * F4 + 1024)
        s["tdx"] = self.at("tdx", [128, NT], F32, toff)
        s["ti"] = self.at("ti", [128, NT], I32, toff)
        s["tv"] = self.at("tv", [128, NT], F32, toff + F4)
        s["trel"] = self.at("trel", [128, NT], BF16, toff + F4)
        s["tt2"] = self.at("tt2", [128, NT], F32, toff + F4)
        s["tt"] = self.at("tt", [128, NT], F32, toff + 2 * F4)
        s["tw"] = self.at("tw", [128, NT], F32, toff + 2 * F4)
        s["txb"] = self.at("txb", [128, 2 * NT], BF16, toff + 3 * F4)
        s["txq"] = self.at("txq", [128, 2 * NT], BF16, toff + 4 * F4)
        s["sm"] = self.at("sm", [128, 32], F32, toff + 5 * F4)
        s["hl8"] = self.at("hl8", [128, 4 * NQ], BF16, toff + 5 * F4 + 128)
        s["trows"] = self.at("trows", [128, 256], BF16, toff + 5 * F4 + 256)
        s["st0"] = self.at("st0", [128, NT], F32, A(NT * 4))
        s["st1"] = self.at("st1", [128, NT], F32, A(NT * 4))
        s["st2"] = self.at("st2", [128, NT], F32, A(NT * 4))
        s["st3"] = self.at("st3", [128, NT], F32, A(NT * 4))
        soff = A(max(4 * G * SC * 4, G * H * 2))
        self.sS = [self.at(f"s{i}", [128, G * SC], F32, soff + i * G * SC * 4)
                   for i in range(4)]
        s["wbig"] = self.at("wbig", [128, G * H], BF16, soff)
        foff = A(max(4 * G * NT * 4, G * C * 4))
        self.fF = [self.at(f"f{i}", [128, G * NT], F32, foff + i * G * NT * 4)
                   for i in range(4)]
        s["w8stg"] = self.at("w8stg", [128, G * C], I32, foff)
        s["qv"] = self.at("qv", [128, G * NT], BF16, foff + 1 * G * NT * 4)
        s["bf1"] = self.at("bf1", [128, G * NT], BF16, A(G * NT * 2))
        woff = A(3 * G * C * 2)
        self.wW = [self.at(f"w{i}", [128, G * C], BF16, woff + i * G * C * 2)
                   for i in range(3)]
        s["bf2"] = self.at("bf2", [128, 16 * NT], BF16, woff + 2 * G * C * 2)
        used = self._off - base
        assert used <= avail - 64, f"sbuf over budget: {used} > {avail - 64}"

        self.pb = [nc.alloc_psum_tensor(f"pb{i}", [128, NT], F32)
                   for i in range(8)]

        self.sems = {n: nc.alloc_semaphore(n) for n in
                     ["vecL", "xL", "e1L", "gL", "zL", "ofL", "rL", "xnL",
                      "kkL0", "kkL1", "a1L", "a1L2",
                      "zS", "e1S", "gS", "ofS", "rS", "a1S", "xnS", "kkS",
                      "outS", "sM", "sV", "sC", "sG", "sW",
                      "dbg0", "dbg1", "dbg2", "dbg3", "dbg4"]}
        self.cnt = {n: 0 for n in self.sems}
        self.bankmark = {}

    def _place(self, nbytes):
        a = self._off
        self._off = a + ((nbytes + 31) // 32) * 32
        return a

    def at(self, name, shape, dt, offset):
        return self.nc.alloc_sbuf_tensor_at(name, shape, dt, offset=offset)

    def inc(self, instr, sem, n=1):
        instr.then_inc(self.sems[sem], n)
        self.cnt[sem] += n
        return self.cnt[sem]

    def w(self, eng, sem, val):
        if val and val > 0:
            eng.wait_ge(self.sems[sem], val)

    def bank_wait(self, eng, i):
        m = self.bankmark.get(i)
        if m:
            self.w(eng, m[0], m[1])

    def bank_set(self, i, sem):
        self.bankmark[i] = (sem, self.cnt[sem])

    def fch(self, buf, g):
        return buf[:, g * NT:(g + 1) * NT]

    def sdat(self, i, g, a, b):
        return self.sS[i][:, g * SC + a: g * SC + b]

    def s_rev(self, i, g, a, n):
        t = self.sS[i].ap()
        return AP(t.tensor, t.offset + g * SC + a, [[t.ap[0][0], 128], [-1, n]])

    def s_str(self, i, a, b):
        t = self.sS[i].ap()
        return AP(t.tensor, t.offset + a,
                  [[t.ap[0][0], 128], [SC, G], [1, b - a]])

    def f_str(self, buf, a, b):
        t = buf.ap()
        return AP(t.tensor, t.offset + a,
                  [[t.ap[0][0], 128], [NT, G], [1, b - a]])

    def col_str(self, buf, col, pitch, n):
        t = buf.ap()
        return AP(t.tensor, t.offset + col, [[t.ap[0][0], 128], [pitch, n]])

    def vcol(self, i, g=None, n=1):
        if g is None:
            return self.s["vec"][:, i * G:(i + 1) * G]
        return self.s["vec"][:, i * G + g: i * G + g + n]

    def ccol(self, tau, g=None):
        if g is None:
            return self.s["c"][:, tau * G: (tau + 1) * G]
        return self.s["c"][:, tau * G + g: tau * G + g + 1]

    def ew_b(self, g, n):
        return self.vcol(V_EW, g).broadcast_to([128, n])

    def xT3(self, tau):
        t = self.d["xT"]
        return AP(t.tensor, t.offset + tau * NT,
                  [[NW, 128], [128 * NW, G], [1, NT]])

    def w_in3(self, name, cols):
        t = self.d[name]
        nchunks = t.shape[0] // 128
        return AP(t.tensor, t.offset,
                  [[cols, 128], [128 * cols, nchunks], [1, cols]])

    def w_out3(self, slot, cols, nchunks):
        t = slot.ap()
        return AP(t.tensor, t.offset,
                  [[t.ap[0][0], 128], [cols, nchunks], [1, cols]])

    def wsl(self, slot, g, m):
        return slot[:, g * C + m * 128: g * C + (m + 1) * 128]

    def wsl_h(self, slot, g, cols, m):
        return slot[:, g * cols + m * 128: g * cols + (m + 1) * 128]


def emit_stats_front(K, blk, src_full, pre=(), banks=(0, 1), xsq_gp=False):
    s, pb = K.s, K.pb
    b0, b1 = banks
    m0 = K.cnt["sM"]
    v0 = K.cnt["sV"]
    g0 = K.cnt["sG"]

    if xsq_gp:
        @blk.gpsimd
        def _(gp):
            for g in range(G):
                if g == 0:
                    for sem, val in pre:
                        K.w(gp, sem, val)
                if g >= 2:
                    K.w(gp, "sM", m0 + g - 1)
                i = gp.tensor_mul(s["txq"][:, (g % 2) * NT:(g % 2 + 1) * NT],
                                  K.fch(src_full, g), K.fch(src_full, g))
                K.inc(i, "sG")
        assert K.cnt["sG"] == g0 + G
    else:
        @blk.vector
        def _(v):
            for g in range(G):
                if g == 0:
                    for sem, val in pre:
                        K.w(v, sem, val)
                if g >= 2:
                    K.w(v, "sM", m0 + g - 1)
                i = v.tensor_mul(s["txq"][:, (g % 2) * NT:(g % 2 + 1) * NT],
                                 K.fch(src_full, g), K.fch(src_full, g))
                K.inc(i, "sV")
        assert K.cnt["sV"] == v0 + G

    @blk.tensor
    def _(t, xsq_gp=xsq_gp):
        for g in range(G):
            if g == 0:
                for sem, val in pre:
                    K.w(t, sem, val)
                K.bank_wait(t, b0)
                K.bank_wait(t, b1)
            t.matmul(pb[b0][:, :], s["onesf"][:, :], K.fch(src_full, g),
                     start=(g == 0), stop=(g == G - 1))
            if xsq_gp:
                K.w(t, "sG", g0 + g + 1)
            else:
                K.w(t, "sV", v0 + g + 1)
            i = t.matmul(pb[b1][:, :], s["ones"][:, :],
                         s["txq"][:, (g % 2) * NT:(g % 2 + 1) * NT],
                         start=(g == 0), stop=(g == G - 1))
            K.inc(i, "sM")
    assert K.cnt["sM"] == m0 + G
    return m0


def emit_stats_back(K, blk, coefs, banks=(0, 1), m0=None):
    s, pb = K.s, K.pb
    b0, b1 = banks

    @blk.vector
    def _(v):
        K.w(v, "sM", m0 + G)
        v.tensor_scalar_mul(s["st0"][:, :], pb[b0][:, :], 1.0 / C)
        i = v.tensor_scalar_mul(s["st1"][:, :], pb[b1][:, :], 1.0 / C)
        K.inc(i, "sV")
        v.tensor_mul(s["st3"][:, :], s["st0"][:, :], s["st0"][:, :])
        v.tensor_sub(s["st1"][:, :], s["st1"][:, :], s["st3"][:, :])
        for dst, kind in coefs:
            if kind == "r":
                i = v.tensor_scalar_add(s[dst][:, :], s["st1"][:, :], EPS)
            elif kind == "rho":
                i = v.tensor_scalar(s[dst][:, :], s["st1"][:, :],
                                    1.0 + EPS, EPS * EPS, Alu.mult, Alu.add)
            elif kind == "tau":
                i = v.tensor_scalar(s[dst][:, :], s["st1"][:, :],
                                    1.0 + EPS_RMS, EPS * EPS_RMS,
                                    Alu.mult, Alu.add)
        K.inc(i, "sV")
    K.bank_set(b0, "sV")
    K.bank_set(b1, "sV")
    vmark = K.cnt["sV"]

    @blk.scalar
    def _(sc):
        K.w(sc, "sV", vmark)
        for dst, kind in coefs:
            sc.activation(s[dst][:, :], s[dst][:, :], Act.Ln)
            i = sc.activation(s[dst][:, :], s[dst][:, :], Act.Exp, scale=-0.5)
        K.inc(i, "sC")
    return K.cnt["sC"]


def emit_stats(K, blk, src_full, coefs, pre=(), banks=(0, 1)):
    m0 = emit_stats_front(K, blk, src_full, pre=pre, banks=banks)
    return emit_stats_back(K, blk, coefs, banks=banks, m0=m0)


def emit_pass_a(K):
    nc, s, d, pb, fF, sS, wW = K.nc, K.s, K.d, K.pb, K.fF, K.sS, K.wW
    with nc.Block("passA", no_gpsimd_drain=True) as blk:
        @blk.gpsimd
        def _(gp):
            K.inc(gp.dma_start(K.w_out3(wW[0], C, G), K.w_in3("WkT", C)),
                  "sW", 16)
            K.inc(gp.dma_start(K.w_out3(wW[1], C, G), K.w_in3("WvT", C)),
                  "sW", 16)
            K.inc(gp.dma_start(K.w_out3(wW[2], C, G), K.w_in3("WrT", C)),
                  "sW", 16)
        wmark = K.cnt["sW"]

        @blk.sync
        def _(sy):
            for dst, src in [
                (s["ones"][:, :], d["onesb"]), (s["identb"][:, :], d["identb"]),
                (s["sel4"][0:2 * NQ, :], d["sel4"]),
                (s["mask"][:, :], d["mask2"]),
                (K.vcol(V_L1W), d["ln1w"]), (K.vcol(V_L1B), d["ln1b"]),
                (K.vcol(V_L2W), d["ln2w"]), (K.vcol(V_L2B), d["ln2b"]),
                (K.vcol(V_U), d["uu"]), (K.vcol(V_EW), d["decay"]),
                (K.vcol(V_FB), d["fbias"]),
                (s["onesf"][0:1, :], d["onesf"]), (s["wst"][0:1, :], d["wst"]),
            ]:
                K.inc(sy.dma_start(dst, src), "vecL", 16)
        ld0 = K.cnt["vecL"]

        @blk.scalar
        def _(sc):
            K.w(sc, "vecL", ld0)
            sc.activation(K.vcol(V_EW), K.vcol(V_EW), Act.Exp)
            sc.drain()
            sc.activation(K.vcol(V_EW), K.vcol(V_EW), Act.Exp, scale=-1.0)
            sc.drain()
            sc.activation(K.vcol(V_EU), K.vcol(V_U), Act.Exp)
            sc.drain()
            K.inc(sc.nop(), "sC")
        prep_c = K.cnt["sC"]

        @blk.tensor
        def _(t):
            K.w(t, "vecL", ld0)
            i = t.matmul(pb[7][:, 0:1], s["onesf"][0:1, :], s["wst"][0:1, 0:1],
                         start=True, stop=True)
            K.inc(i, "sM")
        wsm = K.cnt["sM"]

        @blk.vector
        def _(v):
            K.w(v, "sM", wsm)
            v.tensor_copy(s["wsbc"][:, :], pb[7][:, 0:1])
            v.memset(s["onec"][:, :], 1.0)
            v.memset(s["onesf"][:, :], 1.0)
            v.memset(K.col_str(sS[0], 0, SC, G), 0.0)
            i = v.memset(K.col_str(sS[1], 0, SC, G), 0.0)
            K.inc(i, "sV")
        K.bank_set(7, "sV")
        prep_v = K.cnt["sV"]

        marks = {}
        statsmm = {}

        def emit_xstats(tau):
            @blk.sync
            def _(sy, tau=tau):
                K.w(sy, "sV", marks.get(("xfree", tau - 1), 0))
                K.inc(sy.dma_start(K.f_str(fF[0], 0, NT), K.xT3(tau)), "xL", 16)
            marks[("ldx", tau)] = K.cnt["xL"]

            pre = [("xL", marks[("ldx", tau)]), ("sC", prep_c),
                   ("sV", prep_v), ("sM", statsmm.get(tau - 1, 0))]
            marks[("cm", tau)] = emit_stats(K, blk, fF[0],
                                            [("st2", "r"), ("st3", "rho")],
                                            pre=pre)
            statsmm[tau] = K.cnt["sM"]

        def emit_dxz(tau):
            interior = 1 <= tau <= NI
            cmark = marks[("cm", tau)]

            @blk.vector
            def _(v, tau=tau, cmark=cmark, interior=interior):
                K.w(v, "sC", cmark)
                K.w(v, "zS", marks.get(("spz", tau - 1), 0))
                K.w(v, "sM", marks.get(("projmm", tau - 1), 0))
                for g in range(G):
                    v.tensor_sub(s["tdx"][:, :], K.fch(fF[0], g), s["st0"][:, :])
                    if interior:
                        v.tensor_mul(K.fch(fF[1], g), s["tdx"][:, :],
                                     s["st2"][:, :])
                    v.tensor_mul(s["tt"][:, :], s["tdx"][:, :], s["st3"][:, :])
                    i = v.tensor_scalar(K.fch(s["bf1"], g), s["tt"][:, :],
                                        K.vcol(V_L1W, g), K.vcol(V_L1B, g),
                                        Alu.mult, Alu.add)
                K.inc(i, "sV")
            marks[("xfree", tau)] = K.cnt["sV"]
            marks[("xn", tau)] = K.cnt["sV"]

            if interior:
                @blk.sync
                def _(sy, tau=tau):
                    K.w(sy, "sV", marks[("xn", tau)])
                    K.inc(sy.dma_start(d["zsp"][tau], fF[1][:, :]), "zS", 16)
                marks[("spz", tau)] = K.cnt["zS"]

        def emit_mloop(tau):
            interior = 1 <= tau <= NI
            per_m = 3 if interior else 2
            sc0 = K.cnt["sC"]
            scp = lambda m, j, sc0=sc0, per_m=per_m: sc0 + m * per_m + j
            mk, mv, mr = {}, {}, {}

            @blk.tensor
            def _(t, tau=tau, scp=scp, interior=interior):
                K.w(t, "sW", wmark)
                K.w(t, "sV", marks[("xn", tau)])
                for m in range(G):
                    if m >= 2:
                        K.w(t, "sC", scp(m - 2, 1))
                    else:
                        K.bank_wait(t, 2 + m)
                    for g in range(G):
                        i = t.matmul(pb[2 + (m % 2)][:, :],
                                     K.wsl(wW[0], g, m), K.fch(s["bf1"], g),
                                     start=(g == 0), stop=(g == G - 1))
                    mk[m] = K.inc(i, "sM")
                    if m >= 2:
                        K.w(t, "sC", scp(m - 2, 2))
                    else:
                        K.bank_wait(t, 4 + m)
                    for g in range(G):
                        i = t.matmul(pb[4 + (m % 2)][:, :],
                                     K.wsl(wW[1], g, m), K.fch(s["bf1"], g),
                                     start=(g == 0), stop=(g == G - 1))
                    mv[m] = K.inc(i, "sM")
                    if interior:
                        if m >= 2:
                            K.w(t, "sC", scp(m - 2, 3))
                        else:
                            K.bank_wait(t, 6 + m)
                        for g in range(G):
                            i = t.matmul(pb[6 + (m % 2)][:, :],
                                         K.wsl(wW[2], g, m), K.fch(s["bf1"], g),
                                         start=(g == 0), stop=(g == G - 1))
                        mr[m] = K.inc(i, "sM")
            marks[("projmm", tau)] = K.cnt["sM"]

            gv0 = K.cnt["sV"]
            vneg = {m: gv0 + 2 * m + 1 for m in range(G)}
            gmk = {m: gv0 + 2 * m + 2 for m in range(G - 1)}
            gmk[G - 1] = gv0 + 2 * G + (1 if tau > 0 else 0)
            dcm = [gv0 + 2 * G]

            @blk.scalar
            def _(sc, tau=tau, scp=scp, interior=interior):
                for m in range(G):
                    K.w(sc, "sV", vneg[m])
                    if m == 0:
                        K.w(sc, "e1S", marks.get(("spe", tau - 1), 0))
                        K.w(sc, "sV", marks.get(("outf", tau - 1), 0))
                    i = sc.activation(K.sdat(0, m, 1, SC),
                                      pb[2 + (m % 2)][:, :],
                                      Act.Exp, bias=K.ccol(tau, m))
                    assert K.inc(i, "sC") == scp(m, 1)
                    K.w(sc, "sM", mv[m])
                    if m == 0:
                        K.w(sc, "sV", marks.get(("eglast", tau - 1), 0))
                    else:
                        K.w(sc, "sV", gmk[m - 1])
                    i = sc.copy(s["tv"][:, :], pb[4 + (m % 2)][:, :])
                    assert K.inc(i, "sC") == scp(m, 2)
                    if interior:
                        K.w(sc, "sM", mr[m])
                        if m == 0:
                            K.w(sc, "rS", marks.get(("spr", tau - 1), 0))
                        sc.activation(K.fch(fF[3], m), pb[6 + (m % 2)][:, :],
                                      Act.Exp, scale=-1.0)
                        sc.drain()
                        sc.activation(K.fch(fF[3], m), K.fch(fF[3], m),
                                      Act.Ln, bias=s["onec"][:, 0:1])
                        sc.drain()
                        i = sc.activation(K.fch(fF[3], m), K.fch(fF[3], m),
                                          Act.Exp, scale=-1.0)
                        assert K.inc(i, "sC") == scp(m, 3)
                if tau > 0:
                    K.w(sc, "sV", dcm[0])
                    i = sc.activation(s["sm"][:, 16:24], s["sm"][:, 8:16],
                                      Act.Exp)
                    K.inc(i, "sC")
            marks[("rdone", tau)] = K.cnt["sC"]
            marks[("rscm", tau)] = K.cnt["sC"]
            K.bankmark[2] = ("sC", scp(6, 1))
            K.bankmark[3] = ("sC", scp(7, 1))
            K.bankmark[4] = ("sC", scp(6, 2))
            K.bankmark[5] = ("sC", scp(7, 2))
            if interior:
                K.bankmark[6] = ("sC", scp(6, 3))
                K.bankmark[7] = ("sC", scp(7, 3))

                @blk.sync
                def _(sy, tau=tau):
                    K.w(sy, "rS", marks.get(("spr", tau - 1), 0))
                    K.w(sy, "sC", marks[("rdone", tau)])
                    K.inc(sy.dma_start(d["rsp"][tau], fF[3][:, :]), "rS", 16)
                marks[("spr", tau)] = K.cnt["rS"]

            @blk.vector
            def _(v, tau=tau, gv0=gv0, scp=scp):
                for m in range(G):
                    K.w(v, "sM", mk[m])
                    i = v.tensor_reduce(K.ccol(tau, m), pb[2 + (m % 2)][:, :],
                                        AX, Alu.max, negate=True)
                    assert K.inc(i, "sV") == vneg[m]
                    if m == G - 1 and tau > 0:
                        i = v.tensor_sub(s["sm"][:, 8:16], K.ccol(tau),
                                         K.ccol(tau - 1))
                        assert K.inc(i, "sV") == dcm[0]
                    K.w(v, "sC", scp(m, 2))
                    if m == 0:
                        K.w(v, "gS", marks.get(("spg", tau - 1), 0))
                    if tau == 0:
                        v.tensor_mul(K.sdat(0, m, 1, SC), K.sdat(0, m, 1, SC),
                                     s["mask"][:, 0:NT])
                    elif tau == NTILES - 1:
                        v.tensor_mul(K.sdat(0, m, 1, SC), K.sdat(0, m, 1, SC),
                                     s["mask"][:, NT:2 * NT])
                    i = v.tensor_mul(K.sdat(1, m, 1, SC), K.sdat(0, m, 1, SC),
                                     s["tv"][:, :])
                    assert K.inc(i, "sV") == gmk[m]
            marks[("eglast", tau)] = gmk[G - 1]

        def emit_scan(tau):
            interior = 1 <= tau <= NI
            den0 = K.cnt["sV"]
            rscm = marks[("rscm", tau)]

            @blk.vector
            def _(v, tau=tau, rscm=rscm, interior=interior):
                if tau > 0:
                    K.w(v, "sC", rscm)
                    v.tensor_mul(K.col_str(sS[1], 0, SC, G),
                                 K.col_str(sS[2], NT, SC, G), s["sm"][:, 16:24])
                    v.tensor_mul(K.col_str(sS[0], 0, SC, G),
                                 K.col_str(sS[3], NT, SC, G),
                                 s["sm"][:, 16:24])
                    v.drain()
                for g in range(G):
                    v.tensor_tensor_scan(
                        K.sdat(2, g, 0, SC), K.ew_b(g, SC), K.sdat(1, g, 0, SC),
                        0.0, Alu.mult, Alu.add)
                    i = v.tensor_tensor_scan(
                        K.sdat(3, g, 0, SC), K.ew_b(g, SC), K.sdat(0, g, 0, SC),
                        0.0, Alu.mult, Alu.add)
                K.inc(i, "sV")
                if interior:
                    for g in range(G):
                        v.scalar_tensor_tensor(
                            K.sdat(2, g, 0, NT), K.sdat(1, g, 1, SC),
                            K.vcol(V_EU, g), K.sdat(2, g, 0, NT),
                            Alu.mult, Alu.add)
                        i = v.scalar_tensor_tensor(
                            K.sdat(3, g, 0, NT), K.sdat(0, g, 1, SC),
                            K.vcol(V_EU, g), K.sdat(3, g, 0, NT),
                            Alu.mult, Alu.add)
                    K.inc(i, "sV")
            marks[("scan", tau)] = den0 + 1
            denm = K.cnt["sV"]

            if tau >= 1:
                @blk.sync
                def _(sy, tau=tau):
                    K.w(sy, "sV", marks[("scan", tau)])
                    K.inc(sy.dma_start(d["E1sp"][tau], K.s_str(0, 1, SC)),
                          "e1S", 16)
                    K.inc(sy.dma_start(d["gsp"][tau], K.s_str(1, 1, SC)),
                          "gS", 16)
                marks[("spe", tau)] = K.cnt["e1S"]
                marks[("spg", tau)] = K.cnt["gS"]

            if interior:
                @blk.scalar
                def _(sc, denm=denm):
                    K.w(sc, "sV", denm)
                    sc.activation(K.s_str(3, 0, NT), K.s_str(3, 0, NT), Act.Ln)
                    i = sc.activation(K.s_str(3, 0, NT), K.s_str(3, 0, NT),
                                      Act.Exp, scale=-1.0)
                    K.inc(i, "sC")
                marks[("rdm", tau)] = K.cnt["sC"]

        def emit_outf(tau):
            rdm = marks[("rdm", tau)]

            @blk.vector
            def _(v, tau=tau, rdm=rdm):
                K.w(v, "sC", rdm)
                K.w(v, "ofS", marks.get(("spf", tau - 1), 0))
                i = v.tensor_mul(K.f_str(fF[2], 0, NT),
                                 K.s_str(2, 0, NT), K.s_str(3, 0, NT))
                K.inc(i, "sV")
            marks[("outf", tau)] = K.cnt["sV"]

            @blk.sync
            def _(sy, tau=tau):
                K.w(sy, "sV", marks[("outf", tau)])
                K.inc(sy.dma_start(d["outfsp"][tau], fF[2][:, :]),
                      "ofS", 16)
            marks[("spf", tau)] = K.cnt["ofS"]

        emit_xstats(0)
        emit_dxz(0)
        for tau in range(NTILES):
            emit_mloop(tau)
            if tau + 1 < NTILES:
                emit_xstats(tau + 1)
            emit_scan(tau)
            if tau + 1 < NTILES:
                emit_dxz(tau + 1)
            if 1 <= tau <= NI:
                emit_outf(tau)

        @blk.sync
        def _(sy):
            for n in ["zS", "e1S", "gS", "ofS", "rS", "xL", "vecL"]:
                K.w(sy, n, K.cnt[n])


def emit_pass_b1(K):
    nc, s, d, pb, fF, sS, wW = K.nc, K.s, K.d, K.pb, K.fF, K.sS, K.wW
    K.bankmark = {}
    with nc.Block("passB1", no_gpsimd_drain=True) as blk:
        @blk.gpsimd
        def _(gp):
            K.inc(gp.dma_start(K.w_out3(wW[0], C, G), K.w_in3("WoT", C)),
                  "sW", 16)
        wmark = K.cnt["sW"]

        @blk.vector
        def _(v):
            K.w(v, "sW", wmark)
            for g in range(G):
                v.tensor_scalar_mul(wW[0][:, g * C:(g + 1) * C],
                                    wW[0][:, g * C:(g + 1) * C], 0.5)
            v.memset(K.col_str(sS[0], NT, SC, G), 0.0)
            i = v.memset(K.col_str(sS[1], NT, SC, G), 0.0)
            K.inc(i, "sV")
        wfold = K.cnt["sV"]

        marks = {}

        def emit_loads_eg(tau):
            up = tau + 1

            @blk.sync
            def _(sy, tau=tau, up=up):
                K.w(sy, "sV", marks.get(("wkv", up), marks.get(("scan", up), 0)))
                K.inc(sy.dma_start(K.s_str(0, 0, NT), d["E1sp"][tau]),
                      "e1L", 16)
                K.inc(sy.dma_start(K.s_str(1, 0, NT), d["gsp"][tau]), "gL", 16)
                if 1 <= tau <= NI:
                    K.inc(sy.dma_start(fF[1][:, :], d["outfsp"][tau]),
                          "ofL", 16)
                    K.inc(sy.dma_start(fF[2][:, :], d["rsp"][tau]), "rL", 16)
            marks[("lde", tau)] = K.cnt["e1L"]
            marks[("ldg", tau)] = K.cnt["gL"]
            marks[("ldo", tau)] = K.cnt["ofL"]
            marks[("ldr", tau)] = K.cnt["rL"]

        def emit_loads_z(tau):
            up = tau + 1

            @blk.sync
            def _(sy, tau=tau, up=up):
                K.w(sy, "a1S", marks.get(("spa1", up), 0))
                K.inc(sy.dma_start(fF[0][:, :], d["zsp"][tau]), "zL", 16)
            marks[("ldz", tau)] = K.cnt["zL"]

        emit_loads_eg(NTILES - 1)
        emit_loads_z(NTILES - 2)
        for tau in range(NTILES - 1, 0, -1):
            interior = 1 <= tau <= NI

            dcm = [0]
            if tau < NTILES - 1:
                @blk.vector
                def _(v, tau=tau):
                    i = v.tensor_sub(s["sm"][:, 8:16], K.ccol(tau),
                                     K.ccol(tau + 1))
                    dcm[0] = K.inc(i, "sV")

                @blk.scalar
                def _(sc):
                    K.w(sc, "sV", dcm[0])
                    i = sc.activation(s["sm"][:, 16:24], s["sm"][:, 8:16],
                                      Act.Exp)
                    K.inc(i, "sC")
            rscm = K.cnt["sC"]

            scn = [0]

            @blk.vector
            def _(v, tau=tau, rscm=rscm):
                K.w(v, "e1L", marks[("lde", tau)])
                K.w(v, "gL", marks[("ldg", tau)])
                if tau < NTILES - 1:
                    K.w(v, "sC", rscm)
                    v.tensor_mul(K.col_str(sS[1], NT, SC, G),
                                 K.col_str(sS[2], NT, SC, G), s["sm"][:, 16:24])
                    v.tensor_mul(K.col_str(sS[0], NT, SC, G),
                                 K.col_str(sS[3], NT, SC, G),
                                 s["sm"][:, 16:24])
                    v.drain()
                for g in range(G):
                    v.tensor_tensor_scan(
                        K.sdat(2, g, 0, SC), K.ew_b(g, SC),
                        K.s_rev(1, g, NT, SC), 0.0, Alu.mult, Alu.add)
                    i = v.tensor_tensor_scan(
                        K.sdat(3, g, 0, SC), K.ew_b(g, SC),
                        K.s_rev(0, g, NT, SC), 0.0, Alu.mult, Alu.add)
                scn[0] = K.inc(i, "sV")
                if interior:
                    K.w(v, "sM", marks.get(("womm", tau + 1), 0))
                    for g in range(G):
                        v.scalar_tensor_tensor(
                            K.sdat(1, g, 0, NT), K.sdat(1, g, 0, NT),
                            K.vcol(V_EU, g), K.s_rev(2, g, NT - 1, NT),
                            Alu.mult, Alu.add)
                        i = v.scalar_tensor_tensor(
                            K.sdat(0, g, 0, NT), K.sdat(0, g, 0, NT),
                            K.vcol(V_EU, g), K.s_rev(3, g, NT - 1, NT),
                            Alu.mult, Alu.add)
                    K.inc(i, "sV")
            marks[("scan", tau)] = scn[0]
            denm = K.cnt["sV"]

            if not interior:
                if tau > 1:
                    emit_loads_eg(tau - 1)
                continue

            @blk.scalar
            def _(sc, denm=denm):
                K.w(sc, "sV", denm)
                sc.activation(K.s_str(0, 0, NT), K.s_str(0, 0, NT), Act.Ln)
                i = sc.activation(K.s_str(0, 0, NT), K.s_str(0, 0, NT),
                                  Act.Exp, scale=-1.0)
                K.inc(i, "sC")
            rdm = K.cnt["sC"]

            @blk.vector
            def _(v, tau=tau, rdm=rdm):
                K.w(v, "ofL", marks[("ldo", tau)])
                K.w(v, "rL", marks[("ldr", tau)])
                v.tensor_mul(K.f_str(fF[1], 0, NT), K.f_str(fF[1], 0, NT),
                             K.f_str(fF[2], 0, NT))
                v.tensor_mul(K.s_str(1, 0, NT), K.s_str(1, 0, NT),
                             K.f_str(fF[2], 0, NT))
                K.w(v, "sC", rdm)
                v.tensor_mul(K.s_str(1, 0, NT), K.s_str(1, 0, NT),
                             K.s_str(0, 0, NT))
                i = v.tensor_add(K.f_str(s["bf1"], 0, NT),
                                 K.f_str(fF[1], 0, NT), K.s_str(1, 0, NT))
                K.inc(i, "sV")
            marks[("wkv", tau)] = K.cnt["sV"]
            if tau > 1:
                emit_loads_eg(tau - 1)

            mo = {}
            ev0 = K.cnt["sV"]

            @blk.tensor
            def _(t, tau=tau, ev0=ev0):
                K.w(t, "sV", wfold)
                K.w(t, "sV", marks[("wkv", tau)])
                for m in range(G):
                    if m >= 2:
                        K.w(t, "sV", ev0 + m - 1)
                    else:
                        K.bank_wait(t, m)
                    for g in range(G):
                        i = t.matmul(pb[m % 2][:, :], K.wsl(wW[0], g, m),
                                     K.fch(s["bf1"], g),
                                     start=(g == 0), stop=(g == G - 1))
                    mo[m] = K.inc(i, "sM")
            marks[("womm", tau)] = K.cnt["sM"]
            K.bankmark[0] = ("sV", ev0 + 7)
            K.bankmark[1] = ("sV", ev0 + 8)

            @blk.vector
            def _(v, tau=tau, ev0=ev0):
                for m in range(G):
                    K.w(v, "sM", mo[m])
                    if m == 0:
                        K.w(v, "zL", marks[("ldz", tau)])
                    i = v.tensor_add(K.fch(fF[0], m), K.fch(fF[0], m),
                                     pb[m % 2][:, :])
                    assert K.inc(i, "sV") == ev0 + m + 1
            marks[("evac", tau)] = K.cnt["sV"]
            marks[("a1", tau)] = K.cnt["sV"]

            @blk.sync
            def _(sy, tau=tau):
                K.w(sy, "sV", marks[("a1", tau)])
                K.inc(sy.dma_start(d["a1sp"][tau], fF[0][:, :]), "a1S", 16)
            marks[("spa1", tau)] = K.cnt["a1S"]
            if tau > 1:
                emit_loads_z(tau - 1)

        @blk.sync
        def _(sy):
            for n in ["a1S", "e1L", "gL", "zL", "ofL", "rL"]:
                K.w(sy, n, K.cnt[n])


def emit_pass_b2a(K):
    nc, s, d, pb, fF = K.nc, K.s, K.d, K.pb, K.fF
    K.bankmark = {}
    with nc.Block("passB2a", no_gpsimd_drain=True) as blk:
        @blk.gpsimd
        def _(gp):
            K.inc(gp.dma_start(K.w_out3(s["wbig"], H, G), K.w_in3("WfkT", H)),
                  "sW", 16)
        wmark = K.cnt["sW"]

        marks = {}

        def emit_ldfront(tau):
            @blk.sync
            def _(sy, tau=tau):
                K.w(sy, "sV", marks.get(("xn2", tau - 1), 0))
                K.inc(sy.dma_start(fF[0][:, :], d["a1sp"][tau]), "a1L", 16)
            marks[("ld", tau)] = K.cnt["a1L"]
            marks[("sm0", tau)] = emit_stats_front(
                K, blk, fF[0], pre=[("a1L", marks[("ld", tau)])], xsq_gp=True)

        def emit_xn2(tau):
            cmark = emit_stats_back(K, blk, [("st2", "r")],
                                    m0=marks[("sm0", tau)])

            @blk.vector
            def _(v, tau=tau, cmark=cmark):
                K.w(v, "sC", cmark)
                K.w(v, "xnS", marks.get(("spx", tau - 1), 0))
                K.w(v, "sM", marks.get(("fkmm", tau - 1), 0))
                for g in range(G):
                    v.tensor_sub(s["tdx"][:, :], K.fch(fF[0], g), s["st0"][:, :])
                    v.tensor_mul(s["tt"][:, :], s["tdx"][:, :], s["st2"][:, :])
                    i = v.tensor_scalar(K.fch(s["bf1"], g), s["tt"][:, :],
                                        K.vcol(V_L2W, g), K.vcol(V_L2B, g),
                                        Alu.mult, Alu.add)
                K.inc(i, "sV")
            marks[("xn2", tau)] = K.cnt["sV"]

            @blk.sync
            def _(sy, tau=tau):
                K.w(sy, "sV", marks[("xn2", tau)])
                K.inc(sy.dma_start(d["xn2sp"][tau], s["bf1"][:, :]), "xnS", 16)
            marks[("spx", tau)] = K.cnt["xnS"]

        def emit_wfk(tau):
            sc0 = K.cnt["sC"]
            mh = {}
            marks[("mh", tau)] = mh
            marks[("sc0", tau)] = sc0

            @blk.tensor
            def _(t, tau=tau, sc0=sc0):
                K.w(t, "sW", wmark)
                K.w(t, "sV", marks[("xn2", tau)])
                for h in range(GH):
                    if h >= 4:
                        K.w(t, "sC", sc0 + h - 3)
                    else:
                        K.bank_wait(t, 2 + h)
                    for g in range(G):
                        i = t.matmul(pb[2 + (h % 4)][:, :],
                                     K.wsl_h(s["wbig"], g, H, h),
                                     K.fch(s["bf1"], g),
                                     start=(g == 0), stop=(g == G - 1))
                    mh[h] = K.inc(i, "sM")
            marks[("fkmm", tau)] = K.cnt["sM"]
            for b in range(4):
                K.bankmark[2 + b] = ("sC", sc0 + GH - 3 + b)

            sqv0 = K.cnt["sV"]
            marks[("sqv0", tau)] = sqv0

            @blk.scalar
            def _(sc, sc0=sc0, sqv0=sqv0):
                for h in range(GH):
                    K.w(sc, "sM", mh[h])
                    K.w(sc, "sV", sqv0 + h)
                    i = sc.activation(s["trel"][:, :], pb[2 + (h % 4)][:, :],
                                      Act.Relu)
                    assert K.inc(i, "sC") == sc0 + h + 1

        def emit_sq(tau):
            sc0 = marks[("sc0", tau)]
            sqv0 = marks[("sqv0", tau)]
            assert sqv0 == K.cnt["sV"]

            @blk.sync
            def _(sy, tau=tau, sqv0=sqv0):
                K.w(sy, "sV", sqv0 + 16)
                K.inc(sy.dma_start(d["kksp"][tau, 0], s["bf2"][:, :]),
                      "kkS", 16)
                marks[("spk", tau, 0)] = K.cnt["kkS"]
                K.w(sy, "sV", sqv0 + 32)
                K.inc(sy.dma_start(d["kksp"][tau, 1], s["bf2"][:, :]),
                      "kkS", 16)
                marks[("spk", tau, 1)] = K.cnt["kkS"]

            @blk.vector
            def _(v, tau=tau, sc0=sc0, sqv0=sqv0):
                for h in range(GH):
                    K.w(v, "sC", sc0 + h + 1)
                    if h == 0:
                        K.w(v, "kkS", marks.get(("spk", tau - 1, 1), 0))
                    elif h == 16:
                        K.w(v, "kkS", marks[("spk", tau, 0)])
                    i = v.tensor_mul(s["bf2"][:, (h % 16) * NT:
                                              (h % 16 + 1) * NT],
                                     s["trel"][:, :], s["trel"][:, :])
                    assert K.inc(i, "sV") == sqv0 + h + 1

        emit_ldfront(1)
        emit_xn2(1)
        for tau in range(1, NI + 1):
            emit_wfk(tau)
            if tau < NI:
                emit_ldfront(tau + 1)
            emit_sq(tau)
            if tau < NI:
                emit_xn2(tau + 1)

        @blk.sync
        def _(sy):
            for n in ["xnS", "kkS", "a1L"]:
                K.w(sy, n, K.cnt[n])


def emit_pass_b2b(K):
    nc, s, d, pb, fF, wW = K.nc, K.s, K.d, K.pb, K.fF, K.wW
    K.bankmark = {}
    qv = s["qv"]
    with nc.Block("passB2b", no_gpsimd_drain=True) as blk:
        b2a_sc = K.cnt["sC"]
        b2a_sm = K.cnt["sM"]

        @blk.gpsimd
        def _(gp):
            K.inc(gp.dma_start(K.w_out3(wW[0], C, G), K.w_in3("WfrT", C)),
                  "sW", 16)
            K.inc(gp.dma_start(K.w_out3(s["w8stg"], C, G), K.w_in3("w8T", C)),
                  "sW", 16)
            K.inc(gp.dma_start(K.w_out3(s["wbig"], C, GH), K.w_in3("WfvT", C)),
                  "sW", 16)
        wmark = K.cnt["sW"]

        @blk.vector
        def _(v):
            K.w(v, "sW", wmark)
            for j in range(G * C // NT):
                v.tensor_copy(s["tw"][:, :], s["w8stg"][:, j * NT:(j + 1) * NT])
                i = v.tensor_copy(wW[1][:, j * NT:(j + 1) * NT], s["tw"][:, :])
            K.inc(i, "sV")
        w8c = K.cnt["sV"]

        dbg = "dbg" in DEBUG_SPILLS
        marks = {}

        @blk.sync
        def _(sy):
            K.w(sy, "sM", b2a_sm)
            K.inc(sy.dma_start(s["bf1"][:, :], d["xn2sp"][1]), "xnL", 16)
            K.inc(sy.dma_start(s["bf2"][:, 0:8 * NT],
                               d["kksp"][1, 0][:, 0:8 * NT]), "kkL0", 16)
        marks[("ldxn", 1)] = K.cnt["xnL"]
        marks[("ldk", 1, 0)] = K.cnt["kkL0"]

        def emit_wfr(tau):
            sc0 = K.cnt["sC"]
            marks[("sc0", tau)] = sc0
            mf = {}
            marks[("mf", tau)] = mf
            if tau > 1:
                marks[("stcp", tau - 1)] = K.cnt["sV"] + 1

            @blk.tensor
            def _(t, tau=tau, sc0=sc0):
                K.w(t, "sW", wmark)
                K.w(t, "xnL", marks[("ldxn", tau)])
                for m in range(G):
                    if m >= 2:
                        K.w(t, "sC", sc0 + m - 1)
                    elif m == 0:
                        if tau == 1:
                            K.w(t, "sC", b2a_sc)
                        else:
                            K.w(t, "sV", marks[("stcp", tau - 1)])
                    for g in range(G):
                        i = t.matmul(pb[4 + (m % 2)][:, :], K.wsl(wW[0], g, m),
                                     K.fch(s["bf1"], g),
                                     start=(g == 0), stop=(g == G - 1))
                    mf[m] = K.inc(i, "sM")
            marks[("wfr", tau)] = K.cnt["sM"]

        def emit_sgate(tau):
            sc0 = marks[("sc0", tau)]
            mf = marks[("mf", tau)]

            @blk.scalar
            def _(sc, tau=tau, sc0=sc0, mf=mf):
                for m in range(G):
                    K.w(sc, "sM", mf[m])
                    if m == 0:
                        if tau == 1:
                            K.w(sc, "sV", w8c)
                        K.w(sc, "sV", marks.get(("a2", tau - 1), 0))
                        if dbg:
                            K.w(sc, "dbg4", marks.get(("dsp4", tau - 1), 0))
                    sc.activation(K.fch(fF[2], m), pb[4 + (m % 2)][:, :],
                                  Act.Exp, scale=-1.0)
                    sc.activation(K.fch(fF[2], m), K.fch(fF[2], m),
                                  Act.Ln, bias=s["onec"][:, 0:1])
                    i = sc.activation(K.fch(fF[2], m), K.fch(fF[2], m),
                                      Act.Exp, scale=-1.0)
                    assert K.inc(i, "sC") == sc0 + m + 1
            marks[("sgate", tau)] = K.cnt["sC"]

            if tau < NI:
                @blk.sync
                def _(sy, tau=tau):
                    K.w(sy, "sM", marks[("wfr", tau)])
                    K.inc(sy.dma_start(s["bf1"][:, :], d["xn2sp"][tau + 1]),
                          "xnL", 16)
                marks[("ldxn", tau + 1)] = K.cnt["xnL"]

        def emit_z2r0(tau):
            m0 = K.cnt["sM"]
            marks[("z2m0", tau)] = m0

            @blk.sync
            def _(sy, tau=tau, m0=m0):
                for step in range(1, 4):
                    prev = (m0 + step - 1) if step >= 2 else \
                        marks.get(("zmm", tau - 1, 7), 0)
                    K.w(sy, "sM", prev)
                    srcv = d["kksp"][tau, step // 2][:, (step % 2) * 8 * NT:
                                                     ((step % 2) + 1) * 8 * NT]
                    sem = "kkL0" if step % 2 == 0 else "kkL1"
                    K.inc(sy.dma_start(
                        s["bf2"][:, (step % 2) * 8 * NT:
                                 ((step % 2) + 1) * 8 * NT], srcv), sem, 16)
                    marks[("ldk", tau, step)] = K.cnt[sem]

            @blk.tensor
            def _(t, tau=tau, m0=m0):
                K.w(t, "sW", wmark)
                for step in range(4):
                    sem = "kkL0" if step % 2 == 0 else "kkL1"
                    K.w(t, sem, marks[("ldk", tau, step)])
                    if step == 0:
                        for mm_ in range(4):
                            K.bank_wait(t, mm_)
                    for hh in range(8):
                        h = step * 8 + hh
                        rhs = s["bf2"][:, ((step % 2) * 8 + hh) * NT:
                                       ((step % 2) * 8 + hh + 1) * NT]
                        for mm_ in range(4):
                            i = t.matmul(pb[mm_][:, :],
                                         K.wsl_h(s["wbig"], h, C, 4 + mm_),
                                         rhs, start=(h == 0), stop=(h == 31))
                    assert K.inc(i, "sM") == m0 + step + 1

        def emit_tail_a(tau):
            e80 = K.cnt["sV"]
            m8 = {}

            @blk.tensor
            def _(t, tau=tau, e80=e80):
                K.w(t, "sV", max(marks[("qm", tau)], w8c))
                for m in range(G):
                    if m >= 2:
                        K.w(t, "sV", e80 + m - 1)
                    for g in range(G):
                        i = t.matmul(pb[6 + (m % 2)][:, :], K.wsl(wW[1], g, m),
                                     K.fch(qv, g),
                                     start=(g == 0), stop=(g == G - 1))
                    m8[m] = K.inc(i, "sM")
            marks[("w8mm", tau)] = K.cnt["sM"]
            K.bankmark[6] = ("sV", e80 + 7)
            K.bankmark[7] = ("sV", e80 + 8)

            @blk.vector
            def _(v, tau=tau, e80=e80):
                for m in range(G):
                    K.w(v, "sM", m8[m])
                    if m == 0 and dbg:
                        K.w(v, "dbg3", marks.get(("dsp3", tau - 1), 0))
                    v.tensor_mul(K.fch(fF[3], m), pb[6 + (m % 2)][:, :],
                                 s["st3"][:, :])
                    i = v.tensor_scalar_add(K.fch(fF[3], m), K.fch(fF[3], m),
                                            K.vcol(V_FB, m))
                    assert K.inc(i, "sV") == e80 + m + 1
            marks[("f", tau)] = K.cnt["sV"]

        def emit_loads_a(tau):
            @blk.sync
            def _(sy, tau=tau):
                if tau == 1:
                    K.w(sy, "sV", w8c)
                K.w(sy, "sV", marks.get(("qm", tau - 1), 0))
                if dbg:
                    K.w(sy, "dbg0", marks.get(("dsp0", tau - 1), 0))
                    K.w(sy, "dbg2", marks.get(("dsp2", tau - 1), 0))
                K.inc(sy.dma_start(fF[1][:, 4 * NT:8 * NT],
                                   d["a1sp"][tau][:, 4 * NT:8 * NT]),
                      "a1L2", 16)
                K.w(sy, "sM", marks.get(("w8mm", tau - 1), 0))
                K.inc(sy.dma_start(fF[1][:, 0:4 * NT],
                                   d["a1sp"][tau][:, 0:4 * NT]), "a1L", 16)
            marks[("lda1h", tau)] = K.cnt["a1L2"]
            marks[("lda1", tau)] = K.cnt["a1L"]

        def emit_a2r0(tau):
            m0 = marks[("z2m0", tau)]
            ev0 = K.cnt["sV"]

            @blk.vector
            def _(v, tau=tau, m0=m0, ev0=ev0):
                for mm_ in range(4):
                    m = 4 + mm_
                    K.w(v, "sM", m0 + 4)
                    if mm_ == 0:
                        K.w(v, "sC", marks[("sgate", tau)])
                        K.w(v, "a1L2", marks[("lda1h", tau)])
                    v.tensor_mul(s["tt"][:, :], K.fch(fF[2], m), pb[mm_][:, :])
                    i = v.tensor_add(K.fch(fF[1], m), K.fch(fF[1], m),
                                     s["tt"][:, :])
                    assert K.inc(i, "sV") == ev0 + mm_ + 1
            marks[("a2r0", tau)] = K.cnt["sV"]

        def emit_z2r1(tau):
            m1 = K.cnt["sM"]
            marks[("z2m1", tau)] = m1
            ar0 = marks[("a2r0", tau)]
            marks[("ldk", tau, 6)] = K.cnt["kkL0"] + 16
            marks[("ldk", tau, 7)] = K.cnt["kkL1"] + 16

            @blk.tensor
            def _(t, tau=tau, m1=m1, ar0=ar0):
                for step in range(4, 8):
                    half = (2, 3, 0, 1)[step - 4]
                    if step >= 6:
                        sem = "kkL0" if step % 2 == 0 else "kkL1"
                        K.w(t, sem, marks[("ldk", tau, step)])
                    if step == 4:
                        for mm_ in range(4):
                            K.w(t, "sV", ar0 - 3 + mm_)
                    for hh in range(8):
                        h = half * 8 + hh
                        rhs = s["bf2"][:, ((step % 2) * 8 + hh) * NT:
                                       ((step % 2) * 8 + hh + 1) * NT]
                        for mm_ in range(4):
                            i = t.matmul(pb[mm_][:, :],
                                         K.wsl_h(s["wbig"], h, C, mm_),
                                         rhs, start=(step == 4 and hh == 0),
                                         stop=(step == 7 and hh == 7))
                    assert K.inc(i, "sM") == m1 + step - 3
            marks[("zmm", tau, 6)] = m1 + 3
            marks[("zmm", tau, 7)] = m1 + 4

        def emit_loads_b(tau):
            m1 = marks[("z2m1", tau)]

            @blk.sync
            def _(sy, tau=tau, m1=m1):
                for step in (6, 7):
                    K.w(sy, "sM", m1 + step - 5)
                    srcv = d["kksp"][tau, 0][
                        :, (step % 2) * 8 * NT:((step % 2) + 1) * 8 * NT]
                    sem = "kkL0" if step % 2 == 0 else "kkL1"
                    K.inc(sy.dma_start(
                        s["bf2"][:, (step % 2) * 8 * NT:
                                 ((step % 2) + 1) * 8 * NT], srcv), sem, 16)
                    assert marks[("ldk", tau, step)] == K.cnt[sem]
                if tau < NI:
                    K.w(sy, "sM", m1 + 3)
                    K.inc(sy.dma_start(
                        s["bf2"][:, 0:8 * NT],
                        d["kksp"][tau + 1, 0][:, 0:8 * NT]), "kkL0", 16)
                    marks[("ldk", tau + 1, 0)] = K.cnt["kkL0"]

        def emit_tail_b(tau):
            if dbg:
                @blk.sync
                def _(sy, tau=tau):
                    K.w(sy, "dbg3", marks.get(("dsp3", tau - 1), 0))
                    K.w(sy, "sV", marks[("f", tau)])
                    K.inc(sy.dma_start(d["dbg"][tau, 3], fF[3][:, :]),
                          "dbg3", 16)
                marks[("dsp3", tau)] = K.cnt["dbg3"]

            cmark = emit_stats(K, blk, fF[3], [("st2", "r")],
                               pre=[("sV", marks[("f", tau)])], banks=(6, 7))

            @blk.vector
            def _(v, tau=tau, cmark=cmark):
                K.w(v, "sC", cmark)
                if dbg:
                    K.w(v, "dbg1", marks.get(("dsp1", tau), 0))
                for g in range(G):
                    v.tensor_sub(s["tdx"][:, :], K.fch(fF[3], g), s["st0"][:, :])
                    v.tensor_mul(s["tt"][:, :], s["tdx"][:, :], s["st2"][:, :])
                    i = v.tensor_add(K.fch(fF[0], g), K.fch(fF[0], g),
                                     s["tt"][:, :])
                K.inc(i, "sV")
            marks[("out", tau)] = K.cnt["sV"]

        def emit_spill_out(tau):
            @blk.sync
            def _(sy, tau=tau):
                K.w(sy, "outS", marks.get(("spo", tau - 1), 0))
                K.w(sy, "sV", marks[("out", tau)])
                K.inc(sy.dma_start(d["out"][tau - 1], fF[0][:, :]), "outS", 16)
            marks[("spo", tau)] = K.cnt["outS"]

            @blk.sync
            def _(sy, tau=tau):
                K.w(sy, "outS", marks[("spo", tau)])
                K.inc(sy.dma_start(K.f_str(fF[0], 0, NT), K.xT3(tau + 1)),
                      "xL", 16)
            marks[("ldx2", tau + 1)] = K.cnt["xL"]

        def emit_mid_a(tau):
            m1 = marks[("z2m1", tau)]

            @blk.vector
            def _(v, tau=tau, m1=m1):
                for mm_ in range(4):
                    K.w(v, "sM", m1 + 4)
                    if mm_ == 0:
                        K.w(v, "a1L", marks[("lda1", tau)])
                    v.tensor_mul(s["tt"][:, :], K.fch(fF[2], mm_),
                                 pb[mm_][:, :])
                    i = v.tensor_add(K.fch(fF[1], mm_), K.fch(fF[1], mm_),
                                     s["tt"][:, :])
                    K.inc(i, "sV")
                    K.bank_set(mm_, "sV")
            marks[("a2", tau)] = K.cnt["sV"]
            if dbg:
                @blk.sync
                def _(sy, tau=tau):
                    K.w(sy, "dbg0", marks.get(("dsp0", tau - 1), 0))
                    K.w(sy, "sV", marks[("a2", tau)])
                    K.inc(sy.dma_start(d["dbg"][tau, 0], fF[1][:, :]),
                          "dbg0", 16)
                    K.w(sy, "dbg4", marks.get(("dsp4", tau - 1), 0))
                    K.w(sy, "sC", marks[("sgate", tau)])
                    K.inc(sy.dma_start(d["dbg"][tau, 4], fF[2][:, :]),
                          "dbg4", 16)
                marks[("dsp0", tau)] = K.cnt["dbg0"]
                marks[("dsp4", tau)] = K.cnt["dbg4"]

            cmark = emit_stats(K, blk, fF[1], [("st2", "r")],
                               pre=[("sV", marks[("a2", tau)])], banks=(6, 7))

            @blk.vector
            def _(v, tau=tau, cmark=cmark):
                K.w(v, "sC", cmark)
                K.w(v, "xL", marks[("ldx2", tau)])
                for g in range(G):
                    v.tensor_sub(s["tdx"][:, :], K.fch(fF[1], g), s["st0"][:, :])
                    v.tensor_mul(s["tt"][:, :], s["tdx"][:, :], s["st2"][:, :])
                    i = v.tensor_add(K.fch(fF[0], g), K.fch(fF[0], g),
                                     s["tt"][:, :])
                K.inc(i, "sV")
            marks[("x1", tau)] = K.cnt["sV"]
            if dbg:
                @blk.sync
                def _(sy, tau=tau):
                    K.w(sy, "dbg1", marks.get(("dsp1", tau - 1), 0))
                    K.w(sy, "sV", marks[("x1", tau)])
                    K.inc(sy.dma_start(d["dbg"][tau, 1], fF[0][:, :]),
                          "dbg1", 16)
                marks[("dsp1", tau)] = K.cnt["dbg1"]

        def emit_mid_b(tau):
            cmark = emit_stats(K, blk, fF[0], [("st2", "tau")],
                               pre=[("sV", marks[("x1", tau)])], banks=(6, 7))

            @blk.vector
            def _(v, tau=tau, cmark=cmark):
                K.w(v, "sC", cmark)
                if dbg:
                    K.w(v, "dbg0", marks.get(("dsp0", tau), 0))
                for g in range(G):
                    v.tensor_sub(s["tdx"][:, :], K.fch(fF[0], g), s["st0"][:, :])
                    i = v.tensor_mul(K.fch(fF[1], g), s["tdx"][:, :],
                                     s["st2"][:, :])
                K.inc(i, "sV")
                v.scalar_tensor_tensor(s["tt"][:, :], K.fch(fF[1], 0), -1.0,
                                       K.fch(fF[1], 0), Alu.mult, Alu.max)
                for g in range(1, G):
                    v.scalar_tensor_tensor(s["tt2"][:, :], K.fch(fF[1], g),
                                           -1.0, K.fch(fF[1], g),
                                           Alu.mult, Alu.max)
                    v.tensor_max(s["tt"][:, :], s["tt"][:, :], s["tt2"][:, :])
                v.tensor_copy(s["txb"][:, 0:NT], s["tt"][:, :])
                v.tensor_sub(s["tt2"][:, :], s["tt"][:, :], s["txb"][:, 0:NT])
                i = v.tensor_copy(s["txq"][:, 0:NT], s["tt2"][:, :])
                K.inc(i, "sV")
            marks[("rm", tau)] = K.cnt["sV"]
            mxs = K.cnt["sV"]
            if dbg:
                @blk.sync
                def _(sy, tau=tau):
                    K.w(sy, "dbg2", marks.get(("dsp2", tau - 1), 0))
                    K.w(sy, "sV", marks[("rm", tau)])
                    K.inc(sy.dma_start(d["dbg"][tau, 2], fF[1][:, :]),
                          "dbg2", 16)
                marks[("dsp2", tau)] = K.cnt["dbg2"]

            @blk.tensor
            def _(t, mxs=mxs):
                K.w(t, "sV", mxs)
                K.bank_wait(t, 4)
                for j in range(NQ):
                    t.matmul(pb[4][0:QW, j * 128:(j + 1) * 128],
                             s["txb"][:, j * QW:(j + 1) * QW],
                             s["identb"][:, :], start=True, stop=False)
                    i = t.matmul(pb[4][0:QW, j * 128:(j + 1) * 128],
                                 s["txq"][:, j * QW:(j + 1) * QW],
                                 s["identb"][:, :], start=False, stop=True)
                K.inc(i, "sM")
            trm = K.cnt["sM"]

            @blk.vector
            def _(v, trm=trm):
                K.w(v, "sM", trm)
                for j in range(NQ):
                    i = v.tensor_reduce(s["sm"][0:QW, j:j + 1],
                                        pb[4][0:QW, j * 128:(j + 1) * 128],
                                        AX, Alu.max)
                K.inc(i, "sV")

            @blk.vector
            def _(v):
                v.drain()
                v.tensor_scalar_max(s["sm"][:, 4:4 + NQ], s["sm"][:, 0:NQ],
                                    EPS_ACT)
                v.drain()
                v.reciprocal(s["sm"][:, 8:8 + NQ], s["sm"][:, 4:4 + NQ])
                v.drain()
                v.tensor_scalar_mul(s["sm"][:, 8:8 + NQ], s["sm"][:, 8:8 + NQ],
                                    127.0)
                v.tensor_scalar_mul(s["sm"][:, 12:12 + NQ], s["sm"][:, 4:4 + NQ],
                                    s["wsbc"][:, 0:1])
                v.drain()
                v.tensor_scalar_mul(s["sm"][:, 12:12 + NQ],
                                    s["sm"][:, 12:12 + NQ], 1.0 / 127.0)
                v.drain()
                v.tensor_copy(K.col_str(s["hl8"], 0, 2, NQ), s["sm"][:, 8:8 + NQ])
                v.drain()
                v.tensor_sub(s["sm"][:, 16:16 + NQ], s["sm"][:, 8:8 + NQ],
                             K.col_str(s["hl8"], 0, 2, NQ))
                v.drain()
                v.tensor_copy(K.col_str(s["hl8"], 1, 2, NQ),
                              s["sm"][:, 16:16 + NQ])
                v.tensor_copy(K.col_str(s["hl8"], 2 * NQ, 2, NQ),
                              s["sm"][:, 12:12 + NQ])
                v.drain()
                v.tensor_sub(s["sm"][:, 16:16 + NQ], s["sm"][:, 12:12 + NQ],
                             K.col_str(s["hl8"], 2 * NQ, 2, NQ))
                v.drain()
                v.tensor_copy(K.col_str(s["hl8"], 2 * NQ + 1, 2, NQ),
                              s["sm"][:, 16:16 + NQ])
                v.drain()
                K.inc(v.engine_nop(), "sV")
            hlm = K.cnt["sV"]

            @blk.tensor
            def _(t, hlm=hlm):
                K.w(t, "sV", hlm)
                K.bank_wait(t, 5)
                t.matmul(pb[4][0:2 * NQ, 0:128], s["hl8"][:, 0:2 * NQ],
                         s["identb"][:, :], start=True, stop=True)
                i = t.matmul(pb[5][0:2 * NQ, 0:128],
                             s["hl8"][:, 2 * NQ:4 * NQ],
                             s["identb"][:, :], start=True, stop=True)
                K.inc(i, "sM")
            t2m = K.cnt["sM"]

            @blk.vector
            def _(v, t2m=t2m):
                K.w(v, "sM", t2m)
                v.tensor_copy(s["trows"][0:2 * NQ, 0:128],
                              pb[4][0:2 * NQ, 0:128])
                i = v.tensor_copy(s["trows"][0:2 * NQ, 128:256],
                                  pb[5][0:2 * NQ, 0:128])
                K.inc(i, "sV")
            trowm = K.cnt["sV"]

            @blk.tensor
            def _(t, trowm=trowm):
                K.w(t, "sV", trowm)
                for j in range(NQ):
                    t.matmul(pb[4][:, j * QW:(j + 1) * QW],
                             s["sel4"][0:2 * NQ, j * QW:(j + 1) * QW],
                             s["trows"][0:2 * NQ, 0:QW], start=True, stop=True)
                    i = t.matmul(pb[5][:, j * QW:(j + 1) * QW],
                                 s["sel4"][0:2 * NQ, j * QW:(j + 1) * QW],
                                 s["trows"][0:2 * NQ, 128:128 + QW],
                                 start=True, stop=True)
                K.inc(i, "sM")
            marks[("bcm", tau)] = K.cnt["sM"]

        def emit_quant(tau):
            bcm = marks[("bcm", tau)]

            @blk.vector
            def _(v, tau=tau, bcm=bcm):
                K.w(v, "sM", bcm)
                if dbg:
                    K.w(v, "dbg2", marks.get(("dsp2", tau), 0))
                v.tensor_copy(s["st1"][:, :], pb[4][:, :])
                i = v.tensor_copy(s["st3"][:, :], pb[5][:, :])
                K.inc(i, "sV")
                for g in range(G):
                    v.tensor_mul(s["tt2"][:, :], K.fch(fF[1], g), s["st1"][:, :])
                    v.tensor_copy(s["ti"][:, :], s["tt2"][:, :])
                    i = v.tensor_copy(K.fch(qv, g), s["ti"][:, :])
                K.inc(i, "sV")
            marks[("stc", tau)] = K.cnt["sV"] - 1
            marks[("qm", tau)] = K.cnt["sV"]
            if tau < NI:
                assert marks[("stcp", tau)] == marks[("stc", tau)]

        @blk.sync
        def _(sy):
            K.w(sy, "sV", w8c)
            K.inc(sy.dma_start(K.f_str(fF[0], 0, NT), K.xT3(1)), "xL", 16)
        marks[("ldx2", 1)] = K.cnt["xL"]

        emit_wfr(1)
        emit_sgate(1)
        emit_z2r0(1)
        emit_loads_a(1)
        emit_a2r0(1)
        emit_z2r1(1)
        emit_loads_b(1)

        for tau in range(1, NI):
            emit_mid_a(tau)
            emit_z2r0(tau + 1)
            emit_mid_b(tau)
            emit_wfr(tau + 1)
            emit_quant(tau)
            emit_sgate(tau + 1)
            emit_tail_a(tau)
            emit_loads_a(tau + 1)
            emit_a2r0(tau + 1)
            emit_tail_b(tau)
            emit_z2r1(tau + 1)
            emit_loads_b(tau + 1)
            emit_spill_out(tau)
        emit_mid_a(NI)
        emit_mid_b(NI)
        emit_quant(NI)
        emit_tail_a(NI)
        emit_tail_b(NI)
        emit_spill_out(NI)

        @blk.sync
        def _(sy):
            for n in ["outS", "xL", "a1L", "a1L2", "xnL", "kkL0", "kkL1",
                      "dbg0", "dbg1", "dbg2", "dbg3", "dbg4"]:
                K.w(sy, n, K.cnt[n])


def build_K():
    K = Builder()
    emit_pass_a(K)
    emit_pass_b1(K)
    emit_pass_b2a(K)
    emit_pass_b2b(K)
    return K


def build():
    return build_K().nc


def stage_inputs(inputs):
    x = np.asarray(inputs["L0_attn_x"])
    fm = lambda v: np.ascontiguousarray(np.asarray(v).reshape(G, 128).T)
    ct = lambda w: np.ascontiguousarray(np.asarray(w).T)
    common = {
        "WkT": ct(inputs["L0_attn_Wk"]), "WvT": ct(inputs["L0_attn_Wv"]),
        "WrT": ct(inputs["L0_attn_Wr"]), "WoT": ct(inputs["L0_attn_Wo"]),
        "WfrT": ct(inputs["L0_attn_Wfr"]), "WfkT": ct(inputs["L0_attn_Wfk"]),
        "WfvT": ct(inputs["L0_attn_Wfv"]), "w8T": ct(inputs["L0_ffn_w_int8"]),
        "ln1w": fm(inputs["L0_attn_ln1_w"]), "ln1b": fm(inputs["L0_attn_ln1_b"]),
        "ln2w": fm(inputs["L0_attn_ln2_w"]), "ln2b": fm(inputs["L0_attn_ln2_b"]),
        "decay": fm(inputs["L0_attn_decay"]), "uu": fm(inputs["L0_attn_u"]),
        "fbias": fm(inputs["L0_ffn_bias"]),
        "wst": np.asarray(inputs["L0_ffn_weight_scale"],
                          np.float32).reshape(1, 1),
        "onesb": np.ones((128, 128), ml_dtypes.bfloat16),
        "identb": np.eye(128, dtype=ml_dtypes.bfloat16),
        "onesf": np.ones((1, 128), np.float32),
        "sel4": np.ascontiguousarray(
            np.repeat(np.eye(NQ, dtype=np.float32), 2, axis=0)
            .repeat(QW, axis=1)).astype(ml_dtypes.bfloat16),
    }
    in_maps = []
    for core in range(8):
        b, q = core // 4, core % 4
        lo = q * NCORE - NT
        xT = np.zeros((C, NW), np.float32)
        a, bb = max(lo, 0), min(lo + NW, T)
        xT[:, a - lo: bb - lo] = x[b, a:bb, :].T
        mask = np.ones((128, 2 * NT), np.float32)
        if q == 0:
            mask[:, 0:NT] = 0.0
        if q == 3:
            mask[:, NT:2 * NT] = 0.0
        in_maps.append({**common,
                        "xT": np.ascontiguousarray(xT),
                        "mask2": mask.astype(ml_dtypes.bfloat16)})
    return in_maps


def assemble(results):
    out = np.empty((B, T, C), np.float32)
    for core, r in enumerate(results):
        b, q = core // 4, core % 4
        o = np.asarray(r["out"])
        blk = o.reshape(NI, 128, G, NT).transpose(0, 3, 2, 1).reshape(NCORE, C)
        out[b, q * NCORE:(q + 1) * NCORE, :] = blk
    return out


_NC_CACHE = [None]


def kernel(**inputs):
    in_maps = stage_inputs(inputs)
    if _NC_CACHE[0] is None:
        _NC_CACHE[0] = build()
    res = run_bass_kernel_spmd(_NC_CACHE[0], in_maps, core_ids=list(range(8)))
    return assemble(res.results)


# version 21
# speedup vs baseline: 1.0427x; 1.0427x over previous
import numpy as np
import ml_dtypes
import concourse.bass as bass
import concourse.mybir as mybir
from concourse.bass_utils import run_bass_kernel_spmd
from concourse.ap import AP

F32 = mybir.dt.float32
BF16 = mybir.dt.bfloat16
I32 = mybir.dt.int32
Alu = mybir.AluOpType
Act = mybir.ActivationFunctionType
AX = mybir.AxisListType.X

DEBUG_SPILLS = ()

B, T, C, G, H, GH = 2, 16384, 1024, 8, 4096, 32
NT = 512
NTILES = 10
NI = NTILES - 2
NW = NT * NTILES
NCORE = NT * NI
SC = NT + 1
NQ = max(NT // 128, 1)
QW = min(NT, 128)
EPS = 1e-5
EPS_RMS = 1e-6
EPS_ACT = 1e-5

V_EW, V_EU, V_U, V_L1W, V_L1B, V_L2W, V_L2B, V_FB = range(8)


class Builder:
    def __init__(self):
        nc = bass.Bass("TRN2", target_bir_lowering=False, debug=False,
                       detect_race_conditions=False)
        self.nc = nc

        d = self.d = {}
        for n, shp, dt in [
            ("xT", [C, NW], F32), ("mask2", [128, 2 * NT], BF16),
            ("WkT", [C, C], F32), ("WvT", [C, C], F32), ("WrT", [C, C], F32),
            ("WoT", [C, C], F32), ("WfrT", [C, C], F32),
            ("WfkT", [C, H], F32), ("WfvT", [H, C], F32), ("w8T", [C, C], I32),
            ("ln1w", [128, G], F32), ("ln1b", [128, G], F32),
            ("ln2w", [128, G], F32), ("ln2b", [128, G], F32),
            ("decay", [128, G], F32), ("uu", [128, G], F32),
            ("fbias", [128, G], F32), ("wst", [1, 1], F32),
            ("onesb", [128, 128], BF16), ("identb", [128, 128], BF16),
            ("onesf", [1, 128], F32),
            ("sel4", [2 * NQ, NQ * QW], BF16),
        ]:
            d[n] = nc.dram_tensor(n, shp, dt, kind="ExternalInput").ap()

        d["out"] = nc.dram_tensor("out", [NI, 128, G * NT], F32,
                                  kind="ExternalOutput").ap()
        for n, shp, dt in [
            ("E1sp", [NTILES, 128, G * NT], F32),
            ("gsp", [NTILES, 128, G * NT], F32),
            ("zsp", [NTILES, 128, G * NT], F32),
            ("outfsp", [NTILES, 128, G * NT], F32),
            ("rsp", [NTILES, 128, G * NT], F32),
            ("a1sp", [NTILES, 128, G * NT], F32),
            ("xn2sp", [NTILES, 128, G * NT], BF16),
            ("kksp", [NTILES, 2, 128, 16 * NT], BF16),
            ("dbg", [NTILES, 6, 128, G * NT], F32),
        ]:
            kind = "ExternalOutput" if n in DEBUG_SPILLS else "Internal"
            d[n] = nc.dram_tensor(n, shp, dt, kind=kind).ap()

        s = self.s = {}
        avail = nc.sbuf_bytes_remaining
        arena = nc.alloc_sbuf_tensor("arena", [128, avail - 64],
                                     mybir.dt.uint8)
        base = nc.lookup_mloc(arena).addr
        self._off = base
        A = self._place

        s["ones"] = self.at("ones", [128, 128], BF16, A(256))
        s["identb"] = self.at("identb_s", [128, 128], BF16, A(256))
        s["sel4"] = self.at("sel4_s", [128, NQ * QW], BF16, A(1024))
        s["mask"] = self.at("mask_s", [128, 2 * NT], BF16, A(2 * NT * 2))
        s["c"] = self.at("c_s", [128, G * NTILES], F32, A(G * NTILES * 4))
        s["vec"] = self.at("vec_s", [128, 8 * G], F32, A(8 * G * 4))
        s["wsbc"] = self.at("wsbc_s", [128, 1], F32, A(32))
        s["onec"] = self.at("onec_s", [128, 1], F32, A(32))
        s["onesf"] = self.at("onesf_s", [128, 128], F32, A(512))
        s["wst"] = self.at("wst_s", [128, 1], F32, A(32))
        F4 = NT * 4
        toff = A(5 * F4 + 1024)
        s["tdx"] = self.at("tdx", [128, NT], F32, toff)
        s["ti"] = self.at("ti", [128, NT], I32, toff)
        s["tv"] = self.at("tv", [128, NT], F32, toff + F4)
        s["trel"] = self.at("trel", [128, NT], BF16, toff + F4)
        s["tt2"] = self.at("tt2", [128, NT], F32, toff + F4)
        s["ti2"] = self.at("ti2", [128, NT], I32, toff + F4)
        s["tt"] = self.at("tt", [128, NT], F32, toff + 2 * F4)
        s["tw"] = self.at("tw", [128, NT], F32, toff + 2 * F4)
        s["txb"] = self.at("txb", [128, 2 * NT], BF16, toff + 3 * F4)
        s["txq"] = self.at("txq", [128, 2 * NT], BF16, toff + 4 * F4)
        s["sm"] = self.at("sm", [128, 32], F32, toff + 5 * F4)
        s["hl8"] = self.at("hl8", [128, 4 * NQ], BF16, toff + 5 * F4 + 128)
        s["trows"] = self.at("trows", [128, 256], BF16, toff + 5 * F4 + 256)
        s["st0"] = self.at("st0", [128, NT], F32, A(NT * 4))
        s["st1"] = self.at("st1", [128, NT], F32, A(NT * 4))
        s["st2"] = self.at("st2", [128, NT], F32, A(NT * 4))
        s["st3"] = self.at("st3", [128, NT], F32, A(NT * 4))
        soff = A(max(4 * G * SC * 4, G * H * 2))
        self.sS = [self.at(f"s{i}", [128, G * SC], F32, soff + i * G * SC * 4)
                   for i in range(4)]
        s["wbig"] = self.at("wbig", [128, G * H], BF16, soff)
        foff = A(max(4 * G * NT * 4, G * C * 4))
        self.fF = [self.at(f"f{i}", [128, G * NT], F32, foff + i * G * NT * 4)
                   for i in range(4)]
        s["w8stg"] = self.at("w8stg", [128, G * C], I32, foff)
        s["qv"] = self.at("qv", [128, G * NT], BF16, foff + 1 * G * NT * 4)
        s["bf1"] = self.at("bf1", [128, G * NT], BF16, A(G * NT * 2))
        woff = A(3 * G * C * 2)
        self.wW = [self.at(f"w{i}", [128, G * C], BF16, woff + i * G * C * 2)
                   for i in range(3)]
        s["bf2"] = self.at("bf2", [128, 16 * NT], BF16, woff + 2 * G * C * 2)
        used = self._off - base
        assert used <= avail - 64, f"sbuf over budget: {used} > {avail - 64}"

        self.pb = [nc.alloc_psum_tensor(f"pb{i}", [128, NT], F32)
                   for i in range(8)]

        self.sems = {n: nc.alloc_semaphore(n) for n in
                     ["vecL", "xL", "e1L", "gL", "zL", "ofL", "rL", "xnL",
                      "kkL0", "kkL1", "a1L", "a1L2",
                      "zS", "e1S", "gS", "ofS", "rS", "a1S", "xnS", "kkS",
                      "outS", "sM", "sV", "sC", "sG", "sW",
                      "dbg0", "dbg1", "dbg2", "dbg3", "dbg4"]}
        self.cnt = {n: 0 for n in self.sems}
        self.bankmark = {}

    def _place(self, nbytes):
        a = self._off
        self._off = a + ((nbytes + 31) // 32) * 32
        return a

    def at(self, name, shape, dt, offset):
        return self.nc.alloc_sbuf_tensor_at(name, shape, dt, offset=offset)

    def inc(self, instr, sem, n=1):
        instr.then_inc(self.sems[sem], n)
        self.cnt[sem] += n
        return self.cnt[sem]

    def w(self, eng, sem, val):
        if val and val > 0:
            eng.wait_ge(self.sems[sem], val)

    def bank_wait(self, eng, i):
        m = self.bankmark.get(i)
        if m:
            self.w(eng, m[0], m[1])

    def bank_set(self, i, sem):
        self.bankmark[i] = (sem, self.cnt[sem])

    def fch(self, buf, g):
        return buf[:, g * NT:(g + 1) * NT]

    def sdat(self, i, g, a, b):
        return self.sS[i][:, g * SC + a: g * SC + b]

    def s_rev(self, i, g, a, n):
        t = self.sS[i].ap()
        return AP(t.tensor, t.offset + g * SC + a, [[t.ap[0][0], 128], [-1, n]])

    def s_str(self, i, a, b):
        t = self.sS[i].ap()
        return AP(t.tensor, t.offset + a,
                  [[t.ap[0][0], 128], [SC, G], [1, b - a]])

    def f_str(self, buf, a, b):
        t = buf.ap()
        return AP(t.tensor, t.offset + a,
                  [[t.ap[0][0], 128], [NT, G], [1, b - a]])

    def col_str(self, buf, col, pitch, n):
        t = buf.ap()
        return AP(t.tensor, t.offset + col, [[t.ap[0][0], 128], [pitch, n]])

    def vcol(self, i, g=None, n=1):
        if g is None:
            return self.s["vec"][:, i * G:(i + 1) * G]
        return self.s["vec"][:, i * G + g: i * G + g + n]

    def ccol(self, tau, g=None):
        if g is None:
            return self.s["c"][:, tau * G: (tau + 1) * G]
        return self.s["c"][:, tau * G + g: tau * G + g + 1]

    def ew_b(self, g, n):
        return self.vcol(V_EW, g).broadcast_to([128, n])

    def xT3(self, tau):
        t = self.d["xT"]
        return AP(t.tensor, t.offset + tau * NT,
                  [[NW, 128], [128 * NW, G], [1, NT]])

    def w_in3(self, name, cols):
        t = self.d[name]
        nchunks = t.shape[0] // 128
        return AP(t.tensor, t.offset,
                  [[cols, 128], [128 * cols, nchunks], [1, cols]])

    def w_out3(self, slot, cols, nchunks):
        t = slot.ap()
        return AP(t.tensor, t.offset,
                  [[t.ap[0][0], 128], [cols, nchunks], [1, cols]])

    def wsl(self, slot, g, m):
        return slot[:, g * C + m * 128: g * C + (m + 1) * 128]

    def wsl_h(self, slot, g, cols, m):
        return slot[:, g * cols + m * 128: g * cols + (m + 1) * 128]


def emit_stats_front(K, blk, src_full, pre=(), banks=(0, 1), xsq_gp=False):
    s, pb = K.s, K.pb
    b0, b1 = banks
    m0 = K.cnt["sM"]
    v0 = K.cnt["sV"]
    g0 = K.cnt["sG"]

    if xsq_gp:
        @blk.gpsimd
        def _(gp):
            for g in range(G):
                if g == 0:
                    for sem, val in pre:
                        K.w(gp, sem, val)
                if g >= 2:
                    K.w(gp, "sM", m0 + g - 1)
                i = gp.tensor_mul(s["txq"][:, (g % 2) * NT:(g % 2 + 1) * NT],
                                  K.fch(src_full, g), K.fch(src_full, g))
                K.inc(i, "sG")
        assert K.cnt["sG"] == g0 + G
    else:
        @blk.vector
        def _(v):
            for g in range(G):
                if g == 0:
                    for sem, val in pre:
                        K.w(v, sem, val)
                if g >= 2:
                    K.w(v, "sM", m0 + g - 1)
                i = v.tensor_mul(s["txq"][:, (g % 2) * NT:(g % 2 + 1) * NT],
                                 K.fch(src_full, g), K.fch(src_full, g))
                K.inc(i, "sV")
        assert K.cnt["sV"] == v0 + G

    @blk.tensor
    def _(t, xsq_gp=xsq_gp):
        for g in range(G):
            if g == 0:
                for sem, val in pre:
                    K.w(t, sem, val)
                K.bank_wait(t, b0)
                K.bank_wait(t, b1)
            t.matmul(pb[b0][:, :], s["onesf"][:, :], K.fch(src_full, g),
                     start=(g == 0), stop=(g == G - 1))
            if xsq_gp:
                K.w(t, "sG", g0 + g + 1)
            else:
                K.w(t, "sV", v0 + g + 1)
            i = t.matmul(pb[b1][:, :], s["ones"][:, :],
                         s["txq"][:, (g % 2) * NT:(g % 2 + 1) * NT],
                         start=(g == 0), stop=(g == G - 1))
            K.inc(i, "sM")
    assert K.cnt["sM"] == m0 + G
    return m0


def emit_stats_back(K, blk, coefs, banks=(0, 1), m0=None):
    s, pb = K.s, K.pb
    b0, b1 = banks

    @blk.vector
    def _(v):
        K.w(v, "sM", m0 + G)
        v.tensor_scalar_mul(s["st0"][:, :], pb[b0][:, :], 1.0 / C)
        i = v.tensor_scalar_mul(s["st1"][:, :], pb[b1][:, :], 1.0 / C)
        K.inc(i, "sV")
        v.tensor_mul(s["st3"][:, :], s["st0"][:, :], s["st0"][:, :])
        v.tensor_sub(s["st1"][:, :], s["st1"][:, :], s["st3"][:, :])
        for dst, kind in coefs:
            if kind == "r":
                i = v.tensor_scalar_add(s[dst][:, :], s["st1"][:, :], EPS)
            elif kind == "rho":
                i = v.tensor_scalar(s[dst][:, :], s["st1"][:, :],
                                    1.0 + EPS, EPS * EPS, Alu.mult, Alu.add)
            elif kind == "tau":
                i = v.tensor_scalar(s[dst][:, :], s["st1"][:, :],
                                    1.0 + EPS_RMS, EPS * EPS_RMS,
                                    Alu.mult, Alu.add)
        K.inc(i, "sV")
    K.bank_set(b0, "sV")
    K.bank_set(b1, "sV")
    vmark = K.cnt["sV"]

    @blk.scalar
    def _(sc):
        K.w(sc, "sV", vmark)
        for dst, kind in coefs:
            sc.activation(s[dst][:, :], s[dst][:, :], Act.Ln)
            i = sc.activation(s[dst][:, :], s[dst][:, :], Act.Exp, scale=-0.5)
        K.inc(i, "sC")
    return K.cnt["sC"]


def emit_stats(K, blk, src_full, coefs, pre=(), banks=(0, 1)):
    m0 = emit_stats_front(K, blk, src_full, pre=pre, banks=banks)
    return emit_stats_back(K, blk, coefs, banks=banks, m0=m0)


def emit_pass_a(K):
    nc, s, d, pb, fF, sS, wW = K.nc, K.s, K.d, K.pb, K.fF, K.sS, K.wW
    with nc.Block("passA", no_gpsimd_drain=True) as blk:
        @blk.gpsimd
        def _(gp):
            K.inc(gp.dma_start(K.w_out3(wW[0], C, G), K.w_in3("WkT", C)),
                  "sW", 16)
            K.inc(gp.dma_start(K.w_out3(wW[1], C, G), K.w_in3("WvT", C)),
                  "sW", 16)
            K.inc(gp.dma_start(K.w_out3(wW[2], C, G), K.w_in3("WrT", C)),
                  "sW", 16)
        wmark = K.cnt["sW"]

        @blk.sync
        def _(sy):
            for dst, src in [
                (s["ones"][:, :], d["onesb"]), (s["identb"][:, :], d["identb"]),
                (s["sel4"][0:2 * NQ, :], d["sel4"]),
                (s["mask"][:, :], d["mask2"]),
                (K.vcol(V_L1W), d["ln1w"]), (K.vcol(V_L1B), d["ln1b"]),
                (K.vcol(V_L2W), d["ln2w"]), (K.vcol(V_L2B), d["ln2b"]),
                (K.vcol(V_U), d["uu"]), (K.vcol(V_EW), d["decay"]),
                (K.vcol(V_FB), d["fbias"]),
                (s["onesf"][0:1, :], d["onesf"]), (s["wst"][0:1, :], d["wst"]),
            ]:
                K.inc(sy.dma_start(dst, src), "vecL", 16)
        ld0 = K.cnt["vecL"]

        @blk.scalar
        def _(sc):
            K.w(sc, "vecL", ld0)
            sc.activation(K.vcol(V_EW), K.vcol(V_EW), Act.Exp)
            sc.drain()
            sc.activation(K.vcol(V_EW), K.vcol(V_EW), Act.Exp, scale=-1.0)
            sc.drain()
            sc.activation(K.vcol(V_EU), K.vcol(V_U), Act.Exp)
            sc.drain()
            K.inc(sc.nop(), "sC")
        prep_c = K.cnt["sC"]

        @blk.tensor
        def _(t):
            K.w(t, "vecL", ld0)
            i = t.matmul(pb[7][:, 0:1], s["onesf"][0:1, :], s["wst"][0:1, 0:1],
                         start=True, stop=True)
            K.inc(i, "sM")
        wsm = K.cnt["sM"]

        @blk.vector
        def _(v):
            K.w(v, "sM", wsm)
            v.tensor_copy(s["wsbc"][:, :], pb[7][:, 0:1])
            v.memset(s["onec"][:, :], 1.0)
            v.memset(s["onesf"][:, :], 1.0)
            v.memset(K.col_str(sS[0], 0, SC, G), 0.0)
            i = v.memset(K.col_str(sS[1], 0, SC, G), 0.0)
            K.inc(i, "sV")
        K.bank_set(7, "sV")
        prep_v = K.cnt["sV"]

        marks = {}
        statsmm = {}

        def emit_xstats(tau):
            @blk.sync
            def _(sy, tau=tau):
                K.w(sy, "sV", marks.get(("xfree", tau - 1), 0))
                K.inc(sy.dma_start(K.f_str(fF[0], 0, NT), K.xT3(tau)), "xL", 16)
            marks[("ldx", tau)] = K.cnt["xL"]

            pre = [("xL", marks[("ldx", tau)]), ("sC", prep_c),
                   ("sV", prep_v), ("sM", statsmm.get(tau - 1, 0))]
            marks[("cm", tau)] = emit_stats(K, blk, fF[0],
                                            [("st2", "r"), ("st3", "rho")],
                                            pre=pre)
            statsmm[tau] = K.cnt["sM"]

        def emit_dxz(tau):
            interior = 1 <= tau <= NI
            cmark = marks[("cm", tau)]

            @blk.vector
            def _(v, tau=tau, cmark=cmark, interior=interior):
                K.w(v, "sC", cmark)
                K.w(v, "zS", marks.get(("spz", tau - 1), 0))
                K.w(v, "sM", marks.get(("projmm", tau - 1), 0))
                for g in range(G):
                    v.tensor_sub(s["tdx"][:, :], K.fch(fF[0], g), s["st0"][:, :])
                    if interior:
                        v.tensor_mul(K.fch(fF[1], g), s["tdx"][:, :],
                                     s["st2"][:, :])
                    v.tensor_mul(s["tt"][:, :], s["tdx"][:, :], s["st3"][:, :])
                    i = v.tensor_scalar(K.fch(s["bf1"], g), s["tt"][:, :],
                                        K.vcol(V_L1W, g), K.vcol(V_L1B, g),
                                        Alu.mult, Alu.add)
                K.inc(i, "sV")
            marks[("xfree", tau)] = K.cnt["sV"]
            marks[("xn", tau)] = K.cnt["sV"]

            if interior:
                @blk.sync
                def _(sy, tau=tau):
                    K.w(sy, "sV", marks[("xn", tau)])
                    K.inc(sy.dma_start(d["zsp"][tau], fF[1][:, :]), "zS", 16)
                marks[("spz", tau)] = K.cnt["zS"]

        def emit_mloop(tau):
            interior = 1 <= tau <= NI
            per_m = 3 if interior else 2
            sc0 = K.cnt["sC"]
            scp = lambda m, j, sc0=sc0, per_m=per_m: sc0 + m * per_m + j
            mk, mv, mr = {}, {}, {}

            @blk.tensor
            def _(t, tau=tau, scp=scp, interior=interior):
                K.w(t, "sW", wmark)
                K.w(t, "sV", marks[("xn", tau)])
                for m in range(G):
                    if m >= 2:
                        K.w(t, "sC", scp(m - 2, 1))
                    else:
                        K.bank_wait(t, 2 + m)
                    for g in range(G):
                        i = t.matmul(pb[2 + (m % 2)][:, :],
                                     K.wsl(wW[0], g, m), K.fch(s["bf1"], g),
                                     start=(g == 0), stop=(g == G - 1))
                    mk[m] = K.inc(i, "sM")
                    if m >= 2:
                        K.w(t, "sC", scp(m - 2, 2))
                    else:
                        K.bank_wait(t, 4 + m)
                    for g in range(G):
                        i = t.matmul(pb[4 + (m % 2)][:, :],
                                     K.wsl(wW[1], g, m), K.fch(s["bf1"], g),
                                     start=(g == 0), stop=(g == G - 1))
                    mv[m] = K.inc(i, "sM")
                    if interior:
                        if m >= 2:
                            K.w(t, "sC", scp(m - 2, 3))
                        else:
                            K.bank_wait(t, 6 + m)
                        for g in range(G):
                            i = t.matmul(pb[6 + (m % 2)][:, :],
                                         K.wsl(wW[2], g, m), K.fch(s["bf1"], g),
                                         start=(g == 0), stop=(g == G - 1))
                        mr[m] = K.inc(i, "sM")
            marks[("projmm", tau)] = K.cnt["sM"]

            gv0 = K.cnt["sV"]
            vneg = {m: gv0 + 2 * m + 1 for m in range(G)}
            gmk = {m: gv0 + 2 * m + 2 for m in range(G - 1)}
            gmk[G - 1] = gv0 + 2 * G + (1 if tau > 0 else 0)
            dcm = [gv0 + 2 * G]

            @blk.scalar
            def _(sc, tau=tau, scp=scp, interior=interior):
                for m in range(G):
                    K.w(sc, "sV", vneg[m])
                    if m == 0:
                        K.w(sc, "e1S", marks.get(("spe", tau - 1), 0))
                        K.w(sc, "sV", marks.get(("outf", tau - 1), 0))
                    i = sc.activation(K.sdat(0, m, 1, SC),
                                      pb[2 + (m % 2)][:, :],
                                      Act.Exp, bias=K.ccol(tau, m))
                    assert K.inc(i, "sC") == scp(m, 1)
                    K.w(sc, "sM", mv[m])
                    if m == 0:
                        K.w(sc, "sV", marks.get(("eglast", tau - 1), 0))
                    else:
                        K.w(sc, "sV", gmk[m - 1])
                    i = sc.copy(s["tv"][:, :], pb[4 + (m % 2)][:, :])
                    assert K.inc(i, "sC") == scp(m, 2)
                    if interior:
                        K.w(sc, "sM", mr[m])
                        if m == 0:
                            K.w(sc, "rS", marks.get(("spr", tau - 1), 0))
                        sc.activation(K.fch(fF[3], m), pb[6 + (m % 2)][:, :],
                                      Act.Exp, scale=-1.0)
                        sc.drain()
                        sc.activation(K.fch(fF[3], m), K.fch(fF[3], m),
                                      Act.Ln, bias=s["onec"][:, 0:1])
                        sc.drain()
                        i = sc.activation(K.fch(fF[3], m), K.fch(fF[3], m),
                                          Act.Exp, scale=-1.0)
                        assert K.inc(i, "sC") == scp(m, 3)
                if tau > 0:
                    K.w(sc, "sV", dcm[0])
                    i = sc.activation(s["sm"][:, 16:24], s["sm"][:, 8:16],
                                      Act.Exp)
                    K.inc(i, "sC")
            marks[("rdone", tau)] = K.cnt["sC"]
            marks[("rscm", tau)] = K.cnt["sC"]
            K.bankmark[2] = ("sC", scp(6, 1))
            K.bankmark[3] = ("sC", scp(7, 1))
            K.bankmark[4] = ("sC", scp(6, 2))
            K.bankmark[5] = ("sC", scp(7, 2))
            if interior:
                K.bankmark[6] = ("sC", scp(6, 3))
                K.bankmark[7] = ("sC", scp(7, 3))

            @blk.vector
            def _(v, tau=tau, gv0=gv0, scp=scp):
                for m in range(G):
                    K.w(v, "sM", mk[m])
                    i = v.tensor_reduce(K.ccol(tau, m), pb[2 + (m % 2)][:, :],
                                        AX, Alu.max, negate=True)
                    assert K.inc(i, "sV") == vneg[m]
                    if m == G - 1 and tau > 0:
                        i = v.tensor_sub(s["sm"][:, 8:16], K.ccol(tau),
                                         K.ccol(tau - 1))
                        assert K.inc(i, "sV") == dcm[0]
                    K.w(v, "sC", scp(m, 2))
                    if m == 0:
                        K.w(v, "gS", marks.get(("spg", tau - 1), 0))
                    if tau == 0:
                        v.tensor_mul(K.sdat(0, m, 1, SC), K.sdat(0, m, 1, SC),
                                     s["mask"][:, 0:NT])
                    elif tau == NTILES - 1:
                        v.tensor_mul(K.sdat(0, m, 1, SC), K.sdat(0, m, 1, SC),
                                     s["mask"][:, NT:2 * NT])
                    i = v.tensor_mul(K.sdat(1, m, 1, SC), K.sdat(0, m, 1, SC),
                                     s["tv"][:, :])
                    assert K.inc(i, "sV") == gmk[m]
            marks[("eglast", tau)] = gmk[G - 1]

        def emit_scan(tau):
            interior = 1 <= tau <= NI
            den0 = K.cnt["sV"]
            rscm = marks[("rscm", tau)]

            @blk.vector
            def _(v, tau=tau, rscm=rscm, interior=interior):
                if tau > 0:
                    K.w(v, "sC", rscm)
                    v.tensor_mul(K.col_str(sS[1], 0, SC, G),
                                 K.col_str(sS[2], NT, SC, G), s["sm"][:, 16:24])
                    v.tensor_mul(K.col_str(sS[0], 0, SC, G),
                                 K.col_str(sS[3], NT, SC, G),
                                 s["sm"][:, 16:24])
                    v.drain()
                for g in range(G):
                    v.tensor_tensor_scan(
                        K.sdat(2, g, 0, SC), K.ew_b(g, SC), K.sdat(1, g, 0, SC),
                        0.0, Alu.mult, Alu.add)
                    i = v.tensor_tensor_scan(
                        K.sdat(3, g, 0, SC), K.ew_b(g, SC), K.sdat(0, g, 0, SC),
                        0.0, Alu.mult, Alu.add)
                K.inc(i, "sV")
                if interior:
                    for g in range(G):
                        v.scalar_tensor_tensor(
                            K.sdat(2, g, 0, NT), K.sdat(1, g, 1, SC),
                            K.vcol(V_EU, g), K.sdat(2, g, 0, NT),
                            Alu.mult, Alu.add)
                        i = v.scalar_tensor_tensor(
                            K.sdat(3, g, 0, NT), K.sdat(0, g, 1, SC),
                            K.vcol(V_EU, g), K.sdat(3, g, 0, NT),
                            Alu.mult, Alu.add)
                    K.inc(i, "sV")
            marks[("scan", tau)] = den0 + 1
            denm = K.cnt["sV"]

            if tau >= 1:
                @blk.sync
                def _(sy, tau=tau):
                    K.w(sy, "sV", marks[("scan", tau)])
                    K.inc(sy.dma_start(d["E1sp"][tau], K.s_str(0, 1, SC)),
                          "e1S", 16)
                    K.inc(sy.dma_start(d["gsp"][tau], K.s_str(1, 1, SC)),
                          "gS", 16)
                marks[("spe", tau)] = K.cnt["e1S"]
                marks[("spg", tau)] = K.cnt["gS"]

            if interior:
                @blk.sync
                def _(sy, tau=tau):
                    K.w(sy, "rS", marks.get(("spr", tau - 1), 0))
                    K.w(sy, "sC", marks[("rdone", tau)])
                    K.inc(sy.dma_start(d["rsp"][tau], fF[3][:, :]), "rS", 16)
                marks[("spr", tau)] = K.cnt["rS"]

                @blk.scalar
                def _(sc, denm=denm):
                    K.w(sc, "sV", denm)
                    sc.activation(K.s_str(3, 0, NT), K.s_str(3, 0, NT), Act.Ln)
                    i = sc.activation(K.s_str(3, 0, NT), K.s_str(3, 0, NT),
                                      Act.Exp, scale=-1.0)
                    K.inc(i, "sC")
                marks[("rdm", tau)] = K.cnt["sC"]

        def emit_outf(tau):
            rdm = marks[("rdm", tau)]

            @blk.vector
            def _(v, tau=tau, rdm=rdm):
                K.w(v, "sC", rdm)
                K.w(v, "ofS", marks.get(("spf", tau - 1), 0))
                i = v.tensor_mul(K.f_str(fF[2], 0, NT),
                                 K.s_str(2, 0, NT), K.s_str(3, 0, NT))
                K.inc(i, "sV")
            marks[("outf", tau)] = K.cnt["sV"]

            @blk.sync
            def _(sy, tau=tau):
                K.w(sy, "sV", marks[("outf", tau)])
                K.inc(sy.dma_start(d["outfsp"][tau], fF[2][:, :]),
                      "ofS", 16)
            marks[("spf", tau)] = K.cnt["ofS"]

        emit_xstats(0)
        emit_dxz(0)
        for tau in range(NTILES):
            emit_mloop(tau)
            if tau + 1 < NTILES:
                emit_xstats(tau + 1)
            emit_scan(tau)
            if tau + 1 < NTILES:
                emit_dxz(tau + 1)
            if 1 <= tau <= NI:
                emit_outf(tau)

        @blk.sync
        def _(sy):
            for n in ["zS", "e1S", "gS", "ofS", "rS", "xL", "vecL"]:
                K.w(sy, n, K.cnt[n])


def emit_pass_b1(K):
    nc, s, d, pb, fF, sS, wW = K.nc, K.s, K.d, K.pb, K.fF, K.sS, K.wW
    K.bankmark = {}
    with nc.Block("passB1", no_gpsimd_drain=True) as blk:
        @blk.gpsimd
        def _(gp):
            K.inc(gp.dma_start(K.w_out3(wW[0], C, G), K.w_in3("WoT", C)),
                  "sW", 16)
        wmark = K.cnt["sW"]

        @blk.vector
        def _(v):
            K.w(v, "sW", wmark)
            for g in range(G):
                v.tensor_scalar_mul(wW[0][:, g * C:(g + 1) * C],
                                    wW[0][:, g * C:(g + 1) * C], 0.5)
            v.memset(K.col_str(sS[0], NT, SC, G), 0.0)
            i = v.memset(K.col_str(sS[1], NT, SC, G), 0.0)
            K.inc(i, "sV")
        wfold = K.cnt["sV"]

        marks = {}

        def emit_loads_eg(tau):
            up = tau + 1

            @blk.sync
            def _(sy, tau=tau, up=up):
                K.w(sy, "sV", marks.get(("wkv", up), marks.get(("scan", up), 0)))
                K.inc(sy.dma_start(K.s_str(0, 0, NT), d["E1sp"][tau]),
                      "e1L", 16)
                K.inc(sy.dma_start(K.s_str(1, 0, NT), d["gsp"][tau]), "gL", 16)
                if 1 <= tau <= NI:
                    K.inc(sy.dma_start(fF[1][:, :], d["outfsp"][tau]),
                          "ofL", 16)
                    K.inc(sy.dma_start(fF[2][:, :], d["rsp"][tau]), "rL", 16)
            marks[("lde", tau)] = K.cnt["e1L"]
            marks[("ldg", tau)] = K.cnt["gL"]
            marks[("ldo", tau)] = K.cnt["ofL"]
            marks[("ldr", tau)] = K.cnt["rL"]

        def emit_loads_z(tau):
            up = tau + 1

            @blk.sync
            def _(sy, tau=tau, up=up):
                K.w(sy, "a1S", marks.get(("spa1", up), 0))
                K.inc(sy.dma_start(fF[0][:, :], d["zsp"][tau]), "zL", 16)
            marks[("ldz", tau)] = K.cnt["zL"]

        emit_loads_eg(NTILES - 1)
        emit_loads_z(NTILES - 2)
        for tau in range(NTILES - 1, 0, -1):
            interior = 1 <= tau <= NI

            dcm = [0]
            if tau < NTILES - 1:
                @blk.vector
                def _(v, tau=tau):
                    i = v.tensor_sub(s["sm"][:, 8:16], K.ccol(tau),
                                     K.ccol(tau + 1))
                    dcm[0] = K.inc(i, "sV")

                @blk.scalar
                def _(sc):
                    K.w(sc, "sV", dcm[0])
                    i = sc.activation(s["sm"][:, 16:24], s["sm"][:, 8:16],
                                      Act.Exp)
                    K.inc(i, "sC")
            rscm = K.cnt["sC"]

            scn = [0]

            @blk.vector
            def _(v, tau=tau, rscm=rscm):
                K.w(v, "e1L", marks[("lde", tau)])
                K.w(v, "gL", marks[("ldg", tau)])
                if tau < NTILES - 1:
                    K.w(v, "sC", rscm)
                    v.tensor_mul(K.col_str(sS[1], NT, SC, G),
                                 K.col_str(sS[2], NT, SC, G), s["sm"][:, 16:24])
                    v.tensor_mul(K.col_str(sS[0], NT, SC, G),
                                 K.col_str(sS[3], NT, SC, G),
                                 s["sm"][:, 16:24])
                    v.drain()
                for g in range(G):
                    v.tensor_tensor_scan(
                        K.sdat(2, g, 0, SC), K.ew_b(g, SC),
                        K.s_rev(1, g, NT, SC), 0.0, Alu.mult, Alu.add)
                    i = v.tensor_tensor_scan(
                        K.sdat(3, g, 0, SC), K.ew_b(g, SC),
                        K.s_rev(0, g, NT, SC), 0.0, Alu.mult, Alu.add)
                scn[0] = K.inc(i, "sV")
                if interior:
                    K.w(v, "sM", marks.get(("womm", tau + 1), 0))
                    for g in range(G):
                        v.scalar_tensor_tensor(
                            K.sdat(1, g, 0, NT), K.sdat(1, g, 0, NT),
                            K.vcol(V_EU, g), K.s_rev(2, g, NT - 1, NT),
                            Alu.mult, Alu.add)
                        i = v.scalar_tensor_tensor(
                            K.sdat(0, g, 0, NT), K.sdat(0, g, 0, NT),
                            K.vcol(V_EU, g), K.s_rev(3, g, NT - 1, NT),
                            Alu.mult, Alu.add)
                    K.inc(i, "sV")
            marks[("scan", tau)] = scn[0]
            denm = K.cnt["sV"]

            if not interior:
                if tau > 1:
                    emit_loads_eg(tau - 1)
                continue

            @blk.scalar
            def _(sc, denm=denm):
                K.w(sc, "sV", denm)
                sc.activation(K.s_str(0, 0, NT), K.s_str(0, 0, NT), Act.Ln)
                i = sc.activation(K.s_str(0, 0, NT), K.s_str(0, 0, NT),
                                  Act.Exp, scale=-1.0)
                K.inc(i, "sC")
            rdm = K.cnt["sC"]

            @blk.vector
            def _(v, tau=tau, rdm=rdm):
                K.w(v, "ofL", marks[("ldo", tau)])
                K.w(v, "rL", marks[("ldr", tau)])
                v.tensor_mul(K.f_str(fF[1], 0, NT), K.f_str(fF[1], 0, NT),
                             K.f_str(fF[2], 0, NT))
                v.tensor_mul(K.s_str(1, 0, NT), K.s_str(1, 0, NT),
                             K.f_str(fF[2], 0, NT))
                K.w(v, "sC", rdm)
                v.tensor_mul(K.s_str(1, 0, NT), K.s_str(1, 0, NT),
                             K.s_str(0, 0, NT))
                i = v.tensor_add(K.f_str(s["bf1"], 0, NT),
                                 K.f_str(fF[1], 0, NT), K.s_str(1, 0, NT))
                K.inc(i, "sV")
            marks[("wkv", tau)] = K.cnt["sV"]
            if tau > 1:
                emit_loads_eg(tau - 1)

            mo = {}
            ev0 = K.cnt["sV"]

            @blk.tensor
            def _(t, tau=tau, ev0=ev0):
                K.w(t, "sV", wfold)
                K.w(t, "sV", marks[("wkv", tau)])
                for m in range(G):
                    if m >= 2:
                        K.w(t, "sV", ev0 + m - 1)
                    else:
                        K.bank_wait(t, m)
                    for g in range(G):
                        i = t.matmul(pb[m % 2][:, :], K.wsl(wW[0], g, m),
                                     K.fch(s["bf1"], g),
                                     start=(g == 0), stop=(g == G - 1))
                    mo[m] = K.inc(i, "sM")
            marks[("womm", tau)] = K.cnt["sM"]
            K.bankmark[0] = ("sV", ev0 + 7)
            K.bankmark[1] = ("sV", ev0 + 8)

            @blk.vector
            def _(v, tau=tau, ev0=ev0):
                for m in range(G):
                    K.w(v, "sM", mo[m])
                    if m == 0:
                        K.w(v, "zL", marks[("ldz", tau)])
                    i = v.tensor_add(K.fch(fF[0], m), K.fch(fF[0], m),
                                     pb[m % 2][:, :])
                    assert K.inc(i, "sV") == ev0 + m + 1
            marks[("evac", tau)] = K.cnt["sV"]
            marks[("a1", tau)] = K.cnt["sV"]

            @blk.sync
            def _(sy, tau=tau):
                K.w(sy, "sV", marks[("a1", tau)])
                K.inc(sy.dma_start(d["a1sp"][tau], fF[0][:, :]), "a1S", 16)
            marks[("spa1", tau)] = K.cnt["a1S"]
            if tau > 1:
                emit_loads_z(tau - 1)

        @blk.sync
        def _(sy):
            for n in ["a1S", "e1L", "gL", "zL", "ofL", "rL"]:
                K.w(sy, n, K.cnt[n])


def emit_pass_b2a(K):
    nc, s, d, pb, fF = K.nc, K.s, K.d, K.pb, K.fF
    K.bankmark = {}
    with nc.Block("passB2a", no_gpsimd_drain=True) as blk:
        @blk.gpsimd
        def _(gp):
            K.inc(gp.dma_start(K.w_out3(s["wbig"], H, G), K.w_in3("WfkT", H)),
                  "sW", 16)
        wmark = K.cnt["sW"]

        marks = {}

        def emit_ldfront(tau):
            @blk.sync
            def _(sy, tau=tau):
                K.w(sy, "sV", marks.get(("xn2", tau - 1), 0))
                K.inc(sy.dma_start(fF[0][:, :], d["a1sp"][tau]), "a1L", 16)
            marks[("ld", tau)] = K.cnt["a1L"]
            marks[("sm0", tau)] = emit_stats_front(
                K, blk, fF[0], pre=[("a1L", marks[("ld", tau)])], xsq_gp=True)

        def emit_xn2(tau):
            cmark = emit_stats_back(K, blk, [("st2", "r")],
                                    m0=marks[("sm0", tau)])

            @blk.vector
            def _(v, tau=tau, cmark=cmark):
                K.w(v, "sC", cmark)
                K.w(v, "xnS", marks.get(("spx", tau - 1), 0))
                K.w(v, "sM", marks.get(("fkmm", tau - 1), 0))
                for g in range(G):
                    v.tensor_sub(s["tdx"][:, :], K.fch(fF[0], g), s["st0"][:, :])
                    v.tensor_mul(s["tt"][:, :], s["tdx"][:, :], s["st2"][:, :])
                    i = v.tensor_scalar(K.fch(s["bf1"], g), s["tt"][:, :],
                                        K.vcol(V_L2W, g), K.vcol(V_L2B, g),
                                        Alu.mult, Alu.add)
                K.inc(i, "sV")
            marks[("xn2", tau)] = K.cnt["sV"]

            @blk.sync
            def _(sy, tau=tau):
                K.w(sy, "sV", marks[("xn2", tau)])
                K.inc(sy.dma_start(d["xn2sp"][tau], s["bf1"][:, :]), "xnS", 16)
            marks[("spx", tau)] = K.cnt["xnS"]

        def emit_wfk(tau):
            sc0 = K.cnt["sC"]
            mh = {}
            marks[("mh", tau)] = mh
            marks[("sc0", tau)] = sc0

            @blk.tensor
            def _(t, tau=tau, sc0=sc0):
                K.w(t, "sW", wmark)
                K.w(t, "sV", marks[("xn2", tau)])
                for h in range(GH):
                    if h >= 4:
                        K.w(t, "sC", sc0 + h - 3)
                    else:
                        K.bank_wait(t, 2 + h)
                    for g in range(G):
                        i = t.matmul(pb[2 + (h % 4)][:, :],
                                     K.wsl_h(s["wbig"], g, H, h),
                                     K.fch(s["bf1"], g),
                                     start=(g == 0), stop=(g == G - 1))
                    mh[h] = K.inc(i, "sM")
            marks[("fkmm", tau)] = K.cnt["sM"]
            for b in range(4):
                K.bankmark[2 + b] = ("sC", sc0 + GH - 3 + b)

            sqv0 = K.cnt["sV"]
            marks[("sqv0", tau)] = sqv0

            @blk.scalar
            def _(sc, sc0=sc0, sqv0=sqv0):
                for h in range(GH):
                    K.w(sc, "sM", mh[h])
                    K.w(sc, "sV", sqv0 + h)
                    i = sc.activation(s["trel"][:, :], pb[2 + (h % 4)][:, :],
                                      Act.Relu)
                    assert K.inc(i, "sC") == sc0 + h + 1

        def emit_sq(tau):
            sc0 = marks[("sc0", tau)]
            sqv0 = marks[("sqv0", tau)]
            assert sqv0 == K.cnt["sV"]

            @blk.sync
            def _(sy, tau=tau, sqv0=sqv0):
                K.w(sy, "sV", sqv0 + 16)
                K.inc(sy.dma_start(d["kksp"][tau, 0], s["bf2"][:, :]),
                      "kkS", 16)
                marks[("spk", tau, 0)] = K.cnt["kkS"]
                K.w(sy, "sV", sqv0 + 32)
                K.inc(sy.dma_start(d["kksp"][tau, 1], s["bf2"][:, :]),
                      "kkS", 16)
                marks[("spk", tau, 1)] = K.cnt["kkS"]

            @blk.vector
            def _(v, tau=tau, sc0=sc0, sqv0=sqv0):
                for h in range(GH):
                    K.w(v, "sC", sc0 + h + 1)
                    if h == 0:
                        K.w(v, "kkS", marks.get(("spk", tau - 1, 1), 0))
                    elif h == 16:
                        K.w(v, "kkS", marks[("spk", tau, 0)])
                    i = v.tensor_mul(s["bf2"][:, (h % 16) * NT:
                                              (h % 16 + 1) * NT],
                                     s["trel"][:, :], s["trel"][:, :])
                    assert K.inc(i, "sV") == sqv0 + h + 1

        emit_ldfront(1)
        emit_xn2(1)
        for tau in range(1, NI + 1):
            emit_wfk(tau)
            if tau < NI:
                emit_ldfront(tau + 1)
            emit_sq(tau)
            if tau < NI:
                emit_xn2(tau + 1)

        @blk.sync
        def _(sy):
            for n in ["xnS", "kkS", "a1L"]:
                K.w(sy, n, K.cnt[n])


def emit_pass_b2b(K):
    nc, s, d, pb, fF, wW = K.nc, K.s, K.d, K.pb, K.fF, K.wW
    K.bankmark = {}
    qv = s["qv"]
    with nc.Block("passB2b", no_gpsimd_drain=True) as blk:
        b2a_sc = K.cnt["sC"]
        b2a_sm = K.cnt["sM"]

        @blk.gpsimd
        def _(gp):
            K.inc(gp.dma_start(K.w_out3(wW[0], C, G), K.w_in3("WfrT", C)),
                  "sW", 16)
            K.inc(gp.dma_start(K.w_out3(s["w8stg"], C, G), K.w_in3("w8T", C)),
                  "sW", 16)
            K.inc(gp.dma_start(K.w_out3(s["wbig"], C, GH), K.w_in3("WfvT", C)),
                  "sW", 16)
        wmark = K.cnt["sW"]

        @blk.vector
        def _(v):
            K.w(v, "sW", wmark)
            for j in range(G * C // NT):
                v.tensor_copy(s["tw"][:, :], s["w8stg"][:, j * NT:(j + 1) * NT])
                i = v.tensor_copy(wW[1][:, j * NT:(j + 1) * NT], s["tw"][:, :])
            K.inc(i, "sV")
        w8c = K.cnt["sV"]

        dbg = "dbg" in DEBUG_SPILLS
        marks = {}

        @blk.sync
        def _(sy):
            K.w(sy, "sM", b2a_sm)
            K.inc(sy.dma_start(s["bf1"][:, :], d["xn2sp"][1]), "xnL", 16)
            K.inc(sy.dma_start(s["bf2"][:, 0:8 * NT],
                               d["kksp"][1, 0][:, 0:8 * NT]), "kkL0", 16)
        marks[("ldxn", 1)] = K.cnt["xnL"]
        marks[("ldk", 1, 0)] = K.cnt["kkL0"]

        def emit_wfr(tau):
            sc0 = K.cnt["sC"]
            marks[("sc0", tau)] = sc0
            mf = {}
            marks[("mf", tau)] = mf
            if tau > 1:
                marks[("stcp", tau - 1)] = K.cnt["sV"] + 1

            @blk.tensor
            def _(t, tau=tau, sc0=sc0):
                K.w(t, "sW", wmark)
                K.w(t, "xnL", marks[("ldxn", tau)])
                for m in range(G):
                    if m >= 2:
                        K.w(t, "sC", sc0 + m - 1)
                    elif m == 0:
                        if tau == 1:
                            K.w(t, "sC", b2a_sc)
                        else:
                            K.w(t, "sV", marks[("stcp", tau - 1)])
                    for g in range(G):
                        i = t.matmul(pb[4 + (m % 2)][:, :], K.wsl(wW[0], g, m),
                                     K.fch(s["bf1"], g),
                                     start=(g == 0), stop=(g == G - 1))
                    mf[m] = K.inc(i, "sM")
            marks[("wfr", tau)] = K.cnt["sM"]

        def emit_sgate(tau):
            sc0 = marks[("sc0", tau)]
            mf = marks[("mf", tau)]

            @blk.scalar
            def _(sc, tau=tau, sc0=sc0, mf=mf):
                for m in range(G):
                    K.w(sc, "sM", mf[m])
                    if m == 0:
                        if tau == 1:
                            K.w(sc, "sV", w8c)
                        K.w(sc, "sV", marks.get(("a2", tau - 1), 0))
                        if dbg:
                            K.w(sc, "dbg4", marks.get(("dsp4", tau - 1), 0))
                    sc.activation(K.fch(fF[2], m), pb[4 + (m % 2)][:, :],
                                  Act.Exp, scale=-1.0)
                    sc.activation(K.fch(fF[2], m), K.fch(fF[2], m),
                                  Act.Ln, bias=s["onec"][:, 0:1])
                    i = sc.activation(K.fch(fF[2], m), K.fch(fF[2], m),
                                      Act.Exp, scale=-1.0)
                    assert K.inc(i, "sC") == sc0 + m + 1
            marks[("sgate", tau)] = K.cnt["sC"]

            if tau < NI:
                @blk.sync
                def _(sy, tau=tau):
                    K.w(sy, "sM", marks[("wfr", tau)])
                    K.inc(sy.dma_start(s["bf1"][:, :], d["xn2sp"][tau + 1]),
                          "xnL", 16)
                marks[("ldxn", tau + 1)] = K.cnt["xnL"]

        def emit_z2r0(tau):
            m0 = K.cnt["sM"]
            marks[("z2m0", tau)] = m0

            @blk.sync
            def _(sy, tau=tau, m0=m0):
                for step in range(1, 4):
                    prev = (m0 + step - 1) if step >= 2 else \
                        marks.get(("zmm", tau - 1, 7), 0)
                    K.w(sy, "sM", prev)
                    srcv = d["kksp"][tau, step // 2][:, (step % 2) * 8 * NT:
                                                     ((step % 2) + 1) * 8 * NT]
                    sem = "kkL0" if step % 2 == 0 else "kkL1"
                    K.inc(sy.dma_start(
                        s["bf2"][:, (step % 2) * 8 * NT:
                                 ((step % 2) + 1) * 8 * NT], srcv), sem, 16)
                    marks[("ldk", tau, step)] = K.cnt[sem]

            @blk.tensor
            def _(t, tau=tau, m0=m0):
                K.w(t, "sW", wmark)
                for step in range(4):
                    sem = "kkL0" if step % 2 == 0 else "kkL1"
                    K.w(t, sem, marks[("ldk", tau, step)])
                    if step == 0:
                        for mm_ in range(4):
                            K.bank_wait(t, mm_)
                    for hh in range(8):
                        h = step * 8 + hh
                        rhs = s["bf2"][:, ((step % 2) * 8 + hh) * NT:
                                       ((step % 2) * 8 + hh + 1) * NT]
                        for mm_ in range(4):
                            i = t.matmul(pb[mm_][:, :],
                                         K.wsl_h(s["wbig"], h, C, 4 + mm_),
                                         rhs, start=(h == 0), stop=(h == 31))
                    assert K.inc(i, "sM") == m0 + step + 1

        def emit_tail_a(tau):
            e80 = K.cnt["sV"]
            m8 = {}

            @blk.tensor
            def _(t, tau=tau, e80=e80):
                K.w(t, "sV", max(marks[("qm", tau)], w8c))
                for m in range(G):
                    if m >= 2:
                        K.w(t, "sV", e80 + m - 1)
                    for g in range(G):
                        i = t.matmul(pb[6 + (m % 2)][:, :], K.wsl(wW[1], g, m),
                                     K.fch(qv, g),
                                     start=(g == 0), stop=(g == G - 1))
                    m8[m] = K.inc(i, "sM")
            marks[("w8mm", tau)] = K.cnt["sM"]
            K.bankmark[6] = ("sV", e80 + 7)
            K.bankmark[7] = ("sV", e80 + 8)

            @blk.vector
            def _(v, tau=tau, e80=e80):
                for m in range(G):
                    K.w(v, "sM", m8[m])
                    if m == 0 and dbg:
                        K.w(v, "dbg3", marks.get(("dsp3", tau - 1), 0))
                    v.tensor_mul(K.fch(fF[3], m), pb[6 + (m % 2)][:, :],
                                 s["st3"][:, :])
                    i = v.tensor_scalar_add(K.fch(fF[3], m), K.fch(fF[3], m),
                                            K.vcol(V_FB, m))
                    assert K.inc(i, "sV") == e80 + m + 1
            marks[("f", tau)] = K.cnt["sV"]

        def emit_loads_a(tau):
            @blk.sync
            def _(sy, tau=tau):
                if tau == 1:
                    K.w(sy, "sV", w8c)
                K.w(sy, "sV", marks.get(("qm", tau - 1), 0))
                if dbg:
                    K.w(sy, "dbg0", marks.get(("dsp0", tau - 1), 0))
                    K.w(sy, "dbg2", marks.get(("dsp2", tau - 1), 0))
                K.inc(sy.dma_start(fF[1][:, 4 * NT:8 * NT],
                                   d["a1sp"][tau][:, 4 * NT:8 * NT]),
                      "a1L2", 16)
                K.w(sy, "sM", marks.get(("w8mm", tau - 1), 0))
                K.inc(sy.dma_start(fF[1][:, 0:4 * NT],
                                   d["a1sp"][tau][:, 0:4 * NT]), "a1L", 16)
            marks[("lda1h", tau)] = K.cnt["a1L2"]
            marks[("lda1", tau)] = K.cnt["a1L"]

        def emit_a2r0(tau):
            m0 = marks[("z2m0", tau)]
            ev0 = K.cnt["sV"]

            @blk.vector
            def _(v, tau=tau, m0=m0, ev0=ev0):
                for mm_ in range(4):
                    m = 4 + mm_
                    K.w(v, "sM", m0 + 4)
                    if mm_ == 0:
                        K.w(v, "sC", marks[("sgate", tau)])
                        K.w(v, "a1L2", marks[("lda1h", tau)])
                    v.tensor_mul(s["tt"][:, :], K.fch(fF[2], m), pb[mm_][:, :])
                    i = v.tensor_add(K.fch(fF[1], m), K.fch(fF[1], m),
                                     s["tt"][:, :])
                    assert K.inc(i, "sV") == ev0 + mm_ + 1
            marks[("a2r0", tau)] = K.cnt["sV"]

        def emit_z2r1(tau):
            m1 = K.cnt["sM"]
            marks[("z2m1", tau)] = m1
            ar0 = marks[("a2r0", tau)]
            marks[("ldk", tau, 6)] = K.cnt["kkL0"] + 16
            marks[("ldk", tau, 7)] = K.cnt["kkL1"] + 16

            @blk.tensor
            def _(t, tau=tau, m1=m1, ar0=ar0):
                for step in range(4, 8):
                    half = (2, 3, 0, 1)[step - 4]
                    if step >= 6:
                        sem = "kkL0" if step % 2 == 0 else "kkL1"
                        K.w(t, sem, marks[("ldk", tau, step)])
                    if step == 4:
                        for mm_ in range(4):
                            K.w(t, "sV", ar0 - 3 + mm_)
                    for hh in range(8):
                        h = half * 8 + hh
                        rhs = s["bf2"][:, ((step % 2) * 8 + hh) * NT:
                                       ((step % 2) * 8 + hh + 1) * NT]
                        for mm_ in range(4):
                            i = t.matmul(pb[mm_][:, :],
                                         K.wsl_h(s["wbig"], h, C, mm_),
                                         rhs, start=(step == 4 and hh == 0),
                                         stop=(step == 7 and hh == 7))
                    assert K.inc(i, "sM") == m1 + step - 3
            marks[("zmm", tau, 6)] = m1 + 3
            marks[("zmm", tau, 7)] = m1 + 4

        def emit_loads_b(tau):
            m1 = marks[("z2m1", tau)]

            @blk.sync
            def _(sy, tau=tau, m1=m1):
                for step in (6, 7):
                    K.w(sy, "sM", m1 + step - 5)
                    srcv = d["kksp"][tau, 0][
                        :, (step % 2) * 8 * NT:((step % 2) + 1) * 8 * NT]
                    sem = "kkL0" if step % 2 == 0 else "kkL1"
                    K.inc(sy.dma_start(
                        s["bf2"][:, (step % 2) * 8 * NT:
                                 ((step % 2) + 1) * 8 * NT], srcv), sem, 16)
                    assert marks[("ldk", tau, step)] == K.cnt[sem]
                if tau < NI:
                    K.w(sy, "sM", m1 + 3)
                    K.inc(sy.dma_start(
                        s["bf2"][:, 0:8 * NT],
                        d["kksp"][tau + 1, 0][:, 0:8 * NT]), "kkL0", 16)
                    marks[("ldk", tau + 1, 0)] = K.cnt["kkL0"]

        def emit_tail_b(tau):
            if dbg:
                @blk.sync
                def _(sy, tau=tau):
                    K.w(sy, "dbg3", marks.get(("dsp3", tau - 1), 0))
                    K.w(sy, "sV", marks[("f", tau)])
                    K.inc(sy.dma_start(d["dbg"][tau, 3], fF[3][:, :]),
                          "dbg3", 16)
                marks[("dsp3", tau)] = K.cnt["dbg3"]

            cmark = emit_stats(K, blk, fF[3], [("st2", "r")],
                               pre=[("sV", marks[("f", tau)])], banks=(6, 7))

            @blk.vector
            def _(v, tau=tau, cmark=cmark):
                K.w(v, "sC", cmark)
                if dbg:
                    K.w(v, "dbg1", marks.get(("dsp1", tau), 0))
                for g in range(G):
                    v.tensor_sub(s["tdx"][:, :], K.fch(fF[3], g), s["st0"][:, :])
                    v.tensor_mul(s["tt"][:, :], s["tdx"][:, :], s["st2"][:, :])
                    i = v.tensor_add(K.fch(fF[0], g), K.fch(fF[0], g),
                                     s["tt"][:, :])
                K.inc(i, "sV")
            marks[("out", tau)] = K.cnt["sV"]

        def emit_spill_out(tau):
            @blk.sync
            def _(sy, tau=tau):
                K.w(sy, "outS", marks.get(("spo", tau - 1), 0))
                K.w(sy, "sV", marks[("out", tau)])
                K.inc(sy.dma_start(d["out"][tau - 1], fF[0][:, :]), "outS", 16)
            marks[("spo", tau)] = K.cnt["outS"]

            @blk.sync
            def _(sy, tau=tau):
                K.w(sy, "outS", marks[("spo", tau)])
                K.inc(sy.dma_start(K.f_str(fF[0], 0, NT), K.xT3(tau + 1)),
                      "xL", 16)
            marks[("ldx2", tau + 1)] = K.cnt["xL"]

        def emit_mid_a(tau):
            m1 = marks[("z2m1", tau)]

            @blk.vector
            def _(v, tau=tau, m1=m1):
                for mm_ in range(4):
                    K.w(v, "sM", m1 + 4)
                    if mm_ == 0:
                        K.w(v, "a1L", marks[("lda1", tau)])
                    v.tensor_mul(s["tt"][:, :], K.fch(fF[2], mm_),
                                 pb[mm_][:, :])
                    i = v.tensor_add(K.fch(fF[1], mm_), K.fch(fF[1], mm_),
                                     s["tt"][:, :])
                    K.inc(i, "sV")
                    K.bank_set(mm_, "sV")
            marks[("a2", tau)] = K.cnt["sV"]
            if dbg:
                @blk.sync
                def _(sy, tau=tau):
                    K.w(sy, "dbg0", marks.get(("dsp0", tau - 1), 0))
                    K.w(sy, "sV", marks[("a2", tau)])
                    K.inc(sy.dma_start(d["dbg"][tau, 0], fF[1][:, :]),
                          "dbg0", 16)
                    K.w(sy, "dbg4", marks.get(("dsp4", tau - 1), 0))
                    K.w(sy, "sC", marks[("sgate", tau)])
                    K.inc(sy.dma_start(d["dbg"][tau, 4], fF[2][:, :]),
                          "dbg4", 16)
                marks[("dsp0", tau)] = K.cnt["dbg0"]
                marks[("dsp4", tau)] = K.cnt["dbg4"]

            cmark = emit_stats(K, blk, fF[1], [("st2", "r")],
                               pre=[("sV", marks[("a2", tau)])], banks=(6, 7))

            @blk.vector
            def _(v, tau=tau, cmark=cmark):
                K.w(v, "sC", cmark)
                K.w(v, "xL", marks[("ldx2", tau)])
                for g in range(G):
                    v.tensor_sub(s["tdx"][:, :], K.fch(fF[1], g), s["st0"][:, :])
                    v.tensor_mul(s["tt"][:, :], s["tdx"][:, :], s["st2"][:, :])
                    i = v.tensor_add(K.fch(fF[0], g), K.fch(fF[0], g),
                                     s["tt"][:, :])
                K.inc(i, "sV")
            marks[("x1", tau)] = K.cnt["sV"]
            if dbg:
                @blk.sync
                def _(sy, tau=tau):
                    K.w(sy, "dbg1", marks.get(("dsp1", tau - 1), 0))
                    K.w(sy, "sV", marks[("x1", tau)])
                    K.inc(sy.dma_start(d["dbg"][tau, 1], fF[0][:, :]),
                          "dbg1", 16)
                marks[("dsp1", tau)] = K.cnt["dbg1"]

        def emit_mid_b(tau):
            cmark = emit_stats(K, blk, fF[0], [("st2", "tau")],
                               pre=[("sV", marks[("x1", tau)])], banks=(6, 7))

            @blk.vector
            def _(v, tau=tau, cmark=cmark):
                K.w(v, "sC", cmark)
                if dbg:
                    K.w(v, "dbg0", marks.get(("dsp0", tau), 0))
                for g in range(G):
                    v.tensor_sub(s["tdx"][:, :], K.fch(fF[0], g), s["st0"][:, :])
                    i = v.tensor_mul(K.fch(fF[1], g), s["tdx"][:, :],
                                     s["st2"][:, :])
                K.inc(i, "sV")
                v.scalar_tensor_tensor(s["tt"][:, :], K.fch(fF[1], 0), -1.0,
                                       K.fch(fF[1], 0), Alu.mult, Alu.max)
                for g in range(1, G):
                    v.scalar_tensor_tensor(s["tt2"][:, :], K.fch(fF[1], g),
                                           -1.0, K.fch(fF[1], g),
                                           Alu.mult, Alu.max)
                    v.tensor_max(s["tt"][:, :], s["tt"][:, :], s["tt2"][:, :])
                v.tensor_copy(s["txb"][:, 0:NT], s["tt"][:, :])
                v.tensor_sub(s["tt2"][:, :], s["tt"][:, :], s["txb"][:, 0:NT])
                i = v.tensor_copy(s["txq"][:, 0:NT], s["tt2"][:, :])
                K.inc(i, "sV")
            marks[("rm", tau)] = K.cnt["sV"]
            mxs = K.cnt["sV"]
            if dbg:
                @blk.sync
                def _(sy, tau=tau):
                    K.w(sy, "dbg2", marks.get(("dsp2", tau - 1), 0))
                    K.w(sy, "sV", marks[("rm", tau)])
                    K.inc(sy.dma_start(d["dbg"][tau, 2], fF[1][:, :]),
                          "dbg2", 16)
                marks[("dsp2", tau)] = K.cnt["dbg2"]

            @blk.tensor
            def _(t, mxs=mxs):
                K.w(t, "sV", mxs)
                K.bank_wait(t, 4)
                for j in range(NQ):
                    t.matmul(pb[4][0:QW, j * 128:(j + 1) * 128],
                             s["txb"][:, j * QW:(j + 1) * QW],
                             s["identb"][:, :], start=True, stop=False)
                    i = t.matmul(pb[4][0:QW, j * 128:(j + 1) * 128],
                                 s["txq"][:, j * QW:(j + 1) * QW],
                                 s["identb"][:, :], start=False, stop=True)
                K.inc(i, "sM")
            trm = K.cnt["sM"]

            @blk.vector
            def _(v, trm=trm):
                K.w(v, "sM", trm)
                for j in range(NQ):
                    i = v.tensor_reduce(s["sm"][0:QW, j:j + 1],
                                        pb[4][0:QW, j * 128:(j + 1) * 128],
                                        AX, Alu.max)
                K.inc(i, "sV")

            @blk.vector
            def _(v):
                v.drain()
                v.tensor_scalar_max(s["sm"][:, 4:4 + NQ], s["sm"][:, 0:NQ],
                                    EPS_ACT)
                v.drain()
                v.reciprocal(s["sm"][:, 8:8 + NQ], s["sm"][:, 4:4 + NQ])
                v.drain()
                v.tensor_scalar_mul(s["sm"][:, 8:8 + NQ], s["sm"][:, 8:8 + NQ],
                                    127.0)
                v.tensor_scalar_mul(s["sm"][:, 12:12 + NQ], s["sm"][:, 4:4 + NQ],
                                    s["wsbc"][:, 0:1])
                v.drain()
                v.tensor_scalar_mul(s["sm"][:, 12:12 + NQ],
                                    s["sm"][:, 12:12 + NQ], 1.0 / 127.0)
                v.drain()
                v.tensor_copy(K.col_str(s["hl8"], 0, 2, NQ), s["sm"][:, 8:8 + NQ])
                v.drain()
                v.tensor_sub(s["sm"][:, 16:16 + NQ], s["sm"][:, 8:8 + NQ],
                             K.col_str(s["hl8"], 0, 2, NQ))
                v.drain()
                v.tensor_copy(K.col_str(s["hl8"], 1, 2, NQ),
                              s["sm"][:, 16:16 + NQ])
                v.tensor_copy(K.col_str(s["hl8"], 2 * NQ, 2, NQ),
                              s["sm"][:, 12:12 + NQ])
                v.drain()
                v.tensor_sub(s["sm"][:, 16:16 + NQ], s["sm"][:, 12:12 + NQ],
                             K.col_str(s["hl8"], 2 * NQ, 2, NQ))
                v.drain()
                v.tensor_copy(K.col_str(s["hl8"], 2 * NQ + 1, 2, NQ),
                              s["sm"][:, 16:16 + NQ])
                v.drain()
                K.inc(v.engine_nop(), "sV")
            hlm = K.cnt["sV"]

            @blk.tensor
            def _(t, hlm=hlm):
                K.w(t, "sV", hlm)
                K.bank_wait(t, 5)
                t.matmul(pb[4][0:2 * NQ, 0:128], s["hl8"][:, 0:2 * NQ],
                         s["identb"][:, :], start=True, stop=True)
                i = t.matmul(pb[5][0:2 * NQ, 0:128],
                             s["hl8"][:, 2 * NQ:4 * NQ],
                             s["identb"][:, :], start=True, stop=True)
                K.inc(i, "sM")
            t2m = K.cnt["sM"]

            @blk.vector
            def _(v, t2m=t2m):
                K.w(v, "sM", t2m)
                v.tensor_copy(s["trows"][0:2 * NQ, 0:128],
                              pb[4][0:2 * NQ, 0:128])
                i = v.tensor_copy(s["trows"][0:2 * NQ, 128:256],
                                  pb[5][0:2 * NQ, 0:128])
                K.inc(i, "sV")
            trowm = K.cnt["sV"]

            @blk.tensor
            def _(t, trowm=trowm):
                K.w(t, "sV", trowm)
                for j in range(NQ):
                    t.matmul(pb[4][:, j * QW:(j + 1) * QW],
                             s["sel4"][0:2 * NQ, j * QW:(j + 1) * QW],
                             s["trows"][0:2 * NQ, 0:QW], start=True, stop=True)
                    i = t.matmul(pb[5][:, j * QW:(j + 1) * QW],
                                 s["sel4"][0:2 * NQ, j * QW:(j + 1) * QW],
                                 s["trows"][0:2 * NQ, 128:128 + QW],
                                 start=True, stop=True)
                K.inc(i, "sM")
            marks[("bcm", tau)] = K.cnt["sM"]

        def emit_quant(tau):
            bcm = marks[("bcm", tau)]

            @blk.vector
            def _(v, tau=tau, bcm=bcm):
                K.w(v, "sM", bcm)
                if dbg:
                    K.w(v, "dbg2", marks.get(("dsp2", tau), 0))
                v.tensor_copy(s["st1"][:, :], pb[4][:, :])
                i = v.tensor_copy(s["st3"][:, :], pb[5][:, :])
                K.inc(i, "sV")
                for g in range(G):
                    v.tensor_mul(s["tt2"][:, :], K.fch(fF[1], g), s["st1"][:, :])
                    v.tensor_scalar_add(s["tt2"][:, :], s["tt2"][:, :], 128.5)
                    v.tensor_copy(s["ti"][:, :], s["tt2"][:, :])
                    v.scalar_tensor_tensor(s["tt"][:, :], s["tt2"][:, :], 0.5,
                                           s["ti"][:, :], Alu.subtract,
                                           Alu.subtract)
                    v.tensor_copy(s["ti2"][:, :], s["tt"][:, :])
                    i = v.scalar_tensor_tensor(K.fch(qv, g), s["ti"][:, :],
                                               -128.0, s["ti2"][:, :],
                                               Alu.add, Alu.add)
                K.inc(i, "sV")
            marks[("stc", tau)] = K.cnt["sV"] - 1
            marks[("qm", tau)] = K.cnt["sV"]
            if tau < NI:
                assert marks[("stcp", tau)] == marks[("stc", tau)]

        @blk.sync
        def _(sy):
            K.w(sy, "sV", w8c)
            K.inc(sy.dma_start(K.f_str(fF[0], 0, NT), K.xT3(1)), "xL", 16)
        marks[("ldx2", 1)] = K.cnt["xL"]

        emit_wfr(1)
        emit_sgate(1)
        emit_z2r0(1)
        emit_loads_a(1)
        emit_a2r0(1)
        emit_z2r1(1)
        emit_loads_b(1)

        for tau in range(1, NI):
            emit_mid_a(tau)
            emit_z2r0(tau + 1)
            emit_mid_b(tau)
            emit_wfr(tau + 1)
            emit_quant(tau)
            emit_sgate(tau + 1)
            emit_tail_a(tau)
            emit_loads_a(tau + 1)
            emit_a2r0(tau + 1)
            emit_tail_b(tau)
            emit_z2r1(tau + 1)
            emit_loads_b(tau + 1)
            emit_spill_out(tau)
        emit_mid_a(NI)
        emit_mid_b(NI)
        emit_quant(NI)
        emit_tail_a(NI)
        emit_tail_b(NI)
        emit_spill_out(NI)

        @blk.sync
        def _(sy):
            for n in ["outS", "xL", "a1L", "a1L2", "xnL", "kkL0", "kkL1",
                      "dbg0", "dbg1", "dbg2", "dbg3", "dbg4"]:
                K.w(sy, n, K.cnt[n])


def build_K():
    K = Builder()
    emit_pass_a(K)
    emit_pass_b1(K)
    emit_pass_b2a(K)
    emit_pass_b2b(K)
    return K


def build():
    return build_K().nc


def stage_inputs(inputs):
    x = np.asarray(inputs["L0_attn_x"])
    fm = lambda v: np.ascontiguousarray(np.asarray(v).reshape(G, 128).T)
    ct = lambda w: np.ascontiguousarray(np.asarray(w).T)
    common = {
        "WkT": ct(inputs["L0_attn_Wk"]), "WvT": ct(inputs["L0_attn_Wv"]),
        "WrT": ct(inputs["L0_attn_Wr"]), "WoT": ct(inputs["L0_attn_Wo"]),
        "WfrT": ct(inputs["L0_attn_Wfr"]), "WfkT": ct(inputs["L0_attn_Wfk"]),
        "WfvT": ct(inputs["L0_attn_Wfv"]), "w8T": ct(inputs["L0_ffn_w_int8"]),
        "ln1w": fm(inputs["L0_attn_ln1_w"]), "ln1b": fm(inputs["L0_attn_ln1_b"]),
        "ln2w": fm(inputs["L0_attn_ln2_w"]), "ln2b": fm(inputs["L0_attn_ln2_b"]),
        "decay": fm(inputs["L0_attn_decay"]), "uu": fm(inputs["L0_attn_u"]),
        "fbias": fm(inputs["L0_ffn_bias"]),
        "wst": np.asarray(inputs["L0_ffn_weight_scale"],
                          np.float32).reshape(1, 1),
        "onesb": np.ones((128, 128), ml_dtypes.bfloat16),
        "identb": np.eye(128, dtype=ml_dtypes.bfloat16),
        "onesf": np.ones((1, 128), np.float32),
        "sel4": np.ascontiguousarray(
            np.repeat(np.eye(NQ, dtype=np.float32), 2, axis=0)
            .repeat(QW, axis=1)).astype(ml_dtypes.bfloat16),
    }
    in_maps = []
    for core in range(8):
        b, q = core // 4, core % 4
        lo = q * NCORE - NT
        xT = np.zeros((C, NW), np.float32)
        a, bb = max(lo, 0), min(lo + NW, T)
        xT[:, a - lo: bb - lo] = x[b, a:bb, :].T
        mask = np.ones((128, 2 * NT), np.float32)
        if q == 0:
            mask[:, 0:NT] = 0.0
        if q == 3:
            mask[:, NT:2 * NT] = 0.0
        in_maps.append({**common,
                        "xT": np.ascontiguousarray(xT),
                        "mask2": mask.astype(ml_dtypes.bfloat16)})
    return in_maps


def assemble(results):
    out = np.empty((B, T, C), np.float32)
    for core, r in enumerate(results):
        b, q = core // 4, core % 4
        o = np.asarray(r["out"])
        blk = o.reshape(NI, 128, G, NT).transpose(0, 3, 2, 1).reshape(NCORE, C)
        out[b, q * NCORE:(q + 1) * NCORE, :] = blk
    return out


_NC_CACHE = [None]


def kernel(**inputs):
    in_maps = stage_inputs(inputs)
    if _NC_CACHE[0] is None:
        _NC_CACHE[0] = build()
    res = run_bass_kernel_spmd(_NC_CACHE[0], in_maps, core_ids=list(range(8)))
    return assemble(res.results)


# version 22
# speedup vs baseline: 1.0479x; 1.0049x over previous
import numpy as np
import ml_dtypes
import concourse.bass as bass
import concourse.mybir as mybir
from concourse.bass_utils import run_bass_kernel_spmd
from concourse.ap import AP

F32 = mybir.dt.float32
BF16 = mybir.dt.bfloat16
I32 = mybir.dt.int32
Alu = mybir.AluOpType
Act = mybir.ActivationFunctionType
AX = mybir.AxisListType.X

DEBUG_SPILLS = ()

B, T, C, G, H, GH = 2, 16384, 1024, 8, 4096, 32
NT = 512
NTILES = 10
NI = NTILES - 2
NW = NT * NTILES
NCORE = NT * NI
SC = NT + 1
NQ = max(NT // 128, 1)
QW = min(NT, 128)
EPS = 1e-5
EPS_RMS = 1e-6
EPS_ACT = 1e-5

V_EW, V_EU, V_U, V_L1W, V_L1B, V_L2W, V_L2B, V_FB = range(8)


class Builder:
    def __init__(self):
        nc = bass.Bass("TRN2", target_bir_lowering=False, debug=False,
                       detect_race_conditions=False)
        self.nc = nc

        d = self.d = {}
        for n, shp, dt in [
            ("xT", [C, NW], F32), ("mask2", [128, 2 * NT], BF16),
            ("WkT", [C, C], F32), ("WvT", [C, C], F32), ("WrT", [C, C], F32),
            ("WoT", [C, C], F32), ("WfrT", [C, C], F32),
            ("WfkT", [C, H], F32), ("WfvT", [H, C], F32), ("w8T", [C, C], I32),
            ("ln1w", [128, G], F32), ("ln1b", [128, G], F32),
            ("ln2w", [128, G], F32), ("ln2b", [128, G], F32),
            ("decay", [128, G], F32), ("uu", [128, G], F32),
            ("fbias", [128, G], F32), ("wst", [1, 1], F32),
            ("onesb", [128, 128], BF16), ("identb", [128, 128], BF16),
            ("onesf", [1, 128], F32),
            ("sel4", [2 * NQ, NQ * QW], BF16),
        ]:
            d[n] = nc.dram_tensor(n, shp, dt, kind="ExternalInput").ap()

        d["out"] = nc.dram_tensor("out", [NI, 128, G * NT], F32,
                                  kind="ExternalOutput").ap()
        for n, shp, dt in [
            ("E1sp", [NTILES, 128, G * NT], F32),
            ("gsp", [NTILES, 128, G * NT], F32),
            ("zsp", [NTILES, 128, G * NT], F32),
            ("outfsp", [NTILES, 128, G * NT], F32),
            ("rsp", [NTILES, 128, G * NT], F32),
            ("a1sp", [NTILES, 128, G * NT], F32),
            ("xn2sp", [NTILES, 128, G * NT], BF16),
            ("kksp", [NTILES, 2, 128, 16 * NT], BF16),
            ("dbg", [NTILES, 6, 128, G * NT], F32),
        ]:
            kind = "ExternalOutput" if n in DEBUG_SPILLS else "Internal"
            d[n] = nc.dram_tensor(n, shp, dt, kind=kind).ap()

        s = self.s = {}
        avail = nc.sbuf_bytes_remaining
        arena = nc.alloc_sbuf_tensor("arena", [128, avail - 64],
                                     mybir.dt.uint8)
        base = nc.lookup_mloc(arena).addr
        self._off = base
        A = self._place

        s["ones"] = self.at("ones", [128, 128], BF16, A(256))
        s["identb"] = self.at("identb_s", [128, 128], BF16, A(256))
        s["sel4"] = self.at("sel4_s", [128, NQ * QW], BF16, A(1024))
        s["mask"] = self.at("mask_s", [128, 2 * NT], BF16, A(2 * NT * 2))
        s["c"] = self.at("c_s", [128, G * NTILES], F32, A(G * NTILES * 4))
        s["vec"] = self.at("vec_s", [128, 8 * G], F32, A(8 * G * 4))
        s["wsbc"] = self.at("wsbc_s", [128, 1], F32, A(32))
        s["onec"] = self.at("onec_s", [128, 1], F32, A(32))
        s["ln127"] = self.at("ln127_s", [128, 1], F32, A(32))
        s["onesf"] = self.at("onesf_s", [128, 128], F32, A(512))
        s["wst"] = self.at("wst_s", [128, 1], F32, A(32))
        F4 = NT * 4
        toff = A(5 * F4 + 992)
        s["tdx"] = self.at("tdx", [128, NT], F32, toff)
        s["ti"] = self.at("ti", [128, NT], I32, toff)
        s["tv"] = self.at("tv", [128, NT], F32, toff + F4)
        s["trel"] = self.at("trel", [128, NT], BF16, toff + F4)
        s["tt2"] = self.at("tt2", [128, NT], F32, toff + F4)
        s["ti2"] = self.at("ti2", [128, NT], I32, toff + F4)
        s["tt"] = self.at("tt", [128, NT], F32, toff + 2 * F4)
        s["tw"] = self.at("tw", [128, NT], F32, toff + 2 * F4)
        s["txb"] = self.at("txb", [128, 2 * NT], BF16, toff + 3 * F4)
        s["txq"] = self.at("txq", [128, 2 * NT], BF16, toff + 4 * F4)
        s["sm"] = self.at("sm", [128, 24], F32, toff + 5 * F4)
        s["hl8"] = self.at("hl8", [128, 4 * NQ], BF16, toff + 5 * F4 + 128)
        s["trows"] = self.at("trows", [128, 256], BF16, toff + 5 * F4 + 256)
        s["st0"] = self.at("st0", [128, NT], F32, A(NT * 4))
        s["st1"] = self.at("st1", [128, NT], F32, A(NT * 4))
        s["st2"] = self.at("st2", [128, NT], F32, A(NT * 4))
        s["st3"] = self.at("st3", [128, NT], F32, A(NT * 4))
        soff = A(max(4 * G * SC * 4, G * H * 2))
        self.sS = [self.at(f"s{i}", [128, G * SC], F32, soff + i * G * SC * 4)
                   for i in range(4)]
        s["wbig"] = self.at("wbig", [128, G * H], BF16, soff)
        foff = A(max(4 * G * NT * 4, G * C * 4))
        self.fF = [self.at(f"f{i}", [128, G * NT], F32, foff + i * G * NT * 4)
                   for i in range(4)]
        s["w8stg"] = self.at("w8stg", [128, G * C], I32, foff)
        s["qv"] = self.at("qv", [128, G * NT], BF16, foff + 1 * G * NT * 4)
        s["bf1"] = self.at("bf1", [128, G * NT], BF16, A(G * NT * 2))
        woff = A(3 * G * C * 2)
        self.wW = [self.at(f"w{i}", [128, G * C], BF16, woff + i * G * C * 2)
                   for i in range(3)]
        s["bf2"] = self.at("bf2", [128, 16 * NT], BF16, woff + 2 * G * C * 2)
        used = self._off - base
        assert used <= avail - 64, f"sbuf over budget: {used} > {avail - 64}"

        self.pb = [nc.alloc_psum_tensor(f"pb{i}", [128, NT], F32)
                   for i in range(8)]

        self.sems = {n: nc.alloc_semaphore(n) for n in
                     ["vecL", "xL", "e1L", "gL", "zL", "ofL", "rL", "xnL",
                      "kkL0", "kkL1", "a1L", "a1L2",
                      "zS", "e1S", "gS", "ofS", "rS", "a1S", "xnS", "kkS",
                      "outS", "sM", "sV", "sC", "sG", "sW",
                      "dbg0", "dbg1", "dbg2", "dbg3", "dbg4"]}
        self.cnt = {n: 0 for n in self.sems}
        self.bankmark = {}

    def _place(self, nbytes):
        a = self._off
        self._off = a + ((nbytes + 31) // 32) * 32
        return a

    def at(self, name, shape, dt, offset):
        return self.nc.alloc_sbuf_tensor_at(name, shape, dt, offset=offset)

    def inc(self, instr, sem, n=1):
        instr.then_inc(self.sems[sem], n)
        self.cnt[sem] += n
        return self.cnt[sem]

    def w(self, eng, sem, val):
        if val and val > 0:
            eng.wait_ge(self.sems[sem], val)

    def bank_wait(self, eng, i):
        m = self.bankmark.get(i)
        if m:
            self.w(eng, m[0], m[1])

    def bank_set(self, i, sem):
        self.bankmark[i] = (sem, self.cnt[sem])

    def fch(self, buf, g):
        return buf[:, g * NT:(g + 1) * NT]

    def sdat(self, i, g, a, b):
        return self.sS[i][:, g * SC + a: g * SC + b]

    def s_rev(self, i, g, a, n):
        t = self.sS[i].ap()
        return AP(t.tensor, t.offset + g * SC + a, [[t.ap[0][0], 128], [-1, n]])

    def s_str(self, i, a, b):
        t = self.sS[i].ap()
        return AP(t.tensor, t.offset + a,
                  [[t.ap[0][0], 128], [SC, G], [1, b - a]])

    def f_str(self, buf, a, b):
        t = buf.ap()
        return AP(t.tensor, t.offset + a,
                  [[t.ap[0][0], 128], [NT, G], [1, b - a]])

    def col_str(self, buf, col, pitch, n):
        t = buf.ap()
        return AP(t.tensor, t.offset + col, [[t.ap[0][0], 128], [pitch, n]])

    def vcol(self, i, g=None, n=1):
        if g is None:
            return self.s["vec"][:, i * G:(i + 1) * G]
        return self.s["vec"][:, i * G + g: i * G + g + n]

    def ccol(self, tau, g=None):
        if g is None:
            return self.s["c"][:, tau * G: (tau + 1) * G]
        return self.s["c"][:, tau * G + g: tau * G + g + 1]

    def ew_b(self, g, n):
        return self.vcol(V_EW, g).broadcast_to([128, n])

    def xT3(self, tau):
        t = self.d["xT"]
        return AP(t.tensor, t.offset + tau * NT,
                  [[NW, 128], [128 * NW, G], [1, NT]])

    def w_in3(self, name, cols):
        t = self.d[name]
        nchunks = t.shape[0] // 128
        return AP(t.tensor, t.offset,
                  [[cols, 128], [128 * cols, nchunks], [1, cols]])

    def w_out3(self, slot, cols, nchunks):
        t = slot.ap()
        return AP(t.tensor, t.offset,
                  [[t.ap[0][0], 128], [cols, nchunks], [1, cols]])

    def wsl(self, slot, g, m):
        return slot[:, g * C + m * 128: g * C + (m + 1) * 128]

    def wsl_h(self, slot, g, cols, m):
        return slot[:, g * cols + m * 128: g * cols + (m + 1) * 128]


def emit_stats_front(K, blk, src_full, pre=(), banks=(0, 1), xsq_gp=False):
    s, pb = K.s, K.pb
    b0, b1 = banks
    m0 = K.cnt["sM"]
    v0 = K.cnt["sV"]
    g0 = K.cnt["sG"]

    if xsq_gp:
        @blk.gpsimd
        def _(gp):
            for g in range(G):
                if g == 0:
                    for sem, val in pre:
                        K.w(gp, sem, val)
                if g >= 2:
                    K.w(gp, "sM", m0 + g - 1)
                i = gp.tensor_mul(s["txq"][:, (g % 2) * NT:(g % 2 + 1) * NT],
                                  K.fch(src_full, g), K.fch(src_full, g))
                K.inc(i, "sG")
        assert K.cnt["sG"] == g0 + G
    else:
        @blk.vector
        def _(v):
            for g in range(G):
                if g == 0:
                    for sem, val in pre:
                        K.w(v, sem, val)
                if g >= 2:
                    K.w(v, "sM", m0 + g - 1)
                i = v.tensor_mul(s["txq"][:, (g % 2) * NT:(g % 2 + 1) * NT],
                                 K.fch(src_full, g), K.fch(src_full, g))
                K.inc(i, "sV")
        assert K.cnt["sV"] == v0 + G

    @blk.tensor
    def _(t, xsq_gp=xsq_gp):
        for g in range(G):
            if g == 0:
                for sem, val in pre:
                    K.w(t, sem, val)
                K.bank_wait(t, b0)
                K.bank_wait(t, b1)
            t.matmul(pb[b0][:, :], s["onesf"][:, :], K.fch(src_full, g),
                     start=(g == 0), stop=(g == G - 1))
            if xsq_gp:
                K.w(t, "sG", g0 + g + 1)
            else:
                K.w(t, "sV", v0 + g + 1)
            i = t.matmul(pb[b1][:, :], s["ones"][:, :],
                         s["txq"][:, (g % 2) * NT:(g % 2 + 1) * NT],
                         start=(g == 0), stop=(g == G - 1))
            K.inc(i, "sM")
    assert K.cnt["sM"] == m0 + G
    return m0


def emit_stats_back(K, blk, coefs, banks=(0, 1), m0=None):
    s, pb = K.s, K.pb
    b0, b1 = banks

    @blk.vector
    def _(v):
        K.w(v, "sM", m0 + G)
        v.tensor_scalar_mul(s["st0"][:, :], pb[b0][:, :], 1.0 / C)
        i = v.tensor_scalar_mul(s["st1"][:, :], pb[b1][:, :], 1.0 / C)
        K.inc(i, "sV")
        v.tensor_mul(s["st3"][:, :], s["st0"][:, :], s["st0"][:, :])
        v.tensor_sub(s["st1"][:, :], s["st1"][:, :], s["st3"][:, :])
        for dst, kind in coefs:
            if kind == "r":
                i = v.tensor_scalar_add(s[dst][:, :], s["st1"][:, :], EPS)
            elif kind == "rho":
                i = v.tensor_scalar(s[dst][:, :], s["st1"][:, :],
                                    1.0 + EPS, EPS * EPS, Alu.mult, Alu.add)
            elif kind == "tau":
                i = v.tensor_scalar(s[dst][:, :], s["st1"][:, :],
                                    1.0 + EPS_RMS, EPS * EPS_RMS,
                                    Alu.mult, Alu.add)
        K.inc(i, "sV")
    K.bank_set(b0, "sV")
    K.bank_set(b1, "sV")
    vmark = K.cnt["sV"]

    @blk.scalar
    def _(sc):
        K.w(sc, "sV", vmark)
        for dst, kind in coefs:
            sc.activation(s[dst][:, :], s[dst][:, :], Act.Ln)
            i = sc.activation(s[dst][:, :], s[dst][:, :], Act.Exp, scale=-0.5)
        K.inc(i, "sC")
    return K.cnt["sC"]


def emit_stats(K, blk, src_full, coefs, pre=(), banks=(0, 1)):
    m0 = emit_stats_front(K, blk, src_full, pre=pre, banks=banks)
    return emit_stats_back(K, blk, coefs, banks=banks, m0=m0)


def emit_pass_a(K):
    nc, s, d, pb, fF, sS, wW = K.nc, K.s, K.d, K.pb, K.fF, K.sS, K.wW
    with nc.Block("passA", no_gpsimd_drain=True) as blk:
        @blk.gpsimd
        def _(gp):
            K.inc(gp.dma_start(K.w_out3(wW[0], C, G), K.w_in3("WkT", C)),
                  "sW", 16)
            K.inc(gp.dma_start(K.w_out3(wW[1], C, G), K.w_in3("WvT", C)),
                  "sW", 16)
            K.inc(gp.dma_start(K.w_out3(wW[2], C, G), K.w_in3("WrT", C)),
                  "sW", 16)
        wmark = K.cnt["sW"]

        @blk.sync
        def _(sy):
            for dst, src in [
                (s["ones"][:, :], d["onesb"]), (s["identb"][:, :], d["identb"]),
                (s["sel4"][0:2 * NQ, :], d["sel4"]),
                (s["mask"][:, :], d["mask2"]),
                (K.vcol(V_L1W), d["ln1w"]), (K.vcol(V_L1B), d["ln1b"]),
                (K.vcol(V_L2W), d["ln2w"]), (K.vcol(V_L2B), d["ln2b"]),
                (K.vcol(V_U), d["uu"]), (K.vcol(V_EW), d["decay"]),
                (K.vcol(V_FB), d["fbias"]),
                (s["onesf"][0:1, :], d["onesf"]), (s["wst"][0:1, :], d["wst"]),
            ]:
                K.inc(sy.dma_start(dst, src), "vecL", 16)
        ld0 = K.cnt["vecL"]

        @blk.scalar
        def _(sc):
            K.w(sc, "vecL", ld0)
            sc.activation(K.vcol(V_EW), K.vcol(V_EW), Act.Exp)
            sc.drain()
            sc.activation(K.vcol(V_EW), K.vcol(V_EW), Act.Exp, scale=-1.0)
            sc.drain()
            sc.activation(K.vcol(V_EU), K.vcol(V_U), Act.Exp)
            sc.drain()
            K.inc(sc.nop(), "sC")
        prep_c = K.cnt["sC"]

        @blk.tensor
        def _(t):
            K.w(t, "vecL", ld0)
            i = t.matmul(pb[7][:, 0:1], s["onesf"][0:1, :], s["wst"][0:1, 0:1],
                         start=True, stop=True)
            K.inc(i, "sM")
        wsm = K.cnt["sM"]

        @blk.vector
        def _(v):
            K.w(v, "sM", wsm)
            v.tensor_copy(s["wsbc"][:, :], pb[7][:, 0:1])
            v.memset(s["onec"][:, :], 1.0)
            v.memset(s["onesf"][:, :], 1.0)
            v.memset(s["ln127"][:, :], 4.844187086458591)
            v.memset(K.col_str(sS[0], 0, SC, G), 0.0)
            i = v.memset(K.col_str(sS[1], 0, SC, G), 0.0)
            K.inc(i, "sV")
        K.bank_set(7, "sV")
        prep_v = K.cnt["sV"]

        marks = {}
        statsmm = {}

        def emit_xstats(tau):
            @blk.sync
            def _(sy, tau=tau):
                K.w(sy, "sV", marks.get(("xfree", tau - 1), 0))
                K.inc(sy.dma_start(K.f_str(fF[0], 0, NT), K.xT3(tau)), "xL", 16)
            marks[("ldx", tau)] = K.cnt["xL"]

            pre = [("xL", marks[("ldx", tau)]), ("sC", prep_c),
                   ("sV", prep_v), ("sM", statsmm.get(tau - 1, 0))]
            marks[("cm", tau)] = emit_stats(K, blk, fF[0],
                                            [("st2", "r"), ("st3", "rho")],
                                            pre=pre)
            statsmm[tau] = K.cnt["sM"]

        def emit_dxz(tau):
            interior = 1 <= tau <= NI
            cmark = marks[("cm", tau)]

            @blk.vector
            def _(v, tau=tau, cmark=cmark, interior=interior):
                K.w(v, "sC", cmark)
                K.w(v, "zS", marks.get(("spz", tau - 1), 0))
                K.w(v, "sM", marks.get(("projmm", tau - 1), 0))
                for g in range(G):
                    v.tensor_sub(s["tdx"][:, :], K.fch(fF[0], g), s["st0"][:, :])
                    if interior:
                        v.tensor_mul(K.fch(fF[1], g), s["tdx"][:, :],
                                     s["st2"][:, :])
                    v.tensor_mul(s["tt"][:, :], s["tdx"][:, :], s["st3"][:, :])
                    i = v.tensor_scalar(K.fch(s["bf1"], g), s["tt"][:, :],
                                        K.vcol(V_L1W, g), K.vcol(V_L1B, g),
                                        Alu.mult, Alu.add)
                K.inc(i, "sV")
            marks[("xfree", tau)] = K.cnt["sV"]
            marks[("xn", tau)] = K.cnt["sV"]

            if interior:
                @blk.sync
                def _(sy, tau=tau):
                    K.w(sy, "sV", marks[("xn", tau)])
                    K.inc(sy.dma_start(d["zsp"][tau], fF[1][:, :]), "zS", 16)
                marks[("spz", tau)] = K.cnt["zS"]

        def emit_mloop(tau):
            interior = 1 <= tau <= NI
            per_m = 3 if interior else 2
            sc0 = K.cnt["sC"]
            scp = lambda m, j, sc0=sc0, per_m=per_m: sc0 + m * per_m + j
            mk, mv, mr = {}, {}, {}

            @blk.tensor
            def _(t, tau=tau, scp=scp, interior=interior):
                K.w(t, "sW", wmark)
                K.w(t, "sV", marks[("xn", tau)])
                for m in range(G):
                    if m >= 2:
                        K.w(t, "sC", scp(m - 2, 1))
                    else:
                        K.bank_wait(t, 2 + m)
                    for g in range(G):
                        i = t.matmul(pb[2 + (m % 2)][:, :],
                                     K.wsl(wW[0], g, m), K.fch(s["bf1"], g),
                                     start=(g == 0), stop=(g == G - 1))
                    mk[m] = K.inc(i, "sM")
                    if m >= 2:
                        K.w(t, "sC", scp(m - 2, 2))
                    else:
                        K.bank_wait(t, 4 + m)
                    for g in range(G):
                        i = t.matmul(pb[4 + (m % 2)][:, :],
                                     K.wsl(wW[1], g, m), K.fch(s["bf1"], g),
                                     start=(g == 0), stop=(g == G - 1))
                    mv[m] = K.inc(i, "sM")
                    if interior:
                        if m >= 2:
                            K.w(t, "sC", scp(m - 2, 3))
                        else:
                            K.bank_wait(t, 6 + m)
                        for g in range(G):
                            i = t.matmul(pb[6 + (m % 2)][:, :],
                                         K.wsl(wW[2], g, m), K.fch(s["bf1"], g),
                                         start=(g == 0), stop=(g == G - 1))
                        mr[m] = K.inc(i, "sM")
            marks[("projmm", tau)] = K.cnt["sM"]

            gv0 = K.cnt["sV"]
            vneg = {m: gv0 + 2 * m + 1 for m in range(G)}
            gmk = {m: gv0 + 2 * m + 2 for m in range(G - 1)}
            gmk[G - 1] = gv0 + 2 * G + (1 if tau > 0 else 0)
            dcm = [gv0 + 2 * G]

            @blk.scalar
            def _(sc, tau=tau, scp=scp, interior=interior):
                for m in range(G):
                    K.w(sc, "sV", vneg[m])
                    if m == 0:
                        K.w(sc, "e1S", marks.get(("spe", tau - 1), 0))
                        K.w(sc, "sV", marks.get(("outf", tau - 1), 0))
                    i = sc.activation(K.sdat(0, m, 1, SC),
                                      pb[2 + (m % 2)][:, :],
                                      Act.Exp, bias=K.ccol(tau, m))
                    assert K.inc(i, "sC") == scp(m, 1)
                    K.w(sc, "sM", mv[m])
                    if m == 0:
                        K.w(sc, "sV", marks.get(("eglast", tau - 1), 0))
                    else:
                        K.w(sc, "sV", gmk[m - 1])
                    i = sc.copy(s["tv"][:, :], pb[4 + (m % 2)][:, :])
                    assert K.inc(i, "sC") == scp(m, 2)
                    if interior:
                        K.w(sc, "sM", mr[m])
                        if m == 0:
                            K.w(sc, "rS", marks.get(("spr", tau - 1), 0))
                        sc.activation(K.fch(fF[3], m), pb[6 + (m % 2)][:, :],
                                      Act.Exp, scale=-1.0)
                        sc.drain()
                        sc.activation(K.fch(fF[3], m), K.fch(fF[3], m),
                                      Act.Ln, bias=s["onec"][:, 0:1])
                        sc.drain()
                        i = sc.activation(K.fch(fF[3], m), K.fch(fF[3], m),
                                          Act.Exp, scale=-1.0)
                        assert K.inc(i, "sC") == scp(m, 3)
                if tau > 0:
                    K.w(sc, "sV", dcm[0])
                    i = sc.activation(s["sm"][:, 16:24], s["sm"][:, 8:16],
                                      Act.Exp)
                    K.inc(i, "sC")
            marks[("rdone", tau)] = K.cnt["sC"]
            marks[("rscm", tau)] = K.cnt["sC"]
            K.bankmark[2] = ("sC", scp(6, 1))
            K.bankmark[3] = ("sC", scp(7, 1))
            K.bankmark[4] = ("sC", scp(6, 2))
            K.bankmark[5] = ("sC", scp(7, 2))
            if interior:
                K.bankmark[6] = ("sC", scp(6, 3))
                K.bankmark[7] = ("sC", scp(7, 3))

            @blk.vector
            def _(v, tau=tau, gv0=gv0, scp=scp):
                for m in range(G):
                    K.w(v, "sM", mk[m])
                    i = v.tensor_reduce(K.ccol(tau, m), pb[2 + (m % 2)][:, :],
                                        AX, Alu.max, negate=True)
                    assert K.inc(i, "sV") == vneg[m]
                    if m == G - 1 and tau > 0:
                        i = v.tensor_sub(s["sm"][:, 8:16], K.ccol(tau),
                                         K.ccol(tau - 1))
                        assert K.inc(i, "sV") == dcm[0]
                    K.w(v, "sC", scp(m, 2))
                    if m == 0:
                        K.w(v, "gS", marks.get(("spg", tau - 1), 0))
                    if tau == 0:
                        v.tensor_mul(K.sdat(0, m, 1, SC), K.sdat(0, m, 1, SC),
                                     s["mask"][:, 0:NT])
                    elif tau == NTILES - 1:
                        v.tensor_mul(K.sdat(0, m, 1, SC), K.sdat(0, m, 1, SC),
                                     s["mask"][:, NT:2 * NT])
                    i = v.tensor_mul(K.sdat(1, m, 1, SC), K.sdat(0, m, 1, SC),
                                     s["tv"][:, :])
                    assert K.inc(i, "sV") == gmk[m]
            marks[("eglast", tau)] = gmk[G - 1]

        def emit_scan(tau):
            interior = 1 <= tau <= NI
            den0 = K.cnt["sV"]
            rscm = marks[("rscm", tau)]

            @blk.vector
            def _(v, tau=tau, rscm=rscm, interior=interior):
                if tau > 0:
                    K.w(v, "sC", rscm)
                    v.tensor_mul(K.col_str(sS[1], 0, SC, G),
                                 K.col_str(sS[2], NT, SC, G), s["sm"][:, 16:24])
                    v.tensor_mul(K.col_str(sS[0], 0, SC, G),
                                 K.col_str(sS[3], NT, SC, G),
                                 s["sm"][:, 16:24])
                    v.drain()
                for g in range(G):
                    v.tensor_tensor_scan(
                        K.sdat(2, g, 0, SC), K.ew_b(g, SC), K.sdat(1, g, 0, SC),
                        0.0, Alu.mult, Alu.add)
                    i = v.tensor_tensor_scan(
                        K.sdat(3, g, 0, SC), K.ew_b(g, SC), K.sdat(0, g, 0, SC),
                        0.0, Alu.mult, Alu.add)
                K.inc(i, "sV")
                if interior:
                    for g in range(G):
                        v.scalar_tensor_tensor(
                            K.sdat(2, g, 0, NT), K.sdat(1, g, 1, SC),
                            K.vcol(V_EU, g), K.sdat(2, g, 0, NT),
                            Alu.mult, Alu.add)
                        i = v.scalar_tensor_tensor(
                            K.sdat(3, g, 0, NT), K.sdat(0, g, 1, SC),
                            K.vcol(V_EU, g), K.sdat(3, g, 0, NT),
                            Alu.mult, Alu.add)
                    K.inc(i, "sV")
            marks[("scan", tau)] = den0 + 1
            denm = K.cnt["sV"]

            if tau >= 1:
                @blk.sync
                def _(sy, tau=tau):
                    K.w(sy, "sV", marks[("scan", tau)])
                    K.inc(sy.dma_start(d["E1sp"][tau], K.s_str(0, 1, SC)),
                          "e1S", 16)
                    K.inc(sy.dma_start(d["gsp"][tau], K.s_str(1, 1, SC)),
                          "gS", 16)
                marks[("spe", tau)] = K.cnt["e1S"]
                marks[("spg", tau)] = K.cnt["gS"]

            if interior:
                @blk.sync
                def _(sy, tau=tau):
                    K.w(sy, "rS", marks.get(("spr", tau - 1), 0))
                    K.w(sy, "sC", marks[("rdone", tau)])
                    K.inc(sy.dma_start(d["rsp"][tau], fF[3][:, :]), "rS", 16)
                marks[("spr", tau)] = K.cnt["rS"]

                @blk.scalar
                def _(sc, denm=denm):
                    K.w(sc, "sV", denm)
                    sc.activation(K.s_str(3, 0, NT), K.s_str(3, 0, NT), Act.Ln)
                    i = sc.activation(K.s_str(3, 0, NT), K.s_str(3, 0, NT),
                                      Act.Exp, scale=-1.0)
                    K.inc(i, "sC")
                marks[("rdm", tau)] = K.cnt["sC"]

        def emit_outf(tau):
            rdm = marks[("rdm", tau)]

            @blk.vector
            def _(v, tau=tau, rdm=rdm):
                K.w(v, "sC", rdm)
                K.w(v, "ofS", marks.get(("spf", tau - 1), 0))
                i = v.tensor_mul(K.f_str(fF[2], 0, NT),
                                 K.s_str(2, 0, NT), K.s_str(3, 0, NT))
                K.inc(i, "sV")
            marks[("outf", tau)] = K.cnt["sV"]

            @blk.sync
            def _(sy, tau=tau):
                K.w(sy, "sV", marks[("outf", tau)])
                K.inc(sy.dma_start(d["outfsp"][tau], fF[2][:, :]),
                      "ofS", 16)
            marks[("spf", tau)] = K.cnt["ofS"]

        emit_xstats(0)
        emit_dxz(0)
        for tau in range(NTILES):
            emit_mloop(tau)
            if tau + 1 < NTILES:
                emit_xstats(tau + 1)
            emit_scan(tau)
            if tau + 1 < NTILES:
                emit_dxz(tau + 1)
            if 1 <= tau <= NI:
                emit_outf(tau)

        @blk.sync
        def _(sy):
            for n in ["zS", "e1S", "gS", "ofS", "rS", "xL", "vecL"]:
                K.w(sy, n, K.cnt[n])


def emit_pass_b1(K):
    nc, s, d, pb, fF, sS, wW = K.nc, K.s, K.d, K.pb, K.fF, K.sS, K.wW
    K.bankmark = {}
    with nc.Block("passB1", no_gpsimd_drain=True) as blk:
        @blk.gpsimd
        def _(gp):
            K.inc(gp.dma_start(K.w_out3(wW[0], C, G), K.w_in3("WoT", C)),
                  "sW", 16)
        wmark = K.cnt["sW"]

        @blk.vector
        def _(v):
            K.w(v, "sW", wmark)
            for g in range(G):
                v.tensor_scalar_mul(wW[0][:, g * C:(g + 1) * C],
                                    wW[0][:, g * C:(g + 1) * C], 0.5)
            v.memset(K.col_str(sS[0], NT, SC, G), 0.0)
            i = v.memset(K.col_str(sS[1], NT, SC, G), 0.0)
            K.inc(i, "sV")
        wfold = K.cnt["sV"]

        marks = {}

        def emit_loads_eg(tau):
            up = tau + 1

            @blk.sync
            def _(sy, tau=tau, up=up):
                K.w(sy, "sV", marks.get(("wkv", up), marks.get(("scan", up), 0)))
                K.inc(sy.dma_start(K.s_str(0, 0, NT), d["E1sp"][tau]),
                      "e1L", 16)
                K.inc(sy.dma_start(K.s_str(1, 0, NT), d["gsp"][tau]), "gL", 16)
                if 1 <= tau <= NI:
                    K.inc(sy.dma_start(fF[1][:, :], d["outfsp"][tau]),
                          "ofL", 16)
                    K.inc(sy.dma_start(fF[2][:, :], d["rsp"][tau]), "rL", 16)
            marks[("lde", tau)] = K.cnt["e1L"]
            marks[("ldg", tau)] = K.cnt["gL"]
            marks[("ldo", tau)] = K.cnt["ofL"]
            marks[("ldr", tau)] = K.cnt["rL"]

        def emit_loads_z(tau):
            up = tau + 1

            @blk.sync
            def _(sy, tau=tau, up=up):
                K.w(sy, "a1S", marks.get(("spa1", up), 0))
                K.inc(sy.dma_start(fF[0][:, :], d["zsp"][tau]), "zL", 16)
            marks[("ldz", tau)] = K.cnt["zL"]

        emit_loads_eg(NTILES - 1)
        emit_loads_z(NTILES - 2)
        for tau in range(NTILES - 1, 0, -1):
            interior = 1 <= tau <= NI

            dcm = [0]
            if tau < NTILES - 1:
                @blk.vector
                def _(v, tau=tau):
                    i = v.tensor_sub(s["sm"][:, 8:16], K.ccol(tau),
                                     K.ccol(tau + 1))
                    dcm[0] = K.inc(i, "sV")

                @blk.scalar
                def _(sc):
                    K.w(sc, "sV", dcm[0])
                    i = sc.activation(s["sm"][:, 16:24], s["sm"][:, 8:16],
                                      Act.Exp)
                    K.inc(i, "sC")
            rscm = K.cnt["sC"]

            scn = [0]

            @blk.vector
            def _(v, tau=tau, rscm=rscm):
                K.w(v, "e1L", marks[("lde", tau)])
                K.w(v, "gL", marks[("ldg", tau)])
                if tau < NTILES - 1:
                    K.w(v, "sC", rscm)
                    v.tensor_mul(K.col_str(sS[1], NT, SC, G),
                                 K.col_str(sS[2], NT, SC, G), s["sm"][:, 16:24])
                    v.tensor_mul(K.col_str(sS[0], NT, SC, G),
                                 K.col_str(sS[3], NT, SC, G),
                                 s["sm"][:, 16:24])
                    v.drain()
                for g in range(G):
                    v.tensor_tensor_scan(
                        K.sdat(2, g, 0, SC), K.ew_b(g, SC),
                        K.s_rev(1, g, NT, SC), 0.0, Alu.mult, Alu.add)
                    i = v.tensor_tensor_scan(
                        K.sdat(3, g, 0, SC), K.ew_b(g, SC),
                        K.s_rev(0, g, NT, SC), 0.0, Alu.mult, Alu.add)
                scn[0] = K.inc(i, "sV")
                if interior:
                    K.w(v, "sM", marks.get(("womm", tau + 1), 0))
                    for g in range(G):
                        v.scalar_tensor_tensor(
                            K.sdat(1, g, 0, NT), K.sdat(1, g, 0, NT),
                            K.vcol(V_EU, g), K.s_rev(2, g, NT - 1, NT),
                            Alu.mult, Alu.add)
                        i = v.scalar_tensor_tensor(
                            K.sdat(0, g, 0, NT), K.sdat(0, g, 0, NT),
                            K.vcol(V_EU, g), K.s_rev(3, g, NT - 1, NT),
                            Alu.mult, Alu.add)
                    K.inc(i, "sV")
            marks[("scan", tau)] = scn[0]
            denm = K.cnt["sV"]

            if not interior:
                if tau > 1:
                    emit_loads_eg(tau - 1)
                continue

            @blk.scalar
            def _(sc, denm=denm):
                K.w(sc, "sV", denm)
                sc.activation(K.s_str(0, 0, NT), K.s_str(0, 0, NT), Act.Ln)
                i = sc.activation(K.s_str(0, 0, NT), K.s_str(0, 0, NT),
                                  Act.Exp, scale=-1.0)
                K.inc(i, "sC")
            rdm = K.cnt["sC"]

            @blk.vector
            def _(v, tau=tau, rdm=rdm):
                K.w(v, "ofL", marks[("ldo", tau)])
                K.w(v, "rL", marks[("ldr", tau)])
                v.tensor_mul(K.f_str(fF[1], 0, NT), K.f_str(fF[1], 0, NT),
                             K.f_str(fF[2], 0, NT))
                v.tensor_mul(K.s_str(1, 0, NT), K.s_str(1, 0, NT),
                             K.f_str(fF[2], 0, NT))
                K.w(v, "sC", rdm)
                v.tensor_mul(K.s_str(1, 0, NT), K.s_str(1, 0, NT),
                             K.s_str(0, 0, NT))
                i = v.tensor_add(K.f_str(s["bf1"], 0, NT),
                                 K.f_str(fF[1], 0, NT), K.s_str(1, 0, NT))
                K.inc(i, "sV")
            marks[("wkv", tau)] = K.cnt["sV"]
            if tau > 1:
                emit_loads_eg(tau - 1)

            mo = {}
            ev0 = K.cnt["sV"]

            @blk.tensor
            def _(t, tau=tau, ev0=ev0):
                K.w(t, "sV", wfold)
                K.w(t, "sV", marks[("wkv", tau)])
                for m in range(G):
                    if m >= 2:
                        K.w(t, "sV", ev0 + m - 1)
                    else:
                        K.bank_wait(t, m)
                    for g in range(G):
                        i = t.matmul(pb[m % 2][:, :], K.wsl(wW[0], g, m),
                                     K.fch(s["bf1"], g),
                                     start=(g == 0), stop=(g == G - 1))
                    mo[m] = K.inc(i, "sM")
            marks[("womm", tau)] = K.cnt["sM"]
            K.bankmark[0] = ("sV", ev0 + 7)
            K.bankmark[1] = ("sV", ev0 + 8)

            @blk.vector
            def _(v, tau=tau, ev0=ev0):
                for m in range(G):
                    K.w(v, "sM", mo[m])
                    if m == 0:
                        K.w(v, "zL", marks[("ldz", tau)])
                    i = v.tensor_add(K.fch(fF[0], m), K.fch(fF[0], m),
                                     pb[m % 2][:, :])
                    assert K.inc(i, "sV") == ev0 + m + 1
            marks[("evac", tau)] = K.cnt["sV"]
            marks[("a1", tau)] = K.cnt["sV"]

            @blk.sync
            def _(sy, tau=tau):
                K.w(sy, "sV", marks[("a1", tau)])
                K.inc(sy.dma_start(d["a1sp"][tau], fF[0][:, :]), "a1S", 16)
            marks[("spa1", tau)] = K.cnt["a1S"]
            if tau > 1:
                emit_loads_z(tau - 1)

        @blk.sync
        def _(sy):
            for n in ["a1S", "e1L", "gL", "zL", "ofL", "rL"]:
                K.w(sy, n, K.cnt[n])


def emit_pass_b2a(K):
    nc, s, d, pb, fF = K.nc, K.s, K.d, K.pb, K.fF
    K.bankmark = {}
    with nc.Block("passB2a", no_gpsimd_drain=True) as blk:
        @blk.gpsimd
        def _(gp):
            K.inc(gp.dma_start(K.w_out3(s["wbig"], H, G), K.w_in3("WfkT", H)),
                  "sW", 16)
        wmark = K.cnt["sW"]

        marks = {}

        def emit_ldfront(tau):
            @blk.sync
            def _(sy, tau=tau):
                K.w(sy, "sV", marks.get(("xn2", tau - 1), 0))
                K.inc(sy.dma_start(fF[0][:, :], d["a1sp"][tau]), "a1L", 16)
            marks[("ld", tau)] = K.cnt["a1L"]
            marks[("sm0", tau)] = emit_stats_front(
                K, blk, fF[0], pre=[("a1L", marks[("ld", tau)])], xsq_gp=True)

        def emit_xn2(tau):
            cmark = emit_stats_back(K, blk, [("st2", "r")],
                                    m0=marks[("sm0", tau)])

            @blk.vector
            def _(v, tau=tau, cmark=cmark):
                K.w(v, "sC", cmark)
                K.w(v, "xnS", marks.get(("spx", tau - 1), 0))
                K.w(v, "sM", marks.get(("fkmm", tau - 1), 0))
                for g in range(G):
                    v.tensor_sub(s["tdx"][:, :], K.fch(fF[0], g), s["st0"][:, :])
                    v.tensor_mul(s["tt"][:, :], s["tdx"][:, :], s["st2"][:, :])
                    i = v.tensor_scalar(K.fch(s["bf1"], g), s["tt"][:, :],
                                        K.vcol(V_L2W, g), K.vcol(V_L2B, g),
                                        Alu.mult, Alu.add)
                K.inc(i, "sV")
            marks[("xn2", tau)] = K.cnt["sV"]

            @blk.sync
            def _(sy, tau=tau):
                K.w(sy, "sV", marks[("xn2", tau)])
                K.inc(sy.dma_start(d["xn2sp"][tau], s["bf1"][:, :]), "xnS", 16)
            marks[("spx", tau)] = K.cnt["xnS"]

        def emit_wfk(tau):
            sc0 = K.cnt["sC"]
            mh = {}
            marks[("mh", tau)] = mh
            marks[("sc0", tau)] = sc0

            @blk.tensor
            def _(t, tau=tau, sc0=sc0):
                K.w(t, "sW", wmark)
                K.w(t, "sV", marks[("xn2", tau)])
                for h in range(GH):
                    if h >= 4:
                        K.w(t, "sC", sc0 + h - 3)
                    else:
                        K.bank_wait(t, 2 + h)
                    for g in range(G):
                        i = t.matmul(pb[2 + (h % 4)][:, :],
                                     K.wsl_h(s["wbig"], g, H, h),
                                     K.fch(s["bf1"], g),
                                     start=(g == 0), stop=(g == G - 1))
                    mh[h] = K.inc(i, "sM")
            marks[("fkmm", tau)] = K.cnt["sM"]
            for b in range(4):
                K.bankmark[2 + b] = ("sC", sc0 + GH - 3 + b)

            sqv0 = K.cnt["sV"]
            marks[("sqv0", tau)] = sqv0

            @blk.scalar
            def _(sc, sc0=sc0, sqv0=sqv0):
                for h in range(GH):
                    K.w(sc, "sM", mh[h])
                    K.w(sc, "sV", sqv0 + h)
                    i = sc.activation(s["trel"][:, :], pb[2 + (h % 4)][:, :],
                                      Act.Relu)
                    assert K.inc(i, "sC") == sc0 + h + 1

        def emit_sq(tau):
            sc0 = marks[("sc0", tau)]
            sqv0 = marks[("sqv0", tau)]
            assert sqv0 == K.cnt["sV"]

            @blk.sync
            def _(sy, tau=tau, sqv0=sqv0):
                K.w(sy, "sV", sqv0 + 16)
                K.inc(sy.dma_start(d["kksp"][tau, 0], s["bf2"][:, :]),
                      "kkS", 16)
                marks[("spk", tau, 0)] = K.cnt["kkS"]
                K.w(sy, "sV", sqv0 + 32)
                K.inc(sy.dma_start(d["kksp"][tau, 1], s["bf2"][:, :]),
                      "kkS", 16)
                marks[("spk", tau, 1)] = K.cnt["kkS"]

            @blk.vector
            def _(v, tau=tau, sc0=sc0, sqv0=sqv0):
                for h in range(GH):
                    K.w(v, "sC", sc0 + h + 1)
                    if h == 0:
                        K.w(v, "kkS", marks.get(("spk", tau - 1, 1), 0))
                    elif h == 16:
                        K.w(v, "kkS", marks[("spk", tau, 0)])
                    i = v.tensor_mul(s["bf2"][:, (h % 16) * NT:
                                              (h % 16 + 1) * NT],
                                     s["trel"][:, :], s["trel"][:, :])
                    assert K.inc(i, "sV") == sqv0 + h + 1

        emit_ldfront(1)
        emit_xn2(1)
        for tau in range(1, NI + 1):
            emit_wfk(tau)
            if tau < NI:
                emit_ldfront(tau + 1)
            emit_sq(tau)
            if tau < NI:
                emit_xn2(tau + 1)

        @blk.sync
        def _(sy):
            for n in ["xnS", "kkS", "a1L"]:
                K.w(sy, n, K.cnt[n])


def emit_pass_b2b(K):
    nc, s, d, pb, fF, wW = K.nc, K.s, K.d, K.pb, K.fF, K.wW
    K.bankmark = {}
    qv = s["qv"]
    with nc.Block("passB2b", no_gpsimd_drain=True) as blk:
        b2a_sc = K.cnt["sC"]
        b2a_sm = K.cnt["sM"]
        b2a_sv = K.cnt["sV"]

        @blk.gpsimd
        def _(gp):
            K.inc(gp.dma_start(K.w_out3(wW[0], C, G), K.w_in3("WfrT", C)),
                  "sW", 16)
            K.inc(gp.dma_start(K.w_out3(s["w8stg"], C, G), K.w_in3("w8T", C)),
                  "sW", 16)
            K.inc(gp.dma_start(K.w_out3(s["wbig"], C, GH), K.w_in3("WfvT", C)),
                  "sW", 16)
        wmark = K.cnt["sW"]

        @blk.vector
        def _(v):
            K.w(v, "sW", wmark)
            for j in range(G * C // NT):
                v.tensor_copy(s["tw"][:, :], s["w8stg"][:, j * NT:(j + 1) * NT])
                i = v.tensor_copy(wW[1][:, j * NT:(j + 1) * NT], s["tw"][:, :])
            K.inc(i, "sV")
        w8c = K.cnt["sV"]

        dbg = "dbg" in DEBUG_SPILLS
        marks = {}

        @blk.sync
        def _(sy):
            K.w(sy, "sM", b2a_sm)
            K.inc(sy.dma_start(s["bf1"][:, :], d["xn2sp"][1]), "xnL", 16)
            K.inc(sy.dma_start(s["bf2"][:, 0:8 * NT],
                               d["kksp"][1, 0][:, 0:8 * NT]), "kkL0", 16)
        marks[("ldxn", 1)] = K.cnt["xnL"]
        marks[("ldk", 1, 0)] = K.cnt["kkL0"]

        def emit_wfr(tau):
            sc0 = K.cnt["sC"]
            marks[("sc0", tau)] = sc0
            mf = {}
            marks[("mf", tau)] = mf
            if tau > 1:
                marks[("stcp", tau - 1)] = K.cnt["sV"] + 1

            @blk.tensor
            def _(t, tau=tau, sc0=sc0):
                K.w(t, "sW", wmark)
                K.w(t, "xnL", marks[("ldxn", tau)])
                for m in range(G):
                    if m >= 2:
                        K.w(t, "sC", sc0 + m - 1)
                    elif m == 0:
                        if tau == 1:
                            K.w(t, "sC", b2a_sc)
                        else:
                            K.w(t, "sV", marks[("stcp", tau - 1)])
                    for g in range(G):
                        i = t.matmul(pb[4 + (m % 2)][:, :], K.wsl(wW[0], g, m),
                                     K.fch(s["bf1"], g),
                                     start=(g == 0), stop=(g == G - 1))
                    mf[m] = K.inc(i, "sM")
            marks[("wfr", tau)] = K.cnt["sM"]

        def emit_sgate(tau):
            sc0 = marks[("sc0", tau)]
            mf = marks[("mf", tau)]

            @blk.scalar
            def _(sc, tau=tau, sc0=sc0, mf=mf):
                for m in range(G):
                    K.w(sc, "sM", mf[m])
                    if m == 0:
                        if tau == 1:
                            K.w(sc, "sV", w8c)
                        K.w(sc, "sV", marks.get(("a2", tau - 1), 0))
                        if dbg:
                            K.w(sc, "dbg4", marks.get(("dsp4", tau - 1), 0))
                    sc.activation(K.fch(fF[2], m), pb[4 + (m % 2)][:, :],
                                  Act.Exp, scale=-1.0)
                    sc.activation(K.fch(fF[2], m), K.fch(fF[2], m),
                                  Act.Ln, bias=s["onec"][:, 0:1])
                    i = sc.activation(K.fch(fF[2], m), K.fch(fF[2], m),
                                      Act.Exp, scale=-1.0)
                    assert K.inc(i, "sC") == sc0 + m + 1
            marks[("sgate", tau)] = K.cnt["sC"]

            if tau < NI:
                @blk.sync
                def _(sy, tau=tau):
                    K.w(sy, "sM", marks[("wfr", tau)])
                    K.inc(sy.dma_start(s["bf1"][:, :], d["xn2sp"][tau + 1]),
                          "xnL", 16)
                marks[("ldxn", tau + 1)] = K.cnt["xnL"]

        def emit_z2r0(tau):
            m0 = K.cnt["sM"]
            marks[("z2m0", tau)] = m0

            @blk.sync
            def _(sy, tau=tau, m0=m0):
                for step in range(1, 4):
                    prev = (m0 + step - 1) if step >= 2 else \
                        marks.get(("zmm", tau - 1, 7), 0)
                    K.w(sy, "sM", prev)
                    srcv = d["kksp"][tau, step // 2][:, (step % 2) * 8 * NT:
                                                     ((step % 2) + 1) * 8 * NT]
                    sem = "kkL0" if step % 2 == 0 else "kkL1"
                    K.inc(sy.dma_start(
                        s["bf2"][:, (step % 2) * 8 * NT:
                                 ((step % 2) + 1) * 8 * NT], srcv), sem, 16)
                    marks[("ldk", tau, step)] = K.cnt[sem]

            @blk.tensor
            def _(t, tau=tau, m0=m0):
                K.w(t, "sW", wmark)
                for step in range(4):
                    sem = "kkL0" if step % 2 == 0 else "kkL1"
                    K.w(t, sem, marks[("ldk", tau, step)])
                    if step == 0:
                        if tau == 1:
                            K.w(t, "sC", b2a_sc)
                            K.w(t, "sV", b2a_sv)
                        for mm_ in range(4):
                            K.bank_wait(t, mm_)
                    for hh in range(8):
                        h = step * 8 + hh
                        rhs = s["bf2"][:, ((step % 2) * 8 + hh) * NT:
                                       ((step % 2) * 8 + hh + 1) * NT]
                        for mm_ in range(4):
                            i = t.matmul(pb[mm_][:, :],
                                         K.wsl_h(s["wbig"], h, C, 4 + mm_),
                                         rhs, start=(h == 0), stop=(h == 31))
                    assert K.inc(i, "sM") == m0 + step + 1

        def emit_tail_a(tau):
            e80 = K.cnt["sV"]
            m8 = {}

            @blk.tensor
            def _(t, tau=tau, e80=e80):
                K.w(t, "sV", max(marks[("qm", tau)], w8c))
                for m in range(G):
                    if m >= 2:
                        K.w(t, "sV", e80 + m - 1)
                    for g in range(G):
                        i = t.matmul(pb[6 + (m % 2)][:, :], K.wsl(wW[1], g, m),
                                     K.fch(qv, g),
                                     start=(g == 0), stop=(g == G - 1))
                    m8[m] = K.inc(i, "sM")
            marks[("w8mm", tau)] = K.cnt["sM"]
            K.bankmark[6] = ("sV", e80 + 7)
            K.bankmark[7] = ("sV", e80 + 8)

            @blk.vector
            def _(v, tau=tau, e80=e80):
                for m in range(G):
                    K.w(v, "sM", m8[m])
                    if m == 0 and dbg:
                        K.w(v, "dbg3", marks.get(("dsp3", tau - 1), 0))
                    v.tensor_mul(K.fch(fF[3], m), pb[6 + (m % 2)][:, :],
                                 s["st3"][:, :])
                    i = v.tensor_scalar_add(K.fch(fF[3], m), K.fch(fF[3], m),
                                            K.vcol(V_FB, m))
                    assert K.inc(i, "sV") == e80 + m + 1
            marks[("f", tau)] = K.cnt["sV"]

        def emit_loads_a(tau):
            @blk.sync
            def _(sy, tau=tau):
                if tau == 1:
                    K.w(sy, "sV", w8c)
                K.w(sy, "sV", marks.get(("qm", tau - 1), 0))
                if dbg:
                    K.w(sy, "dbg0", marks.get(("dsp0", tau - 1), 0))
                    K.w(sy, "dbg2", marks.get(("dsp2", tau - 1), 0))
                K.inc(sy.dma_start(fF[1][:, 4 * NT:8 * NT],
                                   d["a1sp"][tau][:, 4 * NT:8 * NT]),
                      "a1L2", 16)
                K.w(sy, "sM", marks.get(("w8mm", tau - 1), 0))
                K.inc(sy.dma_start(fF[1][:, 0:4 * NT],
                                   d["a1sp"][tau][:, 0:4 * NT]), "a1L", 16)
            marks[("lda1h", tau)] = K.cnt["a1L2"]
            marks[("lda1", tau)] = K.cnt["a1L"]

        def emit_a2r0(tau):
            m0 = marks[("z2m0", tau)]
            ev0 = K.cnt["sV"]

            @blk.vector
            def _(v, tau=tau, m0=m0, ev0=ev0):
                for mm_ in range(4):
                    m = 4 + mm_
                    K.w(v, "sM", m0 + 4)
                    if mm_ == 0:
                        K.w(v, "sC", marks[("sgate", tau)])
                        K.w(v, "a1L2", marks[("lda1h", tau)])
                    v.tensor_mul(s["tt"][:, :], K.fch(fF[2], m), pb[mm_][:, :])
                    i = v.tensor_add(K.fch(fF[1], m), K.fch(fF[1], m),
                                     s["tt"][:, :])
                    assert K.inc(i, "sV") == ev0 + mm_ + 1
            marks[("a2r0", tau)] = K.cnt["sV"]

        def emit_z2r1(tau):
            m1 = K.cnt["sM"]
            marks[("z2m1", tau)] = m1
            ar0 = marks[("a2r0", tau)]
            marks[("ldk", tau, 6)] = K.cnt["kkL0"] + 16
            marks[("ldk", tau, 7)] = K.cnt["kkL1"] + 16

            @blk.tensor
            def _(t, tau=tau, m1=m1, ar0=ar0):
                for step in range(4, 8):
                    half = (2, 3, 0, 1)[step - 4]
                    if step >= 6:
                        sem = "kkL0" if step % 2 == 0 else "kkL1"
                        K.w(t, sem, marks[("ldk", tau, step)])
                    if step == 4:
                        for mm_ in range(4):
                            K.w(t, "sV", ar0 - 3 + mm_)
                    for hh in range(8):
                        h = half * 8 + hh
                        rhs = s["bf2"][:, ((step % 2) * 8 + hh) * NT:
                                       ((step % 2) * 8 + hh + 1) * NT]
                        for mm_ in range(4):
                            i = t.matmul(pb[mm_][:, :],
                                         K.wsl_h(s["wbig"], h, C, mm_),
                                         rhs, start=(step == 4 and hh == 0),
                                         stop=(step == 7 and hh == 7))
                    assert K.inc(i, "sM") == m1 + step - 3
            marks[("zmm", tau, 6)] = m1 + 3
            marks[("zmm", tau, 7)] = m1 + 4

        def emit_loads_b(tau):
            m1 = marks[("z2m1", tau)]

            @blk.sync
            def _(sy, tau=tau, m1=m1):
                for step in (6, 7):
                    K.w(sy, "sM", m1 + step - 5)
                    srcv = d["kksp"][tau, 0][
                        :, (step % 2) * 8 * NT:((step % 2) + 1) * 8 * NT]
                    sem = "kkL0" if step % 2 == 0 else "kkL1"
                    K.inc(sy.dma_start(
                        s["bf2"][:, (step % 2) * 8 * NT:
                                 ((step % 2) + 1) * 8 * NT], srcv), sem, 16)
                    assert marks[("ldk", tau, step)] == K.cnt[sem]
                if tau < NI:
                    K.w(sy, "sM", m1 + 3)
                    K.inc(sy.dma_start(
                        s["bf2"][:, 0:8 * NT],
                        d["kksp"][tau + 1, 0][:, 0:8 * NT]), "kkL0", 16)
                    marks[("ldk", tau + 1, 0)] = K.cnt["kkL0"]

        def emit_tail_b(tau):
            if dbg:
                @blk.sync
                def _(sy, tau=tau):
                    K.w(sy, "dbg3", marks.get(("dsp3", tau - 1), 0))
                    K.w(sy, "sV", marks[("f", tau)])
                    K.inc(sy.dma_start(d["dbg"][tau, 3], fF[3][:, :]),
                          "dbg3", 16)
                marks[("dsp3", tau)] = K.cnt["dbg3"]

            cmark = emit_stats(K, blk, fF[3], [("st2", "r")],
                               pre=[("sV", marks[("f", tau)])], banks=(6, 7))

            @blk.vector
            def _(v, tau=tau, cmark=cmark):
                K.w(v, "sC", cmark)
                if dbg:
                    K.w(v, "dbg1", marks.get(("dsp1", tau), 0))
                for g in range(G):
                    v.tensor_sub(s["tdx"][:, :], K.fch(fF[3], g), s["st0"][:, :])
                    v.tensor_mul(s["tt"][:, :], s["tdx"][:, :], s["st2"][:, :])
                    i = v.tensor_add(K.fch(fF[0], g), K.fch(fF[0], g),
                                     s["tt"][:, :])
                K.inc(i, "sV")
            marks[("out", tau)] = K.cnt["sV"]

        def emit_spill_out(tau):
            @blk.sync
            def _(sy, tau=tau):
                K.w(sy, "outS", marks.get(("spo", tau - 1), 0))
                K.w(sy, "sV", marks[("out", tau)])
                K.inc(sy.dma_start(d["out"][tau - 1], fF[0][:, :]), "outS", 16)
            marks[("spo", tau)] = K.cnt["outS"]

            @blk.sync
            def _(sy, tau=tau):
                K.w(sy, "outS", marks[("spo", tau)])
                K.inc(sy.dma_start(K.f_str(fF[0], 0, NT), K.xT3(tau + 1)),
                      "xL", 16)
            marks[("ldx2", tau + 1)] = K.cnt["xL"]

        def emit_mid_a(tau):
            m1 = marks[("z2m1", tau)]

            @blk.vector
            def _(v, tau=tau, m1=m1):
                for mm_ in range(4):
                    K.w(v, "sM", m1 + 4)
                    if mm_ == 0:
                        K.w(v, "a1L", marks[("lda1", tau)])
                    v.tensor_mul(s["tt"][:, :], K.fch(fF[2], mm_),
                                 pb[mm_][:, :])
                    i = v.tensor_add(K.fch(fF[1], mm_), K.fch(fF[1], mm_),
                                     s["tt"][:, :])
                    K.inc(i, "sV")
                    K.bank_set(mm_, "sV")
            marks[("a2", tau)] = K.cnt["sV"]
            if dbg:
                @blk.sync
                def _(sy, tau=tau):
                    K.w(sy, "dbg0", marks.get(("dsp0", tau - 1), 0))
                    K.w(sy, "sV", marks[("a2", tau)])
                    K.inc(sy.dma_start(d["dbg"][tau, 0], fF[1][:, :]),
                          "dbg0", 16)
                    K.w(sy, "dbg4", marks.get(("dsp4", tau - 1), 0))
                    K.w(sy, "sC", marks[("sgate", tau)])
                    K.inc(sy.dma_start(d["dbg"][tau, 4], fF[2][:, :]),
                          "dbg4", 16)
                marks[("dsp0", tau)] = K.cnt["dbg0"]
                marks[("dsp4", tau)] = K.cnt["dbg4"]

            cmark = emit_stats(K, blk, fF[1], [("st2", "r")],
                               pre=[("sV", marks[("a2", tau)])], banks=(6, 7))

            @blk.vector
            def _(v, tau=tau, cmark=cmark):
                K.w(v, "sC", cmark)
                K.w(v, "xL", marks[("ldx2", tau)])
                for g in range(G):
                    v.tensor_sub(s["tdx"][:, :], K.fch(fF[1], g), s["st0"][:, :])
                    v.tensor_mul(s["tt"][:, :], s["tdx"][:, :], s["st2"][:, :])
                    i = v.tensor_add(K.fch(fF[0], g), K.fch(fF[0], g),
                                     s["tt"][:, :])
                K.inc(i, "sV")
            marks[("x1", tau)] = K.cnt["sV"]
            if dbg:
                @blk.sync
                def _(sy, tau=tau):
                    K.w(sy, "dbg1", marks.get(("dsp1", tau - 1), 0))
                    K.w(sy, "sV", marks[("x1", tau)])
                    K.inc(sy.dma_start(d["dbg"][tau, 1], fF[0][:, :]),
                          "dbg1", 16)
                marks[("dsp1", tau)] = K.cnt["dbg1"]

        def emit_mid_b(tau):
            cmark = emit_stats(K, blk, fF[0], [("st2", "tau")],
                               pre=[("sV", marks[("x1", tau)])], banks=(6, 7))

            @blk.vector
            def _(v, tau=tau, cmark=cmark):
                K.w(v, "sC", cmark)
                if dbg:
                    K.w(v, "dbg0", marks.get(("dsp0", tau), 0))
                for g in range(G):
                    v.tensor_sub(s["tdx"][:, :], K.fch(fF[0], g), s["st0"][:, :])
                    i = v.tensor_mul(K.fch(fF[1], g), s["tdx"][:, :],
                                     s["st2"][:, :])
                K.inc(i, "sV")
                v.scalar_tensor_tensor(s["tt"][:, :], K.fch(fF[1], 0), -1.0,
                                       K.fch(fF[1], 0), Alu.mult, Alu.max)
                for g in range(1, G):
                    v.scalar_tensor_tensor(s["tt2"][:, :], K.fch(fF[1], g),
                                           -1.0, K.fch(fF[1], g),
                                           Alu.mult, Alu.max)
                    v.tensor_max(s["tt"][:, :], s["tt"][:, :], s["tt2"][:, :])
                v.tensor_copy(s["txb"][:, 0:NT], s["tt"][:, :])
                v.tensor_sub(s["tt2"][:, :], s["tt"][:, :], s["txb"][:, 0:NT])
                i = v.tensor_copy(s["txq"][:, 0:NT], s["tt2"][:, :])
                K.inc(i, "sV")
            marks[("rm", tau)] = K.cnt["sV"]
            mxs = K.cnt["sV"]
            if dbg:
                @blk.sync
                def _(sy, tau=tau):
                    K.w(sy, "dbg2", marks.get(("dsp2", tau - 1), 0))
                    K.w(sy, "sV", marks[("rm", tau)])
                    K.inc(sy.dma_start(d["dbg"][tau, 2], fF[1][:, :]),
                          "dbg2", 16)
                marks[("dsp2", tau)] = K.cnt["dbg2"]

            @blk.tensor
            def _(t, mxs=mxs):
                K.w(t, "sV", mxs)
                K.bank_wait(t, 4)
                for j in range(NQ):
                    t.matmul(pb[4][0:QW, j * 128:(j + 1) * 128],
                             s["txb"][:, j * QW:(j + 1) * QW],
                             s["identb"][:, :], start=True, stop=False)
                    i = t.matmul(pb[4][0:QW, j * 128:(j + 1) * 128],
                                 s["txq"][:, j * QW:(j + 1) * QW],
                                 s["identb"][:, :], start=False, stop=True)
                K.inc(i, "sM")
            trm = K.cnt["sM"]

            @blk.vector
            def _(v, trm=trm):
                K.w(v, "sM", trm)
                for j in range(NQ):
                    i = v.tensor_reduce(s["sm"][0:QW, j:j + 1],
                                        pb[4][0:QW, j * 128:(j + 1) * 128],
                                        AX, Alu.max)
                K.inc(i, "sV")

            @blk.vector
            def _(v):
                v.drain()
                v.tensor_scalar_max(s["sm"][:, 4:4 + NQ], s["sm"][:, 0:NQ],
                                    EPS_ACT)
                v.drain()
                K.inc(v.engine_nop(), "sV")
                v.tensor_scalar_mul(s["sm"][:, 12:12 + NQ], s["sm"][:, 4:4 + NQ],
                                    s["wsbc"][:, 0:1])
                v.drain()
                v.tensor_scalar_mul(s["sm"][:, 12:12 + NQ],
                                    s["sm"][:, 12:12 + NQ], 1.0 / 127.0)
                v.drain()
                v.tensor_copy(K.col_str(s["hl8"], 2 * NQ, 2, NQ),
                              s["sm"][:, 12:12 + NQ])
                v.drain()
                v.tensor_sub(s["sm"][:, 16:16 + NQ], s["sm"][:, 12:12 + NQ],
                             K.col_str(s["hl8"], 2 * NQ, 2, NQ))
                v.drain()
                v.tensor_copy(K.col_str(s["hl8"], 2 * NQ + 1, 2, NQ),
                              s["sm"][:, 16:16 + NQ])
                v.drain()
                K.inc(v.engine_nop(), "sV")
            sm4m = K.cnt["sV"] - 1

            @blk.scalar
            def _(sc, sm4m=sm4m):
                K.w(sc, "sV", sm4m)
                sc.activation(s["sm"][:, 8:8 + NQ], s["sm"][:, 4:4 + NQ],
                              Act.Ln)
                sc.drain()
                i = sc.activation(s["sm"][:, 8:8 + NQ], s["sm"][:, 8:8 + NQ],
                                  Act.Exp, scale=-1.0)
                sc.drain()
                K.inc(i, "sC")
            scm = K.cnt["sC"]

            @blk.vector
            def _(v, scm=scm):
                K.w(v, "sC", scm)
                v.tensor_scalar_mul(s["sm"][:, 8:8 + NQ], s["sm"][:, 8:8 + NQ],
                                    127.0)
                v.drain()
                v.tensor_copy(K.col_str(s["hl8"], 0, 2, NQ), s["sm"][:, 8:8 + NQ])
                v.drain()
                v.tensor_sub(s["sm"][:, 16:16 + NQ], s["sm"][:, 8:8 + NQ],
                             K.col_str(s["hl8"], 0, 2, NQ))
                v.drain()
                v.tensor_copy(K.col_str(s["hl8"], 1, 2, NQ),
                              s["sm"][:, 16:16 + NQ])
                v.drain()
                K.inc(v.engine_nop(), "sV")
            hlm = K.cnt["sV"]

            @blk.tensor
            def _(t, hlm=hlm):
                K.w(t, "sV", hlm)
                K.bank_wait(t, 5)
                t.matmul(pb[4][0:2 * NQ, 0:128], s["hl8"][:, 0:2 * NQ],
                         s["identb"][:, :], start=True, stop=True)
                i = t.matmul(pb[5][0:2 * NQ, 0:128],
                             s["hl8"][:, 2 * NQ:4 * NQ],
                             s["identb"][:, :], start=True, stop=True)
                K.inc(i, "sM")
            t2m = K.cnt["sM"]

            @blk.vector
            def _(v, t2m=t2m):
                K.w(v, "sM", t2m)
                v.tensor_copy(s["trows"][0:2 * NQ, 0:128],
                              pb[4][0:2 * NQ, 0:128])
                i = v.tensor_copy(s["trows"][0:2 * NQ, 128:256],
                                  pb[5][0:2 * NQ, 0:128])
                K.inc(i, "sV")
            trowm = K.cnt["sV"]

            @blk.tensor
            def _(t, trowm=trowm):
                K.w(t, "sV", trowm)
                for j in range(NQ):
                    t.matmul(pb[4][:, j * QW:(j + 1) * QW],
                             s["sel4"][0:2 * NQ, j * QW:(j + 1) * QW],
                             s["trows"][0:2 * NQ, 0:QW], start=True, stop=True)
                    i = t.matmul(pb[5][:, j * QW:(j + 1) * QW],
                                 s["sel4"][0:2 * NQ, j * QW:(j + 1) * QW],
                                 s["trows"][0:2 * NQ, 128:128 + QW],
                                 start=True, stop=True)
                K.inc(i, "sM")
            marks[("bcm", tau)] = K.cnt["sM"]

        def emit_quant(tau):
            bcm = marks[("bcm", tau)]

            @blk.vector
            def _(v, tau=tau, bcm=bcm):
                K.w(v, "sM", bcm)
                if dbg:
                    K.w(v, "dbg2", marks.get(("dsp2", tau), 0))
                v.tensor_copy(s["st1"][:, :], pb[4][:, :])
                i = v.tensor_copy(s["st3"][:, :], pb[5][:, :])
                K.inc(i, "sV")
                for g in range(G):
                    v.tensor_mul(s["tt2"][:, :], K.fch(fF[1], g), s["st1"][:, :])
                    v.tensor_scalar_add(s["tt2"][:, :], s["tt2"][:, :], 128.5)
                    v.tensor_copy(s["ti"][:, :], s["tt2"][:, :])
                    v.scalar_tensor_tensor(s["tt"][:, :], s["tt2"][:, :], 0.5,
                                           s["ti"][:, :], Alu.subtract,
                                           Alu.subtract)
                    v.tensor_copy(s["ti2"][:, :], s["tt"][:, :])
                    i = v.scalar_tensor_tensor(K.fch(qv, g), s["ti"][:, :],
                                               -128.0, s["ti2"][:, :],
                                               Alu.add, Alu.add)
                K.inc(i, "sV")
            marks[("stc", tau)] = K.cnt["sV"] - 1
            marks[("qm", tau)] = K.cnt["sV"]
            if dbg:
                @blk.vector
                def _(v, tau=tau):
                    v.tensor_copy(s["st0"][:, :], K.fch(qv, 0))
                    i = v.tensor_copy(s["st2"][:, :], K.fch(qv, 1))
                    K.inc(i, "sV")
                qcm = K.cnt["sV"]

                @blk.sync
                def _(sy, tau=tau, qcm=qcm):
                    K.w(sy, "sV", qcm)
                    K.inc(sy.dma_start(d["dbg"][tau, 5][:, 0:NT],
                                       s["st1"][:, :]), "dbg4", 16)
                    K.inc(sy.dma_start(d["dbg"][tau, 5][:, NT:2 * NT],
                                       s["st3"][:, :]), "dbg4", 16)
                    K.inc(sy.dma_start(d["dbg"][tau, 5][:, 2 * NT:3 * NT],
                                       s["st0"][:, :]), "dbg4", 16)
                    K.inc(sy.dma_start(d["dbg"][tau, 5][:, 3 * NT:4 * NT],
                                       s["st2"][:, :]), "dbg4", 16)
                marks[("dsp4", tau)] = K.cnt["dbg4"]
            if tau < NI:
                assert marks[("stcp", tau)] == marks[("stc", tau)]

        @blk.sync
        def _(sy):
            K.w(sy, "sV", w8c)
            K.inc(sy.dma_start(K.f_str(fF[0], 0, NT), K.xT3(1)), "xL", 16)
        marks[("ldx2", 1)] = K.cnt["xL"]

        emit_wfr(1)
        emit_sgate(1)
        emit_z2r0(1)
        emit_loads_a(1)
        emit_a2r0(1)
        emit_z2r1(1)
        emit_loads_b(1)

        for tau in range(1, NI):
            emit_mid_a(tau)
            emit_z2r0(tau + 1)
            emit_mid_b(tau)
            emit_wfr(tau + 1)
            emit_quant(tau)
            emit_sgate(tau + 1)
            emit_tail_a(tau)
            emit_loads_a(tau + 1)
            emit_a2r0(tau + 1)
            emit_tail_b(tau)
            emit_z2r1(tau + 1)
            emit_loads_b(tau + 1)
            emit_spill_out(tau)
        emit_mid_a(NI)
        emit_mid_b(NI)
        emit_quant(NI)
        emit_tail_a(NI)
        emit_tail_b(NI)
        emit_spill_out(NI)

        @blk.sync
        def _(sy):
            for n in ["outS", "xL", "a1L", "a1L2", "xnL", "kkL0", "kkL1",
                      "dbg0", "dbg1", "dbg2", "dbg3", "dbg4"]:
                K.w(sy, n, K.cnt[n])


def build_K():
    K = Builder()
    emit_pass_a(K)
    emit_pass_b1(K)
    emit_pass_b2a(K)
    emit_pass_b2b(K)
    return K


def build():
    return build_K().nc


def stage_inputs(inputs):
    x = np.asarray(inputs["L0_attn_x"])
    fm = lambda v: np.ascontiguousarray(np.asarray(v).reshape(G, 128).T)
    ct = lambda w: np.ascontiguousarray(np.asarray(w).T)
    common = {
        "WkT": ct(inputs["L0_attn_Wk"]), "WvT": ct(inputs["L0_attn_Wv"]),
        "WrT": ct(inputs["L0_attn_Wr"]), "WoT": ct(inputs["L0_attn_Wo"]),
        "WfrT": ct(inputs["L0_attn_Wfr"]), "WfkT": ct(inputs["L0_attn_Wfk"]),
        "WfvT": ct(inputs["L0_attn_Wfv"]), "w8T": ct(inputs["L0_ffn_w_int8"]),
        "ln1w": fm(inputs["L0_attn_ln1_w"]), "ln1b": fm(inputs["L0_attn_ln1_b"]),
        "ln2w": fm(inputs["L0_attn_ln2_w"]), "ln2b": fm(inputs["L0_attn_ln2_b"]),
        "decay": fm(inputs["L0_attn_decay"]), "uu": fm(inputs["L0_attn_u"]),
        "fbias": fm(inputs["L0_ffn_bias"]),
        "wst": np.asarray(inputs["L0_ffn_weight_scale"],
                          np.float32).reshape(1, 1),
        "onesb": np.ones((128, 128), ml_dtypes.bfloat16),
        "identb": np.eye(128, dtype=ml_dtypes.bfloat16),
        "onesf": np.ones((1, 128), np.float32),
        "sel4": np.ascontiguousarray(
            np.repeat(np.eye(NQ, dtype=np.float32), 2, axis=0)
            .repeat(QW, axis=1)).astype(ml_dtypes.bfloat16),
    }
    in_maps = []
    for core in range(8):
        b, q = core // 4, core % 4
        lo = q * NCORE - NT
        xT = np.zeros((C, NW), np.float32)
        a, bb = max(lo, 0), min(lo + NW, T)
        xT[:, a - lo: bb - lo] = x[b, a:bb, :].T
        mask = np.ones((128, 2 * NT), np.float32)
        if q == 0:
            mask[:, 0:NT] = 0.0
        if q == 3:
            mask[:, NT:2 * NT] = 0.0
        in_maps.append({**common,
                        "xT": np.ascontiguousarray(xT),
                        "mask2": mask.astype(ml_dtypes.bfloat16)})
    return in_maps


def assemble(results):
    out = np.empty((B, T, C), np.float32)
    for core, r in enumerate(results):
        b, q = core // 4, core % 4
        o = np.asarray(r["out"])
        blk = o.reshape(NI, 128, G, NT).transpose(0, 3, 2, 1).reshape(NCORE, C)
        out[b, q * NCORE:(q + 1) * NCORE, :] = blk
    return out


_NC_CACHE = [None]


def kernel(**inputs):
    in_maps = stage_inputs(inputs)
    if _NC_CACHE[0] is None:
        _NC_CACHE[0] = build()
    res = run_bass_kernel_spmd(_NC_CACHE[0], in_maps, core_ids=list(range(8)))
    return assemble(res.results)
